# version 07
# baseline (speedup 1.0000x reference)
import numpy as np
import ml_dtypes
from contextlib import ExitStack

import concourse.bass as bass
import concourse.mybir as mybir
from concourse.bass_utils import run_bass_kernel_spmd

F32 = mybir.dt.float32
BF16 = mybir.dt.bfloat16
AF = mybir.ActivationFunctionType
ALU = mybir.AluOpType
AX = mybir.AxisListType

NCORES = 8
DM = 1024
NTOK = 4096
NCH = DM // 128
BF = ml_dtypes.bfloat16


def _rotary_tables():
    DH, W, n = 64, 256, NTOK
    half = np.arange(0, DH, 2, dtype=np.float64)
    inv_freq = 1.0 / 10000.0 ** (half / DH)
    t = np.arange(n, dtype=np.float64)
    freqs = t[:, None] * inv_freq[None, :]
    freqs = np.concatenate([freqs, freqs], -1)
    base_scale = (half + 0.4 * DH) / (1.4 * DH)
    power = (t - n // 2) / (W // 2)
    scale = base_scale[None, :] ** power[:, None]
    scale = np.concatenate([scale, scale], -1)
    cos, sin = np.cos(freqs), np.sin(freqs)
    inv = 1.0 / scale
    outs = []
    for tab in (cos * scale, sin * scale, cos * inv, sin * inv):
        tt = tab.T.astype(np.float32)
        outs.append(np.concatenate([tt, tt], 0).astype(BF))
    return outs


def _local_masks():
    m = np.zeros((4, 128, 256), dtype=np.float32)
    a = np.arange(256)[None, :]
    p = np.arange(128)[:, None]
    for kc in range(4):
        if kc < 2:
            vis = (kc * 128 + p) >= a
        else:
            vis = ((kc - 2) * 128 + p) <= a
        m[kc] = np.where(vis, 0.0, -1e9).astype(np.float32)
    return m


def _r2t():
    R = np.zeros((64, 64), dtype=np.float32)
    for d in range(32):
        R[d, d + 32] = -1.0
        R[d + 32, d] = 1.0
    R2 = np.zeros((128, 128), dtype=np.float32)
    R2[:64, :64] = R
    R2[64:, 64:] = R
    return np.ascontiguousarray(R2.T).astype(BF)


def _expanders():
    dg = np.arange(512)
    e32 = (np.arange(32)[:, None] == (dg // 16)[None, :]).astype(BF)
    e16 = (np.arange(16)[:, None] == (dg % 16)[None, :]).astype(BF)
    return np.ascontiguousarray(e32), np.ascontiguousarray(e16)


def _chunked_vec(v, pad_to=None):
    v = np.asarray(v, dtype=np.float32).ravel()
    if pad_to is not None and v.shape[0] < pad_to:
        v = np.concatenate([v, np.zeros(pad_to - v.shape[0], np.float32)])
    assert v.shape[0] % 128 == 0
    return np.ascontiguousarray(v.reshape(-1, 128).T)


def host_prep(inputs):
    P = {}
    x = np.asarray(inputs["L0_attn_x"], dtype=np.float32)
    per_core = [np.ascontiguousarray(x[c].T) for c in range(NCORES)]

    P["l0a_wqT"] = np.ascontiguousarray(np.asarray(inputs["L0_attn_wq"], np.float32).T)
    P["l0a_bq"] = _chunked_vec(inputs["L0_attn_bq"])
    P["l0a_wuT"] = np.ascontiguousarray(np.asarray(inputs["L0_attn_wu"], np.float32).T)
    P["l0a_bu"] = _chunked_vec(inputs["L0_attn_bu"])

    W1 = np.asarray(inputs["L0_ffn_W1"], np.float32)
    W2 = np.asarray(inputs["L0_ffn_W2"], np.float32)
    DI, DIP = 2730, 2816
    for i in range(6):
        P[f"ffn{i}_wqkvT"] = np.ascontiguousarray(np.asarray(inputs["L0_ffn_Wqkv"], np.float32)[i])
        P[f"ffn{i}_woT"] = np.ascontiguousarray(np.asarray(inputs["L0_ffn_Wo"], np.float32)[i])
        w1a = np.zeros((DM, DIP), np.float32); w1a[:, :DI] = W1[i][:, :DI]
        w1g = np.zeros((DM, DIP), np.float32); w1g[:, :DI] = W1[i][:, DI:]
        P[f"ffn{i}_w1a"] = w1a
        P[f"ffn{i}_w1g"] = w1g
        w2 = np.zeros((DIP, DM), np.float32); w2[:DI, :] = W2[i]
        P[f"ffn{i}_w2"] = w2
        b1 = np.asarray(inputs["L0_ffn_b1"], np.float32)[i]
        P[f"ffn{i}_b1a"] = _chunked_vec(b1[:DI], pad_to=DIP)
        P[f"ffn{i}_b1g"] = _chunked_vec(b1[DI:], pad_to=DIP)
        P[f"ffn{i}_b2"] = _chunked_vec(np.asarray(inputs["L0_ffn_b2"], np.float32)[i])
        P[f"ffn{i}_lng"] = _chunked_vec(np.asarray(inputs["L0_ffn_ln_g"], np.float32)[i])
        P[f"ffn{i}_lnb"] = _chunked_vec(np.asarray(inputs["L0_ffn_ln_b"], np.float32)[i])
        qs = np.asarray(inputs["L0_ffn_q_scale"], np.float32)[i]
        ks = np.asarray(inputs["L0_ffn_k_scale"], np.float32)[i]
        P[f"ffn{i}_qs"] = np.ascontiguousarray(np.tile(qs, 2)[:, None])
        P[f"ffn{i}_ks"] = np.ascontiguousarray(np.tile(ks, 2)[:, None])
        P[f"ffn{i}_qsr"] = np.ascontiguousarray(qs[None, :])
        P[f"ffn{i}_ksr"] = np.ascontiguousarray(ks[None, :])

    P["l1a_wqT"] = np.ascontiguousarray(np.asarray(inputs["L1_attn_Wq"], np.float32).T)
    P["l1a_wkvdT"] = np.ascontiguousarray(np.asarray(inputs["L1_attn_Wkv_down"], np.float32).T)
    P["l1a_wkuT"] = np.ascontiguousarray(np.asarray(inputs["L1_attn_Wk_up"], np.float32).T)
    P["l1a_wvuT"] = np.ascontiguousarray(np.asarray(inputs["L1_attn_Wv_up"], np.float32).T)
    P["l1a_woT"] = np.ascontiguousarray(np.asarray(inputs["L1_attn_Wo"], np.float32).T)

    G1R = np.asarray(inputs["L1_ffn_G1_R"], np.float32); G2R = np.asarray(inputs["L1_ffn_G2_R"], np.float32)
    G1I = np.asarray(inputs["L1_ffn_G1_I"], np.float32); G2I = np.asarray(inputs["L1_ffn_G2_I"], np.float32)
    P["l1f_A_R"] = np.ascontiguousarray(G1R[0].transpose(1, 0, 2))
    P["l1f_B_R"] = np.ascontiguousarray(G2R[:, :, 0, :])
    P["l1f_A_I"] = np.ascontiguousarray(G1I[0].transpose(1, 0, 2))
    P["l1f_B_I"] = np.ascontiguousarray(G2I[:, :, 0, :])
    P["l1f_bias"] = np.concatenate([
        _chunked_vec(inputs["L1_ffn_bias_real"]),
        _chunked_vec(inputs["L1_ffn_bias_imag"])], axis=1)

    cs, ss, ci, si = _rotary_tables()
    P["rot"] = np.ascontiguousarray(np.stack([cs, ss, ci, si]))
    P["msk"] = _local_masks()
    P["r2t"] = _r2t()
    e32, e16 = _expanders()
    P["e32t"] = e32
    P["e16t"] = e16
    bc33 = np.zeros((33, 128), np.float32)
    bc33[0, 0:64] = 1.0
    bc33[32, 64:128] = 1.0
    P["bc33"] = bc33
    return per_core, P



ENGINES = ("sync", "tensor", "scalar", "vector", "gpsimd")
_DMA_NAMES = ("dma_start", "dma_start_transpose")


class _Rec:
    __slots__ = ("eng", "name", "args", "kwargs", "inc", "ms_no", "waits", "stream",
                 "sem_slot")
    def __init__(self, eng, name, args, kwargs):
        self.eng = eng
        self.name = name
        self.args = args
        self.kwargs = kwargs
        self.inc = 0
        self.ms_no = None
        self.waits = []
        self.stream = None


class Milestone:
    __slots__ = ("rec",)
    def __init__(self, rec):
        self.rec = rec


class StreamMark:
    __slots__ = ("stream", "value")
    def __init__(self, stream, value):
        self.stream = stream
        self.value = value


class _EngineProxy:
    def __init__(self, prog, eng):
        self._prog = prog
        self._eng = eng

    def __getattr__(self, name):
        prog, eng = self._prog, self._eng
        def record(*args, _deps=(), _ms=False, _sem=None, **kwargs):
            return prog.record(eng, name, args, kwargs, deps=_deps, ms=_ms, sem=_sem)
        return record


def _ap_footprint(ap):
    t = ap.tensor
    name = t.name
    shape = list(t.shape)
    dims = [list(d) for d in ap.ap]
    off = ap.offset
    is_dram = type(t).__name__.startswith("DRam")
    if is_dram:
        lo = off
        ext = sum(st * (ct - 1) for st, ct in dims if ct > 1)
        return (name, 0, 0, 0, lo, lo + ext)
    epp = 1
    for s in shape[1:]:
        epp *= s
    p_lo = off // epp if epp else 0
    f_lo = off % epp if epp else 0
    p_hi = p_lo
    f_hi = f_lo
    for (st, ct) in dims:
        if ct <= 1:
            continue
        ext = st * (ct - 1)
        if st == epp:
            p_hi += ct - 1
        elif st > 0 and st % epp == 0 and epp > 0:
            p_hi += (st // epp) * (ct - 1)
        else:
            f_hi += ext
    base = getattr(t, "base_partition", 0) or 0
    return (name, 1, base + p_lo, base + p_hi, f_lo, f_hi)


def _overlap(a, b):
    return (a[0] == b[0] and a[2] <= b[3] and b[2] <= a[3]
            and a[4] <= b[5] and b[4] <= a[5])


_NO_AP_OPS = {"__wait_stream__"}


class Prog:

    def __init__(self, nc):
        self.nc = nc
        self.recs = {e: [] for e in ENGINES}
        self.sem_count = {e: 0 for e in ENGINES}
        self.stream_count = {}
        self.stream_eng = {}
        self.writers = {}
        self.readers = {}
        for e in ENGINES:
            setattr(self, e, _EngineProxy(self, e))

    def _dep_on(self, rec, r):
        if r is rec:
            return
        if r.stream is not None:
            if r.stream.startswith("f_"):
                rec.waits.append(("s", r.stream, -1))
            else:
                rec.waits.append(("s", r.stream, self.stream_count[r.stream]))
        else:
            r.inc = 1
            rec.waits.append(("e", r))

    def record(self, eng, name, args, kwargs, deps=(), ms=False, sem=None):
        is_dma = name in _DMA_NAMES
        rec = _Rec(eng, name, args, kwargs)
        if is_dma:
            stream = sem if sem is not None else f"q_{eng}"
            if stream.startswith("f_"):
                stream = f"f_{getattr(self, 'phase_tag', '')}" + stream[1:]
            rec.stream = stream
        elif ms:
            rec.inc = 1
        pf = getattr(self, "_pf", None)
        if pf is not None and eng in getattr(self, "_pf_pending", ()):
            self._pf_pending.discard(eng)
            deps = tuple(deps) + tuple(pf)
        for d in deps:
            if d is None:
                continue
            if isinstance(d, StreamMark):
                rec.waits.append(("s", d.stream, d.value))
            else:
                self._dep_on(rec, d.rec)
        outs = []
        ins = []
        for i, a in enumerate(args):
            if isinstance(a, bass.AP):
                (outs if i == 0 else ins).append(a)
        for k, v in kwargs.items():
            if isinstance(v, bass.AP):
                (outs if k == "accum_out" else ins).append(v)
        if name == "matmul" or name == "transpose":
            pass
        for a in ins:
            fp = _ap_footprint(a)
            for (wfp, wrec) in self.writers.get(fp[0], ()):
                if _overlap(fp, wfp):
                    if (wrec.stream is None and wrec.eng == "tensor"
                            and eng == "tensor"):
                        continue
                    self._dep_on(rec, wrec)
            rl = self.readers.setdefault(fp[0], [])
            rl[:] = [(f, r) for (f, r) in rl
                     if not (r.eng == eng and r.stream is None
                             and f[2] >= fp[2] and f[3] <= fp[3]
                             and f[4] >= fp[4] and f[5] <= fp[5])]
            rl.append((fp, rec))
        for a in outs:
            fp = _ap_footprint(a)
            for (wfp, wrec) in self.writers.get(fp[0], ()):
                if _overlap(fp, wfp):
                    if (wrec.stream is None and wrec.eng == "tensor"
                            and eng == "tensor"):
                        continue
                    self._dep_on(rec, wrec)
            war = {}
            for (rfp, rrec) in self.readers.get(fp[0], ()):
                if _overlap(fp, rfp):
                    if (rrec.stream is None and rrec.eng == "tensor"
                            and eng == "tensor"):
                        continue
                    key = rrec.stream if rrec.stream is not None else rrec.eng
                    war[key] = rrec
            for rrec in war.values():
                self._dep_on(rec, rrec)
            def covers(big, small):
                return (big[2] <= small[2] and big[3] >= small[3]
                        and big[4] <= small[4] and big[5] >= small[5])
            wl = self.writers.setdefault(fp[0], [])
            wl[:] = [(f, r) for (f, r) in wl if not covers(fp, f)]
            wl.append((fp, rec))
            rl = self.readers.get(fp[0])
            if rl:
                rl[:] = [(f, r) for (f, r) in rl if not covers(fp, f)]
        if is_dma:
            rec.inc = 16
            self.stream_count[rec.stream] = self.stream_count.get(rec.stream, 0) + 16
            self.stream_eng[rec.stream] = eng
            rec.ms_no = self.stream_count[rec.stream]
        self.recs[eng].append(rec)
        return Milestone(rec)

    def phase_fence(self):
        self._pf = self.fence_all()
        self._pf_pending = set(ENGINES)

    def fence_all(self):
        out = []
        for eng in ENGINES:
            recs = self.recs[eng]
            if not recs:
                continue
            r = recs[-1]
            if r.stream is None:
                r.inc = 1
                out.append(Milestone(r))
        for s, cnt in self.stream_count.items():
            out.append(StreamMark(s, cnt))
        return tuple(out)

    def wait_all_dmas(self):
        for stream, eng in self.stream_eng.items():
            rec = _Rec(eng, "__wait_stream__", (stream,), {})
            self.recs[eng].append(rec)

    SEM_SLOT = 60000

    def build(self):
        nc = self.nc
        slots = {}
        for en in ENGINES:
            n = 0
            for r in self.recs[en]:
                if r.inc == 1:
                    r.sem_slot = n // self.SEM_SLOT
                    r.ms_no = n % self.SEM_SLOT + 1
                    n += 1
            slots[en] = max(1, (n + self.SEM_SLOT - 1) // self.SEM_SLOT)
        with ExitStack() as ctx:
            e = ctx.enter_context
            sems = {(en, sl): e(nc.semaphore(f"sem_{en}{sl}"))
                    for en in ENGINES for sl in range(slots[en])}
            ssems = {s: e(nc.semaphore(f"ss_{i}")) for i, s in
                     enumerate(self.stream_count)}
            block = e(nc.Block())
            for eng_name in ENGINES:
                recs = self.recs[eng_name]
                if not recs:
                    continue
                def body(eng, recs=recs, eng_name=eng_name):
                    hwm = {}
                    for r in recs:
                        for w in r.waits:
                            if w[0] == "s":
                                kind, key, val = w
                                if val < 0:
                                    val = self.stream_count[key]
                                sem = ssems[key]
                            else:
                                p = w[1]
                                kind = "e"
                                key = (p.eng, p.sem_slot)
                                val = p.ms_no
                                assert val is not None
                                sem = sems[key]
                            if hwm.get((kind, key), -1) >= val:
                                continue
                            hwm[(kind, key)] = val
                            eng.wait_ge(sem, val)
                        if r.name == "__wait_stream__":
                            s = r.args[0]
                            eng.wait_ge(ssems[s], self.stream_count[s])
                            continue
                        instr = getattr(eng, r.name)(*r.args, **r.kwargs)
                        if r.inc == 16:
                            instr.then_inc(ssems[r.stream], 16)
                        elif r.inc == 1:
                            instr.then_inc(sems[(eng_name, r.sem_slot)], 1)
                getattr(block, eng_name)(body)
        return nc


class _NCP:
    def __init__(self, nc, pfx):
        self._nc = nc
        self._pfx = pfx

    def sbuf_tensor(self, name, *a, **kw):
        return self._nc.sbuf_tensor(self._pfx + name, *a, **kw)

    def psum_tensor(self, name, *a, **kw):
        return self._nc.psum_tensor(self._pfx + name, *a, **kw)

    def __getattr__(self, n):
        return getattr(self._nc, n)


class PsumPool:
    def __init__(self, tensors):
        self.free = [{"t": t, "readers": []} for t in tensors]

    def acquire(self):
        b = self.free.pop(0)
        deps = tuple(b["readers"])
        b["readers"] = []
        return b, deps

    @staticmethod
    def done(bank, ms):
        bank["readers"].append(ms)

    def release(self, bank):
        self.free.append(bank)


def ln_rows(P, sum_ps, sumsq_ps, sum_dep, sumsq_dep, rows, n, tw, epsc, war=()):
    v, s = P.vector, P.scalar
    R = lambda k: rows[0:1, k * tw:(k + 1) * tw]
    v.tensor_scalar(R(0), sum_ps, 1.0 / n, None, ALU.mult,
                    _deps=(sum_dep,) + tuple(war))
    v.tensor_scalar(R(2), sumsq_ps, 1.0 / n, None, ALU.mult,
                    _deps=(sumsq_dep,))
    v.tensor_tensor(R(3), R(0), R(0), ALU.mult)
    var_ms = v.tensor_tensor(R(2), R(2), R(3), ALU.subtract, _ms=True)
    s.activation(R(1), R(2), AF.Ln, bias=epsc, _deps=(var_ms,))
    rstd_ms = s.activation(R(1), R(1), AF.Exp, scale=-0.5, _ms=True)
    mr_ms = v.tensor_tensor(R(3), R(0), R(1), ALU.mult, _deps=(rstd_ms,), _ms=True)
    return rstd_ms, mr_ms



def emit_ln_stream(nc, P, mode, xin, yout, xres=None):
    TW = 512
    NT = NTOK // TW
    with ExitStack() as ctx:
        e = ctx.enter_context
        xt = [e(nc.sbuf_tensor(f"xt{r}", [128, NCH, TW], F32)) for r in range(2)]
        xrt = [e(nc.sbuf_tensor(f"xrt{r}", [128, NCH, TW], F32)) for r in range(2)] \
            if mode == "res_ln" else None
        xb = [e(nc.sbuf_tensor(f"xb{r}", [128, NCH, TW], BF16)) for r in range(2)]
        sq = [e(nc.sbuf_tensor(f"sq{r}", [128, NCH, TW], BF16)) for r in range(2)]
        yt = [e(nc.sbuf_tensor(f"yt{r}", [128, NCH, TW], F32)) for r in range(2)]
        rows = [e(nc.sbuf_tensor(f"rows{r}", [1, 4 * TW], F32)) for r in range(2)]
        ones_bf = e(nc.sbuf_tensor("ones_bf", [128, 1], BF16))
        ones_f = e(nc.sbuf_tensor("ones_f", [1, 128], F32))
        epsc = e(nc.sbuf_tensor("epsc", [1, 1], F32))
        ps = [e(nc.psum_tensor(f"ps{k}", [128, 512], F32)) for k in range(6)]

        pool = PsumPool(ps)
        P.vector.memset(ones_bf[:], 1.0)
        P.vector.memset(epsc[:], 1e-5)
        ones_ms = P.vector.memset(ones_f[:], 1.0, _ms=True)

        prev = {}
        out_dmas = []
        for tt in range(NT):
            r = tt % 2
            dep_in = prev.get(("xt_free", r), ())
            dm = None
            for f in range(NCH):
                dm = P.sync.dma_start(xt[r][:, f, :], xin[128 * f:128 * (f + 1),
                                      TW * tt:TW * (tt + 1)], _deps=dep_in if f == 0 else (),
                                      _sem=f"xi{r}")
            if mode == "res_ln":
                dep_res = prev.get(("xrt_free", r), ())
                for f in range(NCH):
                    dm = P.sync.dma_start(xrt[r][:, f, :], xres[128 * f:128 * (f + 1),
                                          TW * tt:TW * (tt + 1)], _deps=dep_res if f == 0 else (),
                                          _sem=f"xr{r}")
            cast_ms = P.vector.tensor_copy(xb[r].rearrange("p a b -> p (a b)"),
                                           xt[r].rearrange("p a b -> p (a b)"),
                                           _deps=(dm,) + prev.get(("xb_free", r), ()), _ms=True)
            sq_ms = P.scalar.activation(sq[r].rearrange("p a b -> p (a b)"),
                                        xb[r].rearrange("p a b -> p (a b)"), AF.Square,
                                        _deps=(cast_ms,) + prev.get(("sq_free", r), ()), _ms=True)
            bs, d0 = pool.acquire()
            bq, d1 = pool.acquire()
            sum_ms = sumsq_ms = None
            for f in range(NCH):
                sum_ms = P.tensor.matmul(bs["t"][0:1, 0:TW], ones_bf[:], xb[r][:, f, :],
                                         start=(f == 0), stop=(f == NCH - 1),
                                         _deps=((cast_ms, ones_ms) + d0) if f == 0 else (),
                                         _ms=(f == NCH - 1))
            for f in range(NCH):
                sumsq_ms = P.tensor.matmul(bq["t"][0:1, 0:TW], ones_bf[:], sq[r][:, f, :],
                                           start=(f == 0), stop=(f == NCH - 1),
                                           _deps=((sq_ms,) + d1) if f == 0 else (),
                                           _ms=(f == NCH - 1))
            rstd_ms, mr_ms = ln_rows(P, bs["t"][0:1, 0:TW], bq["t"][0:1, 0:TW],
                                     sum_ms, sumsq_ms, rows[r], DM, TW, epsc[:],
                                     war=prev.get(("rows_free", r), ()))
            pool.done(bs, rstd_ms)
            pool.done(bq, rstd_ms)
            pool.release(bs)
            pool.release(bq)
            br, d2 = pool.acquire()
            bm, d3 = pool.acquire()
            bcr_ms = P.tensor.matmul(br["t"][:, 0:TW], ones_f[:], rows[r][0:1, TW:2 * TW],
                                     start=True, stop=True,
                                     _deps=(rstd_ms, ones_ms) + d2, _ms=True)
            bcm_ms = P.tensor.matmul(bm["t"][:, 0:TW], ones_f[:], rows[r][0:1, 3 * TW:4 * TW],
                                     start=True, stop=True, _deps=(mr_ms,) + d3, _ms=True)
            prev[("rows_free", r)] = (bcr_ms, bcm_ms)
            last_v = None
            for f in range(NCH):
                P.vector.tensor_tensor(yt[r][:, f, :], xt[r][:, f, :], br["t"][:, 0:TW],
                                       ALU.mult, _deps=(bcr_ms,) + (prev.get(("yt_free", r), ()) if f == 0 else ()))
                if mode == "res_ln":
                    P.vector.tensor_tensor(yt[r][:, f, :], yt[r][:, f, :], bm["t"][:, 0:TW],
                                           ALU.subtract, _deps=(bcm_ms,))
                    last_v = P.vector.tensor_tensor(yt[r][:, f, :], yt[r][:, f, :],
                                                    xrt[r][:, f, :], ALU.add,
                                                    _ms=True)
                else:
                    last_v = P.vector.tensor_tensor(yt[r][:, f, :], yt[r][:, f, :],
                                                    bm["t"][:, 0:TW], ALU.subtract,
                                                    _deps=(bcm_ms,), _ms=True)
            pool.done(br, last_v)
            pool.done(bm, last_v)
            pool.release(br)
            pool.release(bm)
            prev[("xt_free", r)] = (cast_ms, last_v)
            if mode == "res_ln":
                prev[("xrt_free", r)] = (last_v,)
            prev[("xb_free", r)] = (sum_ms, sq_ms)
            prev[("sq_free", r)] = (sumsq_ms,)
            od = None
            for f in range(NCH):
                od = P.gpsimd.dma_start(yout[128 * f:128 * (f + 1), TW * tt:TW * (tt + 1)],
                                        yt[r][:, f, :], _deps=(last_v,) if f == 0 else (),
                                        _sem=f"yo{r}")
            out_dmas.append(od)
            prev[("yt_free", r)] = (od,)



QB = 256
NQB = NTOK // QB
SUP = 1024
NSUP = NTOK // SUP
DIP = 2816
NDJ = DIP // 128


def emit_ffn_layer(nc, P, yin, ynext, xa, w1s, w2s, wqkvT, woT, w1a, w1g, w2,
                   b1a, b1g, b2, lng, lnb, qs, ks, qsr, ksr, rot, msk, r2t, bc33,
                   nqb=NQB, nsup=NSUP, passa=True, passb=True, astage=6,
                   nhp=NCH, qko=("q", "k"), evac=("v", "s")):
    with ExitStack() as octx:
        oe = octx.enter_context
        ones_bf = oe(nc.sbuf_tensor("ones_bf", [128, 1], BF16))
        ones_f = oe(nc.sbuf_tensor("ones_f", [1, 128], F32))
        epsc = oe(nc.sbuf_tensor("epsc", [1, 1], F32))
        negB = oe(nc.sbuf_tensor("negB", [128, 1], F32))
        nrow = oe(nc.sbuf_tensor("nrow", [1, 4], F32))
        b1a_sb = oe(nc.sbuf_tensor("b1a_sb", [128, NDJ], F32))
        b1g_sb = oe(nc.sbuf_tensor("b1g_sb", [128, NDJ], F32))
        b2_sb = oe(nc.sbuf_tensor("b2_sb", [128, NCH], F32))
        lng_sb = oe(nc.sbuf_tensor("lng_sb", [128, NCH], F32))
        lnb_sb = oe(nc.sbuf_tensor("lnb_sb", [128, NCH], F32))
        qs_sb = oe(nc.sbuf_tensor("qs_sb", [128, 1], F32))
        ks_sb = oe(nc.sbuf_tensor("ks_sb", [128, 1], F32))
        r2t_sb = oe(nc.sbuf_tensor("r2t_sb", [128, 128], BF16))
        msk_sb = oe(nc.sbuf_tensor("msk_sb", [128, 4, QB], F32))
        qsr_sb = oe(nc.sbuf_tensor("qsr_sb", [1, 64], F32))
        ksr_sb = oe(nc.sbuf_tensor("ksr_sb", [1, 64], F32))
        bc33_sb = oe(nc.sbuf_tensor("bc33_sb", [33, 128], F32))
        ps = [oe(nc.psum_tensor(f"ps{k}", [128, 512], F32)) for k in range(8)]

        pool = PsumPool(ps)
        P.vector.memset(ones_bf[:], 1.0)
        P.vector.memset(epsc[:], 1e-5)
        ones_ms = P.vector.memset(ones_f[:], 1.0, _ms=True)

        sm = None
        for dst, srcap in ((b1a_sb, b1a), (b1g_sb, b1g), (b2_sb, b2), (lng_sb, lng),
                           (lnb_sb, lnb), (qs_sb, qs), (ks_sb, ks), (r2t_sb, r2t),
                           (bc33_sb, bc33)):
            sm = P.sync.dma_start(dst[:], srcap, _sem="f_init")
        for kc in range(4):
            sm = P.sync.dma_start(msk_sb[:, kc, :], msk[kc], _sem="f_init")

        dqs = P.sync.dma_start(qsr_sb[:], qsr, _sem="f_init")
        dks = P.sync.dma_start(ksr_sb[:], ksr, _sem="f_init")
        P.vector.reduce_max(nrow[0:1, 0:1], qsr_sb[:], axis=AX.X,
                            apply_absolute_value=True, _deps=(dqs,))
        P.vector.reduce_max(nrow[0:1, 1:2], ksr_sb[:], axis=AX.X,
                            apply_absolute_value=True, _deps=(dks,))
        P.vector.tensor_tensor(nrow[0:1, 2:3], nrow[0:1, 0:1], nrow[0:1, 1:2], ALU.mult)
        nb_ms = P.vector.tensor_scalar(nrow[0:1, 2:3], nrow[0:1, 2:3], -8.0, None,
                                       ALU.mult, _ms=True)
        bnb, dnb = pool.acquire()
        nbm = P.tensor.matmul(bnb["t"][:, 0:1], ones_f[:], nrow[0:1, 2:3],
                              start=True, stop=True,
                              _deps=(nb_ms, ones_ms) + dnb, _ms=True)
        negB_ms = P.vector.tensor_copy(negB[:], bnb["t"][:, 0:1], _deps=(nbm,), _ms=True)
        pool.done(bnb, negB_ms)
        pool.release(bnb)

        w1s_done = []
        w2s_done = []
        with ExitStack() as pctx:
            pe = pctx.enter_context
            wpf = [pe(nc.sbuf_tensor(f"wpf{r}", [128, NCH, 128], F32)) for r in range(2)]
            wpb = [pe(nc.sbuf_tensor(f"wpb{r}", [128, 2048], BF16)) for r in range(2)]
            prevd = {}
            for dj in range(NDJ):
                r = dj % 2
                war_b = prevd.get(("wpb", r), ())
                war_f = prevd.get(("wpf", r), ())
                cast2 = None
                for half, wsrc in ((0, w1a), (1, w1g)):
                    dm = None
                    for f in range(NCH):
                        dm = P.sync.dma_start(
                            wpf[r][:, f, :],
                            wsrc[128 * f:128 * (f + 1), 128 * dj:128 * (dj + 1)],
                            _deps=war_f if f == 0 else (), _sem=f"wpf{r}")
                    cms = P.vector.tensor_copy(wpb[r][:, 1024 * half:1024 * (half + 1)],
                                               wpf[r].rearrange("p a b -> p (a b)"),
                                               _deps=(dm,) + (war_b if half == 0 else ()),
                                               _ms=True)
                    war_f = (cms,)
                    cast2 = cms
                od = P.gpsimd.dma_start(w1s[dj], wpb[r][:], _deps=(cast2,), _sem=f"wps{r}")
                prevd[("wpf", r)] = war_f
                prevd[("wpb", r)] = (od,)
                if dj >= NDJ - 2:
                    w1s_done.append(od)
            wpf2 = [pe(nc.sbuf_tensor(f"wpf2{r}", [128, NDJ, 128], F32)) for r in range(2)]
            wpb2 = [pe(nc.sbuf_tensor(f"wpb2{r}", [128, NDJ * 128], BF16)) for r in range(2)]
            for f in range(NCH):
                r = f % 2
                dm = None
                for cj in range(NDJ):
                    dm = P.sync.dma_start(wpf2[r][:, cj, :],
                                          w2[128 * cj:128 * (cj + 1), 128 * f:128 * (f + 1)],
                                          _deps=prevd.get(("wpf2", r), ()) if cj == 0 else (),
                                          _sem=f"wpf2{r}")
                cms = P.vector.tensor_copy(wpb2[r][:], wpf2[r].rearrange("p a b -> p (a b)"),
                                           _deps=(dm,) + prevd.get(("wpb2", r), ()), _ms=True)
                od = P.gpsimd.dma_start(w2s[f], wpb2[r][:], _deps=(cms,), _sem=f"wps2{r}")
                prevd[("wpf2", r)] = (cms,)
                prevd[("wpb2", r)] = (od,)
                if f >= NCH - 2:
                    w2s_done.append(od)

        xa_ms = [None] * NQB
        fenceP = P.fence_all()
        with ExitStack() as actx:
            ae = actx.enter_context
            wqkv_sb = ae(nc.sbuf_tensor("wqkv_sb", [128, NCH, 3072], BF16))
            wo_sb = ae(nc.sbuf_tensor("wo_sb", [128, NCH, DM], BF16))
            wstg = [ae(nc.sbuf_tensor(f"wstg{r}", [128, 1024], F32)) for r in range(2)]
            xin = [ae(nc.sbuf_tensor(f"xinA{r}", [128, NCH, QB + 1], F32)) for r in range(2)]
            xs = [ae(nc.sbuf_tensor(f"xsA{r}", [128, NCH, QB], BF16)) for r in range(2)]
            qpre = ae(nc.sbuf_tensor("qpre", [128, NCH, QB], BF16))
            kpre = ae(nc.sbuf_tensor("kpre", [128, NCH, QB], BF16))
            qsq = ae(nc.sbuf_tensor("qsq", [128, NCH, QB], BF16))
            ksq = ae(nc.sbuf_tensor("ksq", [128, NCH, QB], BF16))
            qhat = ae(nc.sbuf_tensor("qhat", [128, NCH, QB], BF16))
            khat = ae(nc.sbuf_tensor("khat", [128, NCH, QB], BF16))
            qr = ae(nc.sbuf_tensor("qr", [128, NCH, QB], BF16))
            kr_ring = ae(nc.sbuf_tensor("kr_ring", [128, 2, NCH, QB], BF16))
            rote = [ae(nc.sbuf_tensor(f"rote{r}", [128, 4, QB], BF16)) for r in range(2)]
            vslot = ae(nc.sbuf_tensor("vslot", [128, 2, 2, 16, 65], BF16))
            pstage = ae(nc.sbuf_tensor("pstage", [128, 2, 4, QB], BF16))
            ofm = ae(nc.sbuf_tensor("ofm", [128, NCH, QB], BF16))
            o_bf = ae(nc.sbuf_tensor("o_bf", [64, 4, QB], BF16))
            o2_bf = ae(nc.sbuf_tensor("o2_bf", [64, 4, QB], BF16))
            rdrow = ae(nc.sbuf_tensor("rdrow", [1, 4, QB], F32))
            nrm = ae(nc.sbuf_tensor("nrm", [1, 4, 2 * QB], F32))
            nrm2 = ae(nc.sbuf_tensor("nrm2", [33, 4, QB], F32))
            t1f = ae(nc.sbuf_tensor("t1f", [128, 4, QB], F32))
            t2f = ae(nc.sbuf_tensor("t2f", [128, 4, QB], F32))
            xast = [ae(nc.sbuf_tensor(f"xast{r}", [128, NCH, QB], F32)) for r in range(2)]

            P.vector.memset(nrm2[0:33, :, :], 1.0, _ms=True)
            prevw = {}
            cms = None
            for f in range(NCH):
                for part in range(3):
                    r = (f * 3 + part) % 2
                    dm = P.sync.dma_start(wstg[r][:],
                                          wqkvT[128 * f:128 * (f + 1), 1024 * part:1024 * (part + 1)],
                                          _deps=prevw.get(("wstg", r), fenceP), _sem=f"wstg{r}")
                    cms = P.vector.tensor_copy(wqkv_sb[:, f, 1024 * part:1024 * (part + 1)],
                                               wstg[r][:], _deps=(dm,), _ms=True)
                    prevw[("wstg", r)] = (cms,)
            for f in range(NCH):
                r = f % 2
                dm = P.sync.dma_start(wstg[r][:], woT[128 * f:128 * (f + 1), :],
                                      _deps=prevw.get(("wstg", r), fenceP), _sem=f"wstg{r}")
                cms = P.vector.tensor_copy(wo_sb[:, f, :], wstg[r][:], _deps=(dm,), _ms=True)
                prevw[("wstg", r)] = (cms,)
            wload_ms = cms

            vones_ms = None
            for sl in range(2):
                for half in range(2):
                    for j in range(16):
                        vones_ms = P.vector.memset(vslot[:, sl, half, j, 64:65], 1.0,
                                                   _ms=(sl == 1 and half == 1 and j == 15))

            prevq = {}
            for qb in range(nqb if passa else 0):
                rr = qb % 2
                sl = qb % 2
                slp = (qb - 1) % 2
                s0 = qb * QB
                war = prevq.get(("xin", rr), ())
                dm_x = None
                if qb == 0:
                    zm = P.vector.memset(xin[rr].rearrange("p a b -> p (a b)"), 0.0,
                                         _deps=war, _ms=True)
                    for f in range(NCH):
                        dm_x = P.sync.dma_start(xin[rr][:, f, 1:QB + 1],
                                                yin[128 * f:128 * (f + 1), 0:QB],
                                                _deps=(zm,) if f == 0 else (), _sem=f"xiA{rr}")
                else:
                    for f in range(NCH):
                        dm_x = P.sync.dma_start(xin[rr][:, f, :],
                                                yin[128 * f:128 * (f + 1), s0 - 1:s0 + QB],
                                                _deps=war if f == 0 else (), _sem=f"xiA{rr}")
                dm_rot = None
                for t in range(4):
                    dm_rot = P.sync.dma_start(rote[rr][:, t, :], rot[t, :, s0:s0 + QB],
                                              _deps=prevq.get(("rote", rr), ()) if t == 0 else (),
                                              _sem=f"rot{rr}")
                P.scalar.activation(
                    xs[rr][:, 0:4, :], xin[rr][:, 0:4, 1:QB + 1],
                    AF.Copy, _deps=(dm_x,) + prevq.get(("xs", rr), ()))
                xs_ms = P.scalar.activation(
                    xs[rr][:, 4:8, :], xin[rr][:, 4:8, 0:QB],
                    AF.Copy, _ms=True)
                if astage < 2:
                    continue
                for which, pre, sqt in (("q", qpre, qsq), ("k", kpre, ksq)):
                    if which not in qko:
                        continue
                    off = 0 if which == "q" else 1024
                    for hp in range(nhp):
                        bqk, dq = pool.acquire()
                        mm = None
                        for f in range(NCH):
                            mm = P.tensor.matmul(
                                bqk["t"][:, 0:QB],
                                wqkv_sb[:, f, off + 128 * hp:off + 128 * (hp + 1)],
                                xs[rr][:, f, :], start=(f == 0), stop=(f == NCH - 1),
                                _deps=((xs_ms, wload_ms) + dq) if f == 0 else (),
                                _ms=(f == NCH - 1))
                        e1 = e2 = mm
                        if "v" in evac:
                            e1 = P.vector.tensor_copy(pre[:, hp, :], bqk["t"][:, 0:QB],
                                                      _deps=(mm,), _ms=True)
                            pool.done(bqk, e1)
                        if "s" in evac:
                            e2 = P.scalar.activation(sqt[:, hp, :], bqk["t"][:, 0:QB], AF.Square,
                                                     _deps=(mm, e1) + prevq.get(("sq_read", which, hp), ()),
                                                     _ms=True)
                            pool.done(bqk, e2)
                        pool.release(bqk)
                        prevq[("qkpre_ev", which, hp)] = (e1, e2)
                if astage < 3:
                    continue
                v_ev = []
                for th in range(2):
                    for dv in range(2):
                        bv, dv_ = pool.acquire()
                        mm = None
                        for f in range(NCH):
                            mm = P.tensor.matmul(
                                bv["t"][:, 0:512],
                                xs[rr][:, f, 128 * th:128 * (th + 1)],
                                wqkv_sb[:, f, 2048 + 512 * dv:2048 + 512 * (dv + 1)],
                                start=(f == 0), stop=(f == NCH - 1),
                                _deps=((xs_ms,) + dv_ + prevq.get(("v_read", sl), ())) if f == 0 else (),
                                _ms=(f == NCH - 1))
                        ev = P.scalar.activation(
                            vslot[:, sl, th, 8 * dv:8 * (dv + 1), 0:64],
                            bv["t"][:, 0:512].rearrange("p (a b) -> p a b", a=8),
                            AF.Copy, _deps=(mm,), _ms=True)
                        v_ev.append(ev)
                        pool.done(bv, ev)
                        pool.release(bv)
                last_v_mm = mm
                prevq[("xs", rr)] = (last_v_mm,)
                if astage < 4:
                    continue
                qr_ms = {}
                kr_ms = {}
                t2_ms = None
                for which, pre, sqt, ssc, tb0, tb1 in (
                        ("q", qpre, qsq, qs_sb, 0, 1),
                        ("k", kpre, ksq, ks_sb, 2, 3)):
                    qkof = 0 if which == "q" else 2
                    for hp in range(NCH):
                        sc = (hp % 2) + qkof
                        e1, e2 = prevq[("qkpre_ev", which, hp)]
                        bn_e, dne = pool.acquire()
                        bn_o, dno = pool.acquire()
                        m_e = P.tensor.matmul(bn_e["t"][0:1, 0:QB], ones_bf[0:64, :],
                                              sqt[0:64, hp, :], start=True, stop=True,
                                              tile_position=(0, 0),
                                              _deps=(e2, ones_ms) + dne, _ms=True)
                        m_o = P.tensor.matmul(bn_o["t"][0:1, 0:QB], ones_bf[64:128, :],
                                              sqt[64:128, hp, :], start=True, stop=True,
                                              tile_position=(64, 0),
                                              _deps=(e2,) + dno, _ms=True)
                        prevq[("sq_read", which, hp)] = (m_o,)
                        c1 = P.vector.tensor_copy(nrm[0:1, sc, 0:QB], bn_e["t"][0:1, 0:QB],
                                                  _deps=(m_e,), _ms=True)
                        c2 = P.vector.tensor_copy(nrm[0:1, sc, QB:2 * QB], bn_o["t"][0:1, 0:QB],
                                                  _deps=(m_o,), _ms=True)
                        pool.done(bn_e, c1); pool.release(bn_e)
                        pool.done(bn_o, c2); pool.release(bn_o)
                        mx_ms = P.vector.tensor_scalar(nrm[0:1, sc, :], nrm[0:1, sc, :],
                                                       1e-18, None, ALU.max,
                                                       _deps=(c1, c2), _ms=True)
                        P.scalar.activation(nrm[0:1, sc, :], nrm[0:1, sc, :],
                                            AF.Ln, _deps=(mx_ms,))
                        rc_ms = P.scalar.activation(nrm[0:1, sc, :], nrm[0:1, sc, :],
                                                    AF.Exp, scale=-0.5, _ms=True)
                        P.vector.tensor_copy(nrm2[0:1, sc, :], nrm[0:1, sc, 0:QB],
                                             _deps=(rc_ms,))
                        cp2 = P.vector.tensor_copy(nrm2[32:33, sc, :], nrm[0:1, sc, QB:2 * QB],
                                                   _ms=True)
                        bb, dbb = pool.acquire()
                        bmm = P.tensor.matmul(bb["t"][:, 0:QB], bc33_sb[:],
                                              nrm2[0:33, sc, :], start=True, stop=True,
                                              _deps=(cp2, sm) + dbb, _ms=True)
                        prevq[("nrm",)] = (bmm,)
                        hat = qhat if which == "q" else khat
                        hat_ms = P.vector.scalar_tensor_tensor(
                            hat[:, hp, :], pre[:, hp, :], ssc[:], bb["t"][:, 0:QB],
                            ALU.mult, ALU.mult, _deps=(bmm, sm), _ms=True)
                        pool.done(bb, hat_ms)
                        pool.release(bb)
                        br, drr = pool.acquire()
                        rmm = P.tensor.matmul(br["t"][:, 0:QB], r2t_sb[:], hat[:, hp, :],
                                              start=True, stop=True,
                                              _deps=(hat_ms, sm) + drr, _ms=True)
                        P.vector.tensor_tensor(t1f[:, sc, :], hat[:, hp, :], rote[rr][:, tb0, :],
                                               ALU.mult, _deps=(dm_rot, hat_ms))
                        t2_ms = P.vector.tensor_tensor(t2f[:, sc, :], br["t"][:, 0:QB],
                                                       rote[rr][:, tb1, :], ALU.mult,
                                                       _deps=(rmm,), _ms=True)
                        pool.done(br, t2_ms)
                        pool.release(br)
                        if which == "q":
                            rm = P.vector.tensor_tensor(qr[:, hp, :], t1f[:, sc, :], t2f[:, sc, :],
                                                        ALU.add,
                                                        _deps=prevq.get(("qr_read", hp), ()),
                                                        _ms=True)
                            qr_ms[hp] = rm
                        else:
                            rm = P.vector.tensor_tensor(kr_ring[:, sl, hp, :], t1f[:, sc, :], t2f[:, sc, :],
                                                        ALU.add,
                                                        _deps=prevq.get(("kr_read", sl), ()),
                                                        _ms=True)
                            kr_ms[hp] = rm
                prevq[("rote", rr)] = (t2_ms,)
                if astage < 5:
                    continue
                wo_dep = []
                last_av = None
                last_s = None
                for hp in range(NCH):
                    kcs = list(range(2, 4)) if qb == 0 else list(range(4))
                    exp_ms = {}
                    for kc in kcs:
                        ksl = slp if kc < 2 else sl
                        koff = 128 * (kc % 2)
                        kdep = kr_ms[hp] if kc >= 2 else prevq.get(("kr_prev", hp), (None,))[0]
                        bse, dse = pool.acquire()
                        bso, dso = pool.acquire()
                        me = P.tensor.matmul(bse["t"][:, 0:QB],
                                             kr_ring[0:64, ksl, hp, koff:koff + 128],
                                             qr[0:64, hp, :], start=True, stop=True,
                                             tile_position=(0, 0),
                                             _deps=(qr_ms[hp], kdep) + dse, _ms=True)
                        mo = P.tensor.matmul(bso["t"][:, 0:QB],
                                             kr_ring[64:128, ksl, hp, koff:koff + 128],
                                             qr[64:128, hp, :], start=True, stop=True,
                                             tile_position=(64, 0),
                                             _deps=dso, _ms=True)
                        last_s = mo
                        for par, bsx, mx in ((0, bse, me), (1, bso, mo)):
                            mk = P.vector.tensor_tensor(bsx["t"][:, 0:QB], bsx["t"][:, 0:QB],
                                                        msk_sb[:, kc, :], ALU.add,
                                                        _deps=(mx, sm), _ms=True)
                            em = P.scalar.activation(pstage[:, par, kc, :], bsx["t"][:, 0:QB],
                                                     AF.Exp, bias=negB[:], scale=8.0,
                                                     _deps=(mk, negB_ms) +
                                                           prevq.get(("pst", hp, par, kc), ()),
                                                     _ms=True)
                            exp_ms[(par, kc)] = em
                            pool.done(bsx, em)
                            pool.release(bsx)
                    for par in range(2):
                        h = 2 * hp + par
                        bav, dav = pool.acquire()
                        mm = None
                        for ii, kc in enumerate(kcs):
                            ksl = slp if kc < 2 else sl
                            vdep = vones_ms if qb == 0 else None
                            mm = P.tensor.matmul(
                                bav["t"][0:65, 0:QB],
                                vslot[:, ksl, kc % 2, h, 0:65],
                                pstage[:, par, kc, :],
                                start=(ii == 0), stop=(ii == len(kcs) - 1),
                                _deps=((exp_ms[(par, kc)], vdep, v_ev[-1]) + dav) if ii == 0
                                      else (exp_ms[(par, kc)],),
                                _ms=(ii == len(kcs) - 1))
                        last_av = mm
                        for kc in kcs:
                            prevq[("pst", hp, par, kc)] = (mm,)
                        hh = (2 * hp + par) % 4
                        P.scalar.activation(rdrow[:, hh, :], bav["t"][64:65, 0:QB], AF.Ln,
                                            _deps=(mm,))
                        rd = P.scalar.activation(rdrow[:, hh, :], rdrow[:, hh, :], AF.Exp,
                                                 scale=-1.0, _ms=True)
                        bbc, dbc = pool.acquire()
                        bcm = P.tensor.matmul(bbc["t"][0:64, 0:QB], ones_f[:, 0:64],
                                              rdrow[:, hh, :],
                                              start=True, stop=True, tile_position=(0, 0),
                                              _deps=(rd,) + dbc, _ms=True)
                        oc = P.scalar.activation(o_bf[:, hh, :], bav["t"][0:64, 0:QB], AF.Copy,
                                                 _deps=(mm, rd), _ms=True)
                        pool.done(bav, oc)
                        pool.done(bav, rd)
                        pool.release(bav)
                        if par == 0:
                            tms = P.vector.tensor_tensor(ofm[0:64, hp, :], o_bf[:, hh, :],
                                                         bbc["t"][0:64, 0:QB], ALU.mult,
                                                         _deps=(oc, bcm) + prevq.get(("ofm_w",), ()),
                                                         _ms=True)
                            wo_dep.append(tms)
                        else:
                            tms = P.vector.tensor_tensor(o2_bf[:, hh, :], o_bf[:, hh, :],
                                                         bbc["t"][0:64, 0:QB], ALU.mult,
                                                         _deps=(oc, bcm),
                                                         _ms=True)
                            dmo = P.gpsimd.dma_start(ofm[64:128, hp, :], o2_bf[:, hh, :],
                                                     _deps=(tms,) + prevq.get(("ofm_w",), ()),
                                                     _sem="dmo")
                            wo_dep.append(dmo)
                        pool.done(bbc, tms)
                        pool.release(bbc)
                    prevq[("qr_read", hp)] = (last_s,)
                    prevq[("kr_prev", hp)] = (kr_ms[hp],)
                prevq[("kr_read", sl)] = (last_s,)
                prevq[("kr_read", slp)] = (last_s,)
                prevq[("v_read", sl)] = (last_av,)
                prevq[("v_read", slp)] = (last_av,)
                if astage < 6:
                    continue
                rms = None
                wo_mm = None
                for f in range(NCH):
                    bwo, dwo = pool.acquire()
                    mm = None
                    for cc in range(NCH):
                        mm = P.tensor.matmul(bwo["t"][:, 0:QB],
                                             wo_sb[:, cc, 128 * f:128 * (f + 1)],
                                             ofm[:, cc, :], start=(cc == 0), stop=(cc == NCH - 1),
                                             _deps=(tuple(wo_dep) + dwo) if cc == 0 else (),
                                             _ms=(cc == NCH - 1))
                    wo_mm = mm
                    rms = P.vector.tensor_tensor(xast[rr][:, f, :], xin[rr][:, f, 1:QB + 1],
                                                 bwo["t"][:, 0:QB], ALU.add,
                                                 _deps=(mm,) + (prevq.get(("xast", rr), ()) if f == 0 else ()),
                                                 _ms=True)
                    pool.done(bwo, rms)
                    pool.release(bwo)
                    xa_ms[qb] = P.gpsimd.dma_start(xa[128 * f:128 * (f + 1), s0:s0 + QB],
                                                   xast[rr][:, f, :], _deps=(rms,),
                                                   _sem=f"xa{rr}")
                prevq[("ofm_w",)] = (wo_mm,)
                prevq[("xast", rr)] = (xa_ms[qb],)
                prevq[("xin", rr)] = (xs_ms, rms)

        fence = P.fence_all()

        with ExitStack() as bctx:
            be = bctx.enter_context
            xin2 = [be(nc.sbuf_tensor(f"xinB{r}", [128, NCH, SUP + 1], F32)) for r in range(2)]
            xs2 = be(nc.sbuf_tensor("xs2", [128, NCH, SUP], BF16))
            sq2 = be(nc.sbuf_tensor("sq2", [128, NCH, 512], BF16))
            xnb = be(nc.sbuf_tensor("xnb", [128, 2, NCH, SUP], BF16))
            hst = be(nc.sbuf_tensor("hst", [128, NDJ, SUP], BF16))
            w1b = [be(nc.sbuf_tensor(f"w1b{r}", [128, 2048], BF16)) for r in range(2)]
            w2b = [be(nc.sbuf_tensor(f"w2b{r}", [128, NDJ * 128], BF16)) for r in range(2)]
            gat = be(nc.sbuf_tensor("gat", [128, 512], BF16))
            tfa = be(nc.sbuf_tensor("tfa", [128, 512], F32))
            tfb = be(nc.sbuf_tensor("tfb", [128, 512], F32))
            tfc = be(nc.sbuf_tensor("tfc", [128, 2, 512], F32))
            rows = be(nc.sbuf_tensor("rowsB", [1, 4 * 512], F32))

            prevs = {}
            nts = nsup if passb else 0

            def stage(ts):
                r = ts % 2
                s0 = ts * SUP
                if ts == 0:
                    zm = P.vector.memset(xin2[r][:, :, 0:1], 0.0, _deps=fence, _ms=True)
                    for f in range(NCH):
                        P.sync.dma_start(
                            xin2[r][:, f, 1:SUP + 1], xa[128 * f:128 * (f + 1), 0:SUP],
                            _deps=((zm,) + tuple(xa_ms[0:4])) if f == 0 else (),
                            _sem=f"xiB{r}")
                else:
                    for f in range(NCH):
                        P.sync.dma_start(
                            xin2[r][:, f, :], xa[128 * f:128 * (f + 1), s0 - 1:s0 + SUP],
                            _deps=tuple(xa_ms[4 * ts - 1:4 * ts + 4]) if f == 0 else (),
                            _sem=f"xiB{r}")
                P.scalar.activation(xs2[:, 0:4, :], xin2[r][:, 0:4, 1:SUP + 1],
                                    AF.Copy, _deps=fence if ts == 0 else ())
                xc2 = P.scalar.activation(xs2[:, 4:8, :], xin2[r][:, 4:8, 0:SUP],
                                          AF.Copy, _ms=True)
                for nt in range(2):
                    t0 = 512 * nt
                    sqm = P.scalar.activation(sq2[:, :, :], xs2[:, :, t0:t0 + 512],
                                              AF.Square,
                                              _deps=fence if ts == 0 and nt == 0 else (),
                                              _ms=True)
                    bs_, d0 = pool.acquire()
                    bq_, d1 = pool.acquire()
                    summ = sqsm = None
                    for f in range(NCH):
                        summ = P.tensor.matmul(bs_["t"][0:1, 0:512], ones_bf[:],
                                               xs2[:, f, t0:t0 + 512],
                                               start=(f == 0), stop=(f == NCH - 1),
                                               _deps=((xc2, ones_ms) + d0) if f == 0 else (),
                                               _ms=(f == NCH - 1))
                    for f in range(NCH):
                        sqsm = P.tensor.matmul(bq_["t"][0:1, 0:512], ones_bf[:],
                                               sq2[:, f, :], start=(f == 0),
                                               stop=(f == NCH - 1),
                                               _deps=((sqm,) + d1) if f == 0 else (),
                                               _ms=(f == NCH - 1))
                    rstd_ms, mr_ms = ln_rows(P, bs_["t"][0:1, 0:512], bq_["t"][0:1, 0:512],
                                             summ, sqsm, rows, DM, 512, epsc[:],
                                             war=prevs.get(("rows",), fence))
                    pool.done(bs_, rstd_ms); pool.release(bs_)
                    pool.done(bq_, rstd_ms); pool.release(bq_)
                    bcr, d2 = pool.acquire()
                    bcm_, d3 = pool.acquire()
                    bcr_ms = P.tensor.matmul(bcr["t"][:, 0:512], ones_f[:],
                                             rows[0:1, 512:1024],
                                             start=True, stop=True,
                                             _deps=(rstd_ms,) + d2, _ms=True)
                    bcm_ms = P.tensor.matmul(bcm_["t"][:, 0:512], ones_f[:],
                                             rows[0:1, 3 * 512:4 * 512],
                                             start=True, stop=True,
                                             _deps=(mr_ms,) + d3, _ms=True)
                    prevs[("rows",)] = (bcr_ms, bcm_ms)
                    lastxn = None
                    for f in range(NCH):
                        P.vector.tensor_tensor(tfa[:], xs2[:, f, t0:t0 + 512],
                                               bcr["t"][:, 0:512], ALU.mult,
                                               _deps=(bcr_ms,))
                        P.vector.tensor_tensor(tfb[:], tfa[:], bcm_["t"][:, 0:512],
                                               ALU.subtract, _deps=(bcm_ms,))
                        lastxn = P.scalar.activation(xnb[:, r, f, t0:t0 + 512], tfb[:],
                                                     AF.Identity,
                                                     bias=lnb_sb[:, f:f + 1],
                                                     scale=lng_sb[:, f:f + 1],
                                                     _deps=(sm,), _ms=True)
                    pool.done(bcr, lastxn); pool.release(bcr)
                    pool.done(bcm_, lastxn); pool.release(bcm_)

            def heavy(ts):
                r = ts % 2
                s0 = ts * SUP
                h_ms = []
                mma = mmg = None
                for dj in range(NDJ):
                    rw = dj % 2
                    wd = P.sync.dma_start(w1b[rw][:], w1s[dj], _sem=f"w1b{rw}",
                                          _deps=() if ts else fence)
                    for nt in range(2):
                        t0 = 512 * nt
                        ba, da = pool.acquire()
                        bg, dg = pool.acquire()
                        for f in range(NCH):
                            mma = P.tensor.matmul(ba["t"][:, 0:512],
                                                  w1b[rw][:, 128 * f:128 * (f + 1)],
                                                  xnb[:, r, f, t0:t0 + 512],
                                                  start=(f == 0), stop=(f == NCH - 1),
                                                  _deps=((wd,) + da) if f == 0 else (),
                                                  _ms=(f == NCH - 1))
                        for f in range(NCH):
                            mmg = P.tensor.matmul(bg["t"][:, 0:512],
                                                  w1b[rw][:, 1024 + 128 * f:1024 + 128 * (f + 1)],
                                                  xnb[:, r, f, t0:t0 + 512],
                                                  start=(f == 0), stop=(f == NCH - 1),
                                                  _deps=((wd,) + dg) if f == 0 else (),
                                                  _ms=(f == NCH - 1))
                        gm = P.scalar.activation(gat[:], bg["t"][:, 0:512], AF.Gelu,
                                                 bias=b1g_sb[:, dj:dj + 1],
                                                 _deps=(mmg, sm), _ms=True)
                        pool.done(bg, gm); pool.release(bg)
                        hm = P.vector.scalar_tensor_tensor(hst[:, dj, t0:t0 + 512],
                                                           ba["t"][:, 0:512],
                                                           b1a_sb[:, dj:dj + 1],
                                                           gat[:], ALU.add, ALU.mult,
                                                           _deps=(mma, gm), _ms=True)
                        pool.done(ba, hm); pool.release(ba)
                        h_ms.append(hm)
                for f in range(NCH):
                    rw = f % 2
                    wd = P.sync.dma_start(w2b[rw][:], w2s[f], _sem=f"w2b{rw}",
                                          _deps=() if ts else fence)
                    for nt in range(2):
                        t0 = 512 * nt
                        bo, do_ = pool.acquire()
                        mm = None
                        for cj in range(NDJ):
                            mm = P.tensor.matmul(bo["t"][:, 0:512],
                                                 w2b[rw][:, 128 * cj:128 * (cj + 1)],
                                                 hst[:, cj, t0:t0 + 512],
                                                 start=(cj == 0), stop=(cj == NDJ - 1),
                                                 _deps=((wd, h_ms[-1]) + do_) if cj == 0 else (),
                                                 _ms=(cj == NDJ - 1))
                        tm = P.scalar.activation(tfc[:, f % 2, :], bo["t"][:, 0:512],
                                                 AF.Identity,
                                                 bias=b2_sb[:, f:f + 1],
                                                 _deps=(mm,), _ms=True)
                        pool.done(bo, tm); pool.release(bo)
                        rm = P.vector.tensor_tensor(xin2[r][:, f, 1 + t0:1 + t0 + 512],
                                                    xin2[r][:, f, 1 + t0:1 + t0 + 512],
                                                    tfc[:, f % 2, :], ALU.add,
                                                    _deps=(tm,), _ms=True)
                        P.gpsimd.dma_start(
                            ynext[128 * f:128 * (f + 1), s0 + t0:s0 + t0 + 512],
                            xin2[r][:, f, 1 + t0:1 + t0 + 512], _deps=(rm,),
                            _sem=f"yn{r}")

            if nts:
                stage(0)
                for ts in range(nts):
                    if ts + 1 < nts:
                        stage(ts + 1)
                    heavy(ts)


def build_ffn_layer(**kw):
    nc = bass.Bass("TRN2", target_bir_lowering=False)
    yin = nc.dram_tensor("yin", [DM, NTOK], F32, kind="ExternalInput").ap()
    ynext = nc.dram_tensor("ynext", [DM, NTOK], F32, kind="ExternalOutput").ap()
    xa = nc.dram_tensor("xa", [DM, NTOK], F32).ap()
    w1s = nc.dram_tensor("w1s", [NDJ, 128, 2048], BF16).ap()
    w2s = nc.dram_tensor("w2s", [NCH, 128, NDJ * 128], BF16).ap()
    names = [("wqkvT", [DM, 3072]), ("woT", [DM, DM]), ("w1a", [DM, DIP]),
             ("w1g", [DM, DIP]), ("w2", [DIP, DM]), ("b1a", [128, NDJ]),
             ("b1g", [128, NDJ]), ("b2", [128, NCH]), ("lng", [128, NCH]),
             ("lnb", [128, NCH]), ("qs", [128, 1]), ("ks", [128, 1]),
             ("qsr", [1, 64]), ("ksr", [1, 64])]
    aps = [nc.dram_tensor(n, s, F32, kind="ExternalInput").ap() for n, s in names]
    rot = nc.dram_tensor("rot", [4, 128, NTOK], BF16, kind="ExternalInput").ap()
    msk = nc.dram_tensor("msk", [4, 128, QB], F32, kind="ExternalInput").ap()
    r2t = nc.dram_tensor("r2t", [128, 128], BF16, kind="ExternalInput").ap()
    bc33 = nc.dram_tensor("bc33", [33, 128], F32, kind="ExternalInput").ap()
    P = Prog(nc)
    emit_ffn_layer(nc, P, yin, ynext, xa, w1s, w2s, *aps, rot, msk, r2t, bc33, **kw)
    P.wait_all_dmas()
    P.build()
    return nc



def emit_l0attn(nc, P, xin, xout, wqT, wuT, bq, bu, idb, ofm_dr, stage=9, s2v="mtea"):
    TW = 512
    NT = NTOK // TW
    with ExitStack() as octx:
        oe = octx.enter_context
        ones_bf = oe(nc.sbuf_tensor("ones_bf", [128, 1], BF16))
        ones_f = oe(nc.sbuf_tensor("ones_f", [1, 128], F32))
        epsc = oe(nc.sbuf_tensor("epsc", [1, 1], F32))
        id_sb = oe(nc.sbuf_tensor("id_sb", [128, 128], BF16))
        bq_sb = oe(nc.sbuf_tensor("bq_sb", [128, NCH], F32))
        bu_sb = oe(nc.sbuf_tensor("bu_sb", [128, NCH], F32))
        qgrp = oe(nc.sbuf_tensor("qgrp", [128, 8, 16, 256], BF16))
        rows = oe(nc.sbuf_tensor("rows", [1, 4 * TW], F32))
        negBg = oe(nc.sbuf_tensor("negBg", [128, 16], F32))
        nBrow = oe(nc.sbuf_tensor("nBrow", [1, 16], F32))
        ps = [oe(nc.psum_tensor(f"ps{k}", [128, 512], F32)) for k in range(6)]
        tps2 = [oe(nc.psum_tensor(f"tps{k}", [128, 512], BF16)) for k in range(2)]

        pool = PsumPool(ps)
        P.vector.memset(ones_bf[:], 1.0)
        P.vector.memset(epsc[:], 1e-5)
        ones_ms = P.vector.memset(ones_f[:], 1.0, _ms=True)
        for dst, srcap in ((bq_sb, bq), (bu_sb, bu), (id_sb, idb)):
            P.sync.dma_start(dst[:], srcap, _sem="f_init")

        with ExitStack() as c0:
            e0 = c0.enter_context
            y_bf = e0(nc.sbuf_tensor("y_bf", [128, NCH, NTOK], BF16))
            c1 = ExitStack()
            e1 = c1.enter_context
            xt = [e1(nc.sbuf_tensor(f"xt{r}", [128, NCH, TW], F32)) for r in range(2)]
            xb = [e1(nc.sbuf_tensor(f"xb{r}", [128, NCH, TW], BF16)) for r in range(1)]
            sq = [e1(nc.sbuf_tensor(f"sq{r}", [128, NCH, TW], BF16)) for r in range(1)]

            for tt in range(NT):
                r = tt % 2
                dm = None
                for f in range(NCH):
                    dm = P.sync.dma_start(xt[r][:, f, :],
                                          xin[128 * f:128 * (f + 1), TW * tt:TW * (tt + 1)],
                                          _sem=f"xi{r}")
                cast_ms = P.vector.tensor_copy(xb[0][:, :, :], xt[r][:, :, :], _ms=True)
                sq_ms = P.scalar.activation(sq[0][:, :, :], xb[0][:, :, :], AF.Square, _ms=True)
                bs, d0 = pool.acquire()
                bqp, d1 = pool.acquire()
                summ = sumsq = None
                for f in range(NCH):
                    summ = P.tensor.matmul(bs["t"][0:1, 0:TW], ones_bf[:], xb[0][:, f, :],
                                           start=(f == 0), stop=(f == NCH - 1),
                                           _deps=d0 if f == 0 else (), _ms=(f == NCH - 1))
                for f in range(NCH):
                    sumsq = P.tensor.matmul(bqp["t"][0:1, 0:TW], ones_bf[:], sq[0][:, f, :],
                                            start=(f == 0), stop=(f == NCH - 1),
                                            _deps=d1 if f == 0 else (), _ms=(f == NCH - 1))
                rstd_ms, mr_ms = ln_rows(P, bs["t"][0:1, 0:TW], bqp["t"][0:1, 0:TW],
                                         summ, sumsq, rows, DM, TW, epsc[:])
                pool.done(bs, rstd_ms); pool.release(bs)
                pool.done(bqp, rstd_ms); pool.release(bqp)
                br, d2 = pool.acquire()
                bm, d3 = pool.acquire()
                bcr = P.tensor.matmul(br["t"][:, 0:TW], ones_f[:], rows[0:1, TW:2 * TW],
                                      start=True, stop=True, _deps=d2, _ms=True)
                bcm = P.tensor.matmul(bm["t"][:, 0:TW], ones_f[:], rows[0:1, 3 * TW:4 * TW],
                                      start=True, stop=True, _deps=d3, _ms=True)
                last = None
                for f in range(NCH):
                    P.vector.tensor_tensor(xt[r][:, f, :], xt[r][:, f, :], br["t"][:, 0:TW],
                                           ALU.mult)
                    last = P.vector.tensor_tensor(y_bf[:, f, TW * tt:TW * (tt + 1)],
                                                  xt[r][:, f, :], bm["t"][:, 0:TW],
                                                  ALU.subtract, _ms=True)
                pool.done(br, last); pool.release(br)
                pool.done(bm, last); pool.release(bm)

            c1.close()
            e2s = c0.enter_context
            wstg = [e2s(nc.sbuf_tensor(f"wstg{r}", [128, 1024], F32)) for r in range(2)]
            wq_sb = e2s(nc.sbuf_tensor("wq_sb", [128, NCH, DM], BF16))
            tmpb = [e2s(nc.sbuf_tensor(f"tmpb{r}", [128, 512], BF16)) for r in range(2)]
            fence1 = P.fence_all()
            for f in range(NCH):
                r = f % 2
                dm = P.sync.dma_start(wstg[r][:], wqT[128 * f:128 * (f + 1), :],
                                      _sem=f"wstg{r}", _deps=fence1 if f < 2 else ())
                P.vector.tensor_copy(wq_sb[:, f, :], wstg[r][:], _ms=True)

            for tt in range(NT):
                parity = tt // 4
                for f in range(NCH):
                    bpq, dq = pool.acquire()
                    mm = None
                    for cc in range(NCH):
                        mm = P.tensor.matmul(bpq["t"][:, 0:TW],
                                             wq_sb[:, cc, 128 * f:128 * (f + 1)],
                                             y_bf[:, cc, TW * tt:TW * (tt + 1)],
                                             start=(cc == 0), stop=(cc == NCH - 1),
                                             _deps=dq if cc == 0 else (), _ms=(cc == NCH - 1))
                    r = (tt * NCH + f) % 2
                    ev = P.scalar.activation(tmpb[r][:], bpq["t"][:, 0:TW], AF.Identity,
                                             bias=bq_sb[:, f:f + 1], _ms=True)
                    pool.done(bpq, ev)
                    pool.release(bpq)
                    for rh in range(2):
                        for sh in range(2):
                            g = 2 * tt + sh
                            src = tmpb[r][64 * rh:64 * (rh + 1), 256 * sh:256 * (sh + 1)]
                            dst = qgrp[64 * parity:64 * (parity + 1), g % 8, 2 * f + rh, :]
                            if rh == parity:
                                P.vector.tensor_copy(dst, src, _ms=True)
                            else:
                                P.gpsimd.dma_start(dst, src, _sem=f"qgrp{r}")

        if stage < 2:
            return
        fence1b = P.fence_all()
        with ExitStack() as c2:
            e2 = c2.enter_context
            vgrp = e2(nc.sbuf_tensor("vgrp", [128, 16, 32, 65], BF16))
            tvb = e2(nc.sbuf_tensor("tvb", [128, 2, 128], BF16))
            qsqg = e2(nc.sbuf_tensor("qsqg", [128, 16, 256], BF16))
            nrmr = e2(nc.sbuf_tensor("nrmr", [1, 256 + 1], F32))
            pstg = e2(nc.sbuf_tensor("pstg", [128, 2, 2, 512], BF16))
            oq = e2(nc.sbuf_tensor("oq", [64, 512], BF16))
            onrm = e2(nc.sbuf_tensor("onrm", [64, 512], BF16))
            rdrow = e2(nc.sbuf_tensor("rdrow", [1, 512], F32))

            if "m" in s2v:
                for g in range(16):
                    for kc in range(32):
                        P.vector.memset(vgrp[:, g, kc, 64:65], 1.0,
                                        _deps=fence1b if (g == 0 and kc == 0) else (),
                                        _ms=(g == 15 and kc == 31))
            for gp in range(8 if "t" in s2v else 0):
                for c in range(16):
                    for h in range(2):
                        kc = 2 * c + h
                        slot = kc % 2
                        tm = P.tensor.transpose(tps2[slot][:, 0:128],
                                                qgrp[:, gp, c, 128 * h:128 * (h + 1)],
                                                id_sb[:], _deps=fence1b, _ms=True)
                        tf = P.vector.tensor_copy(tvb[:, slot, :],
                                                  tps2[slot][:, 0:128],
                                                  _deps=(tm,), _ms=True)
                        P.vector.tensor_copy(vgrp[:, gp, kc, 0:64],
                                             tvb[:, slot, 0:64], _ms=True)
                        P.scalar.activation(vgrp[:, gp + 8, kc, 0:64],
                                            tvb[:, slot, 64:128],
                                            AF.Copy, _deps=(tf,), _ms=True)
            for gp in range(8 if stage >= 3 else 0):
                sqm = P.scalar.activation(qsqg[:, :, :], qgrp[:, gp, :, :], AF.Square,
                                          _ms=True)
                for par in range(2):
                    g = gp + 8 * par
                    for c in range(16):
                        bn, dn = pool.acquire()
                        nm = P.tensor.matmul(bn["t"][0:1, 0:256],
                                             ones_bf[64 * par:64 * (par + 1), :],
                                             qsqg[64 * par:64 * (par + 1), c, :],
                                             start=True, stop=True,
                                             tile_position=(64 * par, 0),
                                             _deps=dn, _ms=True)
                        rm = P.vector.reduce_max(nrmr[0:1, 256:257], bn["t"][0:1, 0:256],
                                                 axis=AX.X, _ms=True)
                        pool.done(bn, rm)
                        pool.release(bn)
                        if c == 0:
                            P.vector.tensor_copy(nBrow[0:1, g:g + 1], nrmr[0:1, 256:257])
                        else:
                            P.vector.tensor_tensor(nBrow[0:1, g:g + 1], nBrow[0:1, g:g + 1],
                                                   nrmr[0:1, 256:257], ALU.max)
            nb_ms = P.vector.tensor_scalar(nBrow[0:1, :], nBrow[0:1, :], -1.0 / 32.0,
                                           None, ALU.mult, _ms=True)
            bnb, dnb = pool.acquire()
            nbm = P.tensor.matmul(bnb["t"][:, 0:16], ones_f[:], nBrow[0:1, :],
                                  start=True, stop=True, _deps=dnb + (ones_ms,), _ms=True)
            ngb_ms = P.vector.tensor_copy(negBg[:], bnb["t"][:, 0:16], _ms=True)
            pool.done(bnb, ngb_ms)
            pool.release(bnb)

            for gp in range(8 if stage >= 4 else 0):
                for qt in range(8):
                    bo = {}
                    od = {}
                    for par in range(2):
                        bo[par], od[par] = pool.acquire()
                    avm = {0: None, 1: None}
                    for kc in range(32):
                        rr = kc % 2
                        bse, dse = pool.acquire()
                        bso, dso = pool.acquire()
                        qv = qgrp.rearrange("p g c s -> p g (c s)")
                        me = P.tensor.matmul(bse["t"][:, 0:512],
                                             qgrp[0:64, gp, kc // 2, 128 * (kc % 2):128 * (kc % 2) + 128],
                                             qv[0:64, gp, 512 * qt:512 * (qt + 1)],
                                             start=True, stop=True, tile_position=(0, 0),
                                             _deps=dse, _ms=True)
                        mo = P.tensor.matmul(bso["t"][:, 0:512],
                                             qgrp[64:128, gp, kc // 2, 128 * (kc % 2):128 * (kc % 2) + 128],
                                             qv[64:128, gp, 512 * qt:512 * (qt + 1)],
                                             start=True, stop=True, tile_position=(64, 0),
                                             _deps=dso, _ms=True)
                        for par, bsx, mx in ((0, bse, me), (1, bso, mo)):
                            g = gp + 8 * par
                            em = P.scalar.activation(pstg[:, par, rr, :], bsx["t"][:, 0:512],
                                                     AF.Exp, bias=negBg[:, g:g + 1],
                                                     scale=1.0 / 32.0,
                                                     _deps=(mx, ngb_ms), _ms=True)
                            pool.done(bsx, em)
                            pool.release(bsx)
                            avm[par] = P.tensor.matmul(
                                bo[par]["t"][0:65, 0:512],
                                vgrp[:, g, kc, 0:65],
                                pstg[:, par, rr, :],
                                start=(kc == 0), stop=(kc == 31),
                                _deps=(em,) + (od[par] if kc == 0 else ()),
                                _ms=(kc == 31))
                    for par in range(2):
                        g = gp + 8 * par
                        P.scalar.activation(rdrow[:], bo[par]["t"][64:65, 0:512], AF.Ln,
                                            _deps=(avm[par],))
                        rd = P.scalar.activation(rdrow[:], rdrow[:], AF.Exp, scale=-1.0,
                                                 _ms=True)
                        bbc, dbc = pool.acquire()
                        bcm = P.tensor.matmul(bbc["t"][0:64, 0:512], ones_f[:, 0:64],
                                              rdrow[:], start=True, stop=True,
                                              tile_position=(0, 0), _deps=(rd,) + dbc,
                                              _ms=True)
                        oc = P.scalar.activation(oq[:], bo[par]["t"][0:64, 0:512], AF.Copy,
                                                 _deps=(avm[par], rd), _ms=True)
                        pool.done(bo[par], oc)
                        pool.release(bo[par])
                        nm = P.vector.tensor_tensor(onrm[:], oq[:], bbc["t"][0:64, 0:512],
                                                    ALU.mult, _ms=True)
                        pool.done(bbc, nm)
                        pool.release(bbc)
                        s0 = 256 * g
                        P.gpsimd.dma_start(ofm_dr[128 * qt:128 * qt + 64, s0:s0 + 256],
                                           onrm[:, 0:256], _sem="ofm")
                        P.gpsimd.dma_start(ofm_dr[128 * qt + 64:128 * qt + 128, s0:s0 + 256],
                                           onrm[:, 256:512], _sem="ofm")

        if stage < 5:
            return
        fence2 = P.fence_all()
        with ExitStack() as c3:
            e3 = c3.enter_context
            wu_sb = e3(nc.sbuf_tensor("wu_sb", [128, NCH, DM], BF16))
            wstg2 = [e3(nc.sbuf_tensor(f"wstg2{r}", [128, 1024], F32)) for r in range(2)]
            x0t = [e3(nc.sbuf_tensor(f"x0t{r}", [128, NCH, TW], F32)) for r in range(2)]
            wut = [e3(nc.sbuf_tensor(f"wut{r}", [128, NCH, TW], F32)) for r in range(1)]
            ofmt = [e3(nc.sbuf_tensor(f"ofmt{r}", [128, NCH, TW], BF16)) for r in range(2)]
            wub = [e3(nc.sbuf_tensor(f"wub{r}", [128, NCH, TW], BF16)) for r in range(1)]
            sq3 = [e3(nc.sbuf_tensor(f"sq3{r}", [128, NCH, TW], BF16)) for r in range(1)]

            for f in range(NCH):
                r = f % 2
                P.sync.dma_start(wstg2[r][:], wuT[128 * f:128 * (f + 1), :],
                                 _sem=f"ws2{r}", _deps=fence2 if f < 2 else ())
                P.vector.tensor_copy(wu_sb[:, f, :], wstg2[r][:], _ms=True)

            out_dmas = []
            for tt in range(NT):
                r = tt % 2
                for f in range(NCH):
                    P.sync.dma_start(x0t[r][:, f, :],
                                     xin[128 * f:128 * (f + 1), TW * tt:TW * (tt + 1)],
                                     _sem=f"x0{r}", _deps=fence2 if (tt < 2 and f == 0) else ())
                for f in range(NCH):
                    P.sync.dma_start(ofmt[r][:, f, :],
                                     ofm_dr[128 * f:128 * (f + 1), TW * tt:TW * (tt + 1)],
                                     _sem=f"ofi{r}")
                for f in range(NCH):
                    bw, dw = pool.acquire()
                    mm = None
                    for cc in range(NCH):
                        mm = P.tensor.matmul(bw["t"][:, 0:TW],
                                             wu_sb[:, cc, 128 * f:128 * (f + 1)],
                                             ofmt[r][:, cc, :],
                                             start=(cc == 0), stop=(cc == NCH - 1),
                                             _deps=dw if cc == 0 else (), _ms=(cc == NCH - 1))
                    ev = P.scalar.activation(wut[0][:, f, :], bw["t"][:, 0:TW], AF.Identity,
                                             bias=bu_sb[:, f:f + 1], _ms=True)
                    pool.done(bw, ev)
                    pool.release(bw)
                cast_ms = P.vector.tensor_copy(wub[0][:, :, :], wut[0][:, :, :], _ms=True)
                sq_ms = P.scalar.activation(sq3[0][:, :, :], wub[0][:, :, :], AF.Square,
                                            _ms=True)
                bs, d0 = pool.acquire()
                bqp, d1 = pool.acquire()
                summ = sumsq = None
                for f in range(NCH):
                    summ = P.tensor.matmul(bs["t"][0:1, 0:TW], ones_bf[:], wub[0][:, f, :],
                                           start=(f == 0), stop=(f == NCH - 1),
                                           _deps=d0 if f == 0 else (), _ms=(f == NCH - 1))
                for f in range(NCH):
                    sumsq = P.tensor.matmul(bqp["t"][0:1, 0:TW], ones_bf[:], sq3[0][:, f, :],
                                            start=(f == 0), stop=(f == NCH - 1),
                                            _deps=d1 if f == 0 else (), _ms=(f == NCH - 1))
                rstd_ms, mr_ms = ln_rows(P, bs["t"][0:1, 0:TW], bqp["t"][0:1, 0:TW],
                                         summ, sumsq, rows, DM, TW, epsc[:])
                pool.done(bs, rstd_ms); pool.release(bs)
                pool.done(bqp, rstd_ms); pool.release(bqp)
                br, d2 = pool.acquire()
                bm, d3 = pool.acquire()
                bcr = P.tensor.matmul(br["t"][:, 0:TW], ones_f[:], rows[0:1, TW:2 * TW],
                                      start=True, stop=True, _deps=d2, _ms=True)
                bcm = P.tensor.matmul(bm["t"][:, 0:TW], ones_f[:], rows[0:1, 3 * TW:4 * TW],
                                      start=True, stop=True, _deps=d3, _ms=True)
                last = None
                for f in range(NCH):
                    P.vector.tensor_tensor(wut[0][:, f, :], wut[0][:, f, :], br["t"][:, 0:TW],
                                           ALU.mult)
                    P.vector.tensor_tensor(wut[0][:, f, :], wut[0][:, f, :], bm["t"][:, 0:TW],
                                           ALU.subtract)
                    last = P.vector.tensor_tensor(x0t[r][:, f, :], x0t[r][:, f, :],
                                                  wut[0][:, f, :], ALU.add, _ms=True)
                pool.done(br, last); pool.release(br)
                pool.done(bm, last); pool.release(bm)
                od = None
                for f in range(NCH):
                    od = P.gpsimd.dma_start(xout[128 * f:128 * (f + 1), TW * tt:TW * (tt + 1)],
                                            x0t[r][:, f, :], _sem=f"xo{r}")
                out_dmas.append(od)


def build_l0attn(**kw):
    nc = bass.Bass("TRN2", target_bir_lowering=False)
    xin = nc.dram_tensor("xin", [DM, NTOK], F32, kind="ExternalInput").ap()
    xout = nc.dram_tensor("xout", [DM, NTOK], F32, kind="ExternalOutput").ap()
    wqT = nc.dram_tensor("wqT", [DM, DM], F32, kind="ExternalInput").ap()
    wuT = nc.dram_tensor("wuT", [DM, DM], F32, kind="ExternalInput").ap()
    bq = nc.dram_tensor("bq", [128, NCH], F32, kind="ExternalInput").ap()
    bu = nc.dram_tensor("bu", [128, NCH], F32, kind="ExternalInput").ap()
    idb = nc.dram_tensor("idb", [128, 128], BF16, kind="ExternalInput").ap()
    ofm_dr = nc.dram_tensor("ofm_dr", [DM, NTOK], BF16).ap()
    P = Prog(nc)
    emit_l0attn(nc, P, xin, xout, wqT, wuT, bq, bu, idb, ofm_dr, **kw)
    P.wait_all_dmas()
    P.build()
    return nc



def emit_l1attn(nc, P, xin, xout, wqT, wkvdT, wkuT, wvuT, woT):
    TW = 512
    NT = NTOK // TW
    NH = 8
    HQ = 1024
    with ExitStack() as octx:
        oe = octx.enter_context
        ones_bf = oe(nc.sbuf_tensor("ones_bf", [128, 1], BF16))
        ones_f = oe(nc.sbuf_tensor("ones_f", [1, 128], F32))
        epsc = oe(nc.sbuf_tensor("epsc", [1, 1], F32))
        rows = oe(nc.sbuf_tensor("rows", [1, 4 * TW], F32))
        y_bf = oe(nc.sbuf_tensor("y_bf", [128, NCH, NTOK], BF16))
        outac = oe(nc.sbuf_tensor("outac", [128, NCH, HQ], F32))
        kvl = oe(nc.sbuf_tensor("kvl", [64, NTOK], BF16))
        wku_sb = oe(nc.sbuf_tensor("wku_sb", [64, 2048], BF16))
        wvu_sb = oe(nc.sbuf_tensor("wvu_sb", [64, 2048], BF16))
        nrow = oe(nc.sbuf_tensor("nrow", [1, 8], F32))
        negB = oe(nc.sbuf_tensor("negB", [128, 1], F32))
        ps = [oe(nc.psum_tensor(f"ps{k}", [128, 512], F32)) for k in range(8)]

        pool = PsumPool(ps)
        P.vector.memset(ones_bf[:], 1.0)
        P.vector.memset(epsc[:], 1e-5)
        ones_ms = P.vector.memset(ones_f[:], 1.0, _ms=True)

        with ExitStack() as c1:
            e1 = c1.enter_context
            xt = [e1(nc.sbuf_tensor(f"xt{r}", [128, NCH, TW], F32)) for r in range(2)]
            xb = e1(nc.sbuf_tensor("xb", [128, NCH, TW], BF16))
            sq = e1(nc.sbuf_tensor("sq", [128, NCH, TW], BF16))
            wstg = [e1(nc.sbuf_tensor(f"wstg{r}", [128, 1024], F32)) for r in range(1)]
            for tt in range(NT):
                r = tt % 2
                for f in range(NCH):
                    P.sync.dma_start(xt[r][:, f, :],
                                     xin[128 * f:128 * (f + 1), TW * tt:TW * (tt + 1)],
                                     _sem=f"xi{r}")
                P.vector.tensor_copy(xb[:, :, :], xt[r][:, :, :], _ms=True)
                P.scalar.activation(sq[:, :, :], xb[:, :, :], AF.Square, _ms=True)
                bs, d0 = pool.acquire()
                bqp, d1 = pool.acquire()
                summ = sumsq = None
                for f in range(NCH):
                    summ = P.tensor.matmul(bs["t"][0:1, 0:TW], ones_bf[:], xb[:, f, :],
                                           start=(f == 0), stop=(f == NCH - 1),
                                           _deps=d0 if f == 0 else (), _ms=(f == NCH - 1))
                for f in range(NCH):
                    sumsq = P.tensor.matmul(bqp["t"][0:1, 0:TW], ones_bf[:], sq[:, f, :],
                                            start=(f == 0), stop=(f == NCH - 1),
                                            _deps=d1 if f == 0 else (), _ms=(f == NCH - 1))
                rstd_ms, mr_ms = ln_rows(P, bs["t"][0:1, 0:TW], bqp["t"][0:1, 0:TW],
                                         summ, sumsq, rows, DM, TW, epsc[:])
                pool.done(bs, rstd_ms); pool.release(bs)
                pool.done(bqp, rstd_ms); pool.release(bqp)
                br, d2 = pool.acquire()
                bm, d3 = pool.acquire()
                P.tensor.matmul(br["t"][:, 0:TW], ones_f[:], rows[0:1, TW:2 * TW],
                                start=True, stop=True, _deps=d2, _ms=True)
                P.tensor.matmul(bm["t"][:, 0:TW], ones_f[:], rows[0:1, 3 * TW:4 * TW],
                                start=True, stop=True, _deps=d3, _ms=True)
                last = None
                for f in range(NCH):
                    P.vector.tensor_tensor(xt[r][:, f, :], xt[r][:, f, :], br["t"][:, 0:TW],
                                           ALU.mult)
                    last = P.vector.tensor_tensor(y_bf[:, f, TW * tt:TW * (tt + 1)],
                                                  xt[r][:, f, :], bm["t"][:, 0:TW],
                                                  ALU.subtract, _ms=True)
                pool.done(br, last); pool.release(br)
                pool.done(bm, last); pool.release(bm)

            wkv_sb = e1(nc.sbuf_tensor("wkv_sb", [128, NCH, 64], BF16))
            for f in range(NCH):
                dm = P.sync.dma_start(wstg[0][:, 0:64], wkvdT[128 * f:128 * (f + 1), :],
                                      _sem="wst0")
                P.vector.tensor_copy(wkv_sb[:, f, :], wstg[0][:, 0:64], _ms=True)
            for tt in range(NT):
                bkv, dkv = pool.acquire()
                mm = None
                for f in range(NCH):
                    mm = P.tensor.matmul(bkv["t"][0:64, 0:TW], wkv_sb[:, f, :],
                                         y_bf[:, f, TW * tt:TW * (tt + 1)],
                                         start=(f == 0), stop=(f == NCH - 1),
                                         _deps=dkv if f == 0 else (), _ms=(f == NCH - 1))
                ev = P.vector.tensor_copy(kvl[:, TW * tt:TW * (tt + 1)], bkv["t"][0:64, 0:TW],
                                          _ms=True)
                pool.done(bkv, ev)
                pool.release(bkv)
            for dst, srcap in ((wku_sb, wkuT), (wvu_sb, wvuT)):
                for hlf in range(2):
                    dm = P.sync.dma_start(wstg[0][0:64, :], srcap[:, 1024 * hlf:1024 * (hlf + 1)],
                                          _sem="wst0")
                    P.vector.tensor_copy(dst[:, 1024 * hlf:1024 * (hlf + 1)],
                                         wstg[0][0:64, :], _ms=True)

        fence1 = P.fence_all()
        with ExitStack() as c2:
            e2 = c2.enter_context
            qh = e2(nc.sbuf_tensor("qh", [128, 2, HQ], BF16))
            kh = e2(nc.sbuf_tensor("kh", [128, 2, NTOK], BF16))
            vh = e2(nc.sbuf_tensor("vh", [128, 32, 256], BF16))
            pstg = e2(nc.sbuf_tensor("pstg", [128, 2, 512], BF16))
            sqs = e2(nc.sbuf_tensor("sqs", [128, 2, 512], BF16))
            wqh = [e2(nc.sbuf_tensor(f"wqh{r}", [128, NCH, 256], F32)) for r in range(1)]
            wqhb = [e2(nc.sbuf_tensor(f"wqhb{r}", [128, NCH, 256], BF16)) for r in range(1)]
            woh = [e2(nc.sbuf_tensor(f"woh{r}", [128, 2, DM], F32)) for r in range(1)]
            wohb = [e2(nc.sbuf_tensor(f"wohb{r}", [128, 2, DM], BF16)) for r in range(1)]
            o_bf = e2(nc.sbuf_tensor("o_bf", [128, 2, 512], BF16))
            rdrow = e2(nc.sbuf_tensor("rdrow", [1, 512], F32))
            xo = [e2(nc.sbuf_tensor(f"xo{r}", [128, NCH, 256], F32)) for r in range(1)]
            ob = e2(nc.sbuf_tensor("ob", [128, NCH, 256], BF16))
            sqo = e2(nc.sbuf_tensor("sqo", [128, NCH, 256], BF16))

            for half in range(4):
                q0 = half * HQ
                for h in range(NH):
                    rr = 0
                    for f in range(NCH):
                        P.sync.dma_start(wqh[rr][:, f, :],
                                         wqT[128 * f:128 * (f + 1), 256 * h:256 * (h + 1)],
                                         _sem=f"wq{rr}", _deps=fence1 if (half == 0 and h < 2 and f == 0) else ())
                    P.vector.tensor_copy(wqhb[rr][:, :, :], wqh[rr][:, :, :], _ms=True)
                    P.sync.dma_start(woh[rr][:, 0, :], woT[256 * h:256 * h + 128, :],
                                     _sem=f"wo{rr}")
                    P.sync.dma_start(woh[rr][:, 1, :], woT[256 * h + 128:256 * (h + 1), :],
                                     _sem=f"wo{rr}")
                    P.vector.tensor_copy(wohb[rr][:, :, :], woh[rr][:, :, :], _ms=True)
                    for dc in range(2):
                        for nt4 in range(HQ // TW):
                            bq_, dq = pool.acquire()
                            mm = None
                            for f in range(NCH):
                                mm = P.tensor.matmul(
                                    bq_["t"][:, 0:TW],
                                    wqhb[rr][:, f, 128 * dc:128 * (dc + 1)],
                                    y_bf[:, f, q0 + TW * nt4:q0 + TW * (nt4 + 1)],
                                    start=(f == 0), stop=(f == NCH - 1),
                                    _deps=dq if f == 0 else (), _ms=(f == NCH - 1))
                            ev = P.vector.tensor_copy(qh[:, dc, TW * nt4:TW * (nt4 + 1)],
                                                      bq_["t"][:, 0:TW], _ms=True)
                            pool.done(bq_, ev)
                            pool.release(bq_)
                    for dc in range(2):
                        for nt in range(NT):
                            bk_, dk = pool.acquire()
                            mm = P.tensor.matmul(
                                bk_["t"][:, 0:TW],
                                wku_sb[0:64, 256 * h + 128 * dc:256 * h + 128 * (dc + 1)],
                                kvl[:, TW * nt:TW * (nt + 1)],
                                start=True, stop=True, _deps=dk, _ms=True)
                            ev = P.vector.tensor_copy(kh[:, dc, TW * nt:TW * (nt + 1)],
                                                      bk_["t"][:, 0:TW], _ms=True)
                            pool.done(bk_, ev)
                            pool.release(bk_)
                    for kc in range(32):
                        bv_, dv = pool.acquire()
                        mm = P.tensor.matmul(bv_["t"][:, 0:256],
                                             kvl[:, 128 * kc:128 * (kc + 1)],
                                             wvu_sb[0:64, 256 * h:256 * (h + 1)],
                                             start=True, stop=True, _deps=dv, _ms=True)
                        ev = P.scalar.activation(vh[:, kc, :], bv_["t"][:, 0:256], AF.Copy,
                                                 _ms=True)
                        pool.done(bv_, ev)
                        pool.release(bv_)
                    bn, dn = pool.acquire()
                    rm = None
                    for nt4 in range(HQ // TW):
                        P.scalar.activation(sqs[:, 0:2, :], qh[:, :, TW * nt4:TW * (nt4 + 1)],
                                            AF.Square, _ms=True)
                        mm = None
                        for dc in range(2):
                            mm = P.tensor.matmul(bn["t"][0:1, 0:TW], ones_bf[:],
                                                 sqs[:, dc, :],
                                                 start=(dc == 0), stop=(dc == 1),
                                                 _deps=dn if (nt4 == 0 and dc == 0) else (),
                                                 _ms=(dc == 1))
                        rm = P.vector.reduce_max(nrow[0:1, 2:3], bn["t"][0:1, 0:TW],
                                                 axis=AX.X, _ms=True)
                        if nt4 == 0:
                            P.vector.tensor_copy(nrow[0:1, 0:1], nrow[0:1, 2:3])
                        else:
                            P.vector.tensor_tensor(nrow[0:1, 0:1], nrow[0:1, 0:1],
                                                   nrow[0:1, 2:3], ALU.max)
                    pool.done(bn, rm)
                    pool.release(bn)
                    P.scalar.activation(nrow[0:1, 0:1], nrow[0:1, 0:1], AF.Sqrt, _ms=True)
                    bn2, dn2 = pool.acquire()
                    for nt in range(NT):
                        P.scalar.activation(sqs[:, 0:2, :], kh[:, :, TW * nt:TW * (nt + 1)],
                                            AF.Square, _ms=True)
                        mm = None
                        for dc in range(2):
                            mm = P.tensor.matmul(bn2["t"][0:1, 0:TW], ones_bf[:],
                                                 sqs[:, dc, :],
                                                 start=(dc == 0), stop=(dc == 1),
                                                 _deps=dn2 if (nt == 0 and dc == 0) else (),
                                                 _ms=(dc == 1))
                        rm = P.vector.reduce_max(nrow[0:1, 3:4], bn2["t"][0:1, 0:TW],
                                                 axis=AX.X, _ms=True)
                        if nt == 0:
                            P.vector.tensor_copy(nrow[0:1, 1:2], nrow[0:1, 3:4])
                        else:
                            P.vector.tensor_tensor(nrow[0:1, 1:2], nrow[0:1, 1:2],
                                                   nrow[0:1, 3:4], ALU.max)
                    pool.done(bn2, rm)
                    pool.release(bn2)
                    P.scalar.activation(nrow[0:1, 1:2], nrow[0:1, 1:2], AF.Sqrt, _ms=True)
                    P.vector.tensor_tensor(nrow[0:1, 4:5], nrow[0:1, 0:1], nrow[0:1, 1:2],
                                           ALU.mult)
                    nb = P.vector.tensor_scalar(nrow[0:1, 4:5], nrow[0:1, 4:5], -1.0 / 16.0,
                                                None, ALU.mult, _ms=True)
                    bnb, dnb = pool.acquire()
                    nbm = P.tensor.matmul(bnb["t"][:, 0:1], ones_f[:], nrow[0:1, 4:5],
                                          start=True, stop=True, _deps=(nb,) + dnb, _ms=True)
                    ngb = P.vector.tensor_copy(negB[:], bnb["t"][:, 0:1], _ms=True)
                    pool.done(bnb, ngb)
                    pool.release(bnb)
                    for qt in range(HQ // TW):
                        bo1, do1 = pool.acquire()
                        bo2, do2 = pool.acquire()
                        bd, dd = pool.acquire()
                        for kc in range(32):
                            rr2 = kc % 2
                            bS, dS = pool.acquire()
                            mm = None
                            for dc in range(2):
                                mm = P.tensor.matmul(
                                    bS["t"][:, 0:TW],
                                    kh[:, dc, 128 * kc:128 * (kc + 1)],
                                    qh[:, dc, TW * qt:TW * (qt + 1)],
                                    start=(dc == 0), stop=(dc == 1),
                                    _deps=dS if dc == 0 else (), _ms=(dc == 1))
                            em = P.scalar.activation(pstg[:, rr2, :], bS["t"][:, 0:TW],
                                                     AF.Exp, bias=negB[:], scale=1.0 / 16.0,
                                                     _deps=(mm, ngb), _ms=True)
                            pool.done(bS, em)
                            pool.release(bS)
                            P.tensor.matmul(bd["t"][0:1, 0:TW], ones_bf[:], pstg[:, rr2, :],
                                            start=(kc == 0), stop=(kc == 31),
                                            _deps=(em,) + (dd if kc == 0 else ()),
                                            _ms=(kc == 31))
                            for dh in range(2):
                                P.tensor.matmul(bo1["t"][:, 0:TW] if dh == 0 else bo2["t"][:, 0:TW],
                                                vh[:, kc, 128 * dh:128 * (dh + 1)],
                                                pstg[:, rr2, :],
                                                start=(kc == 0), stop=(kc == 31),
                                                _deps=(em,) + ((do1 + do2) if kc == 0 else ()),
                                                _ms=(kc == 31))
                        rd = P.vector.reciprocal(rdrow[:], bd["t"][0:1, 0:TW], _ms=True)
                        pool.done(bd, rd)
                        pool.release(bd)
                        bbc, dbc = pool.acquire()
                        bcm = P.tensor.matmul(bbc["t"][:, 0:TW], ones_f[:], rdrow[:],
                                              start=True, stop=True, _deps=(rd,) + dbc,
                                              _ms=True)
                        for dh, bo_ in ((0, bo1), (1, bo2)):
                            oc = P.scalar.activation(o_bf[:, dh, :], bo_["t"][:, 0:TW],
                                                     AF.Copy, _ms=True)
                            nm = P.vector.tensor_tensor(o_bf[:, dh, :], o_bf[:, dh, :],
                                                        bbc["t"][:, 0:TW], ALU.mult,
                                                        _deps=(oc, bcm), _ms=True)
                            pool.done(bo_, oc)
                            pool.release(bo_)
                        pool.done(bbc, nm)
                        pool.release(bbc)
                        for f in range(NCH):
                            bw, dw = pool.acquire()
                            mm = None
                            for dc in range(2):
                                mm = P.tensor.matmul(bw["t"][:, 0:TW],
                                                     wohb[rr][:, dc, 128 * f:128 * (f + 1)],
                                                     o_bf[:, dc, :],
                                                     start=(dc == 0), stop=(dc == 1),
                                                     _deps=dw if dc == 0 else (),
                                                     _ms=(dc == 1))
                            if h == 0:
                                ra = P.vector.tensor_copy(outac[:, f, TW * qt:TW * (qt + 1)],
                                                          bw["t"][:, 0:TW], _ms=True)
                            else:
                                ra = P.vector.tensor_tensor(outac[:, f, TW * qt:TW * (qt + 1)],
                                                            outac[:, f, TW * qt:TW * (qt + 1)],
                                                            bw["t"][:, 0:TW], ALU.add, _ms=True)
                            pool.done(bw, ra)
                            pool.release(bw)
                SW = 256
                for tt4 in range(HQ // SW):
                    tq = half * HQ + SW * tt4
                    r = 0
                    for f in range(NCH):
                        P.sync.dma_start(xo[r][:, f, :],
                                         xin[128 * f:128 * (f + 1), tq:tq + SW],
                                         _sem=f"xr{r}")
                    P.vector.tensor_copy(ob[:, :, :],
                                         outac[:, :, SW * tt4:SW * (tt4 + 1)], _ms=True)
                    P.scalar.activation(sqo[:, :, :], ob[:, :, :], AF.Square, _ms=True)
                    bs, d0 = pool.acquire()
                    bqp, d1 = pool.acquire()
                    summ = sumsq = None
                    for f in range(NCH):
                        summ = P.tensor.matmul(bs["t"][0:1, 0:SW], ones_bf[:], ob[:, f, :],
                                               start=(f == 0), stop=(f == NCH - 1),
                                               _deps=d0 if f == 0 else (), _ms=(f == NCH - 1))
                    for f in range(NCH):
                        sumsq = P.tensor.matmul(bqp["t"][0:1, 0:SW], ones_bf[:], sqo[:, f, :],
                                                start=(f == 0), stop=(f == NCH - 1),
                                                _deps=d1 if f == 0 else (), _ms=(f == NCH - 1))
                    rstd_ms, mr_ms = ln_rows(P, bs["t"][0:1, 0:SW], bqp["t"][0:1, 0:SW],
                                             summ, sumsq, rows, DM, SW, epsc[:])
                    pool.done(bs, rstd_ms); pool.release(bs)
                    pool.done(bqp, rstd_ms); pool.release(bqp)
                    br, d2 = pool.acquire()
                    bm, d3 = pool.acquire()
                    P.tensor.matmul(br["t"][:, 0:SW], ones_f[:], rows[0:1, SW:2 * SW],
                                    start=True, stop=True, _deps=d2, _ms=True)
                    P.tensor.matmul(bm["t"][:, 0:SW], ones_f[:], rows[0:1, 3 * SW:4 * SW],
                                    start=True, stop=True, _deps=d3, _ms=True)
                    last = None
                    for f in range(NCH):
                        t1 = P.vector.tensor_tensor(outac[:, f, SW * tt4:SW * (tt4 + 1)],
                                                    outac[:, f, SW * tt4:SW * (tt4 + 1)],
                                                    br["t"][:, 0:SW], ALU.mult, _ms=True)
                        P.vector.tensor_tensor(outac[:, f, SW * tt4:SW * (tt4 + 1)],
                                               outac[:, f, SW * tt4:SW * (tt4 + 1)],
                                               bm["t"][:, 0:SW], ALU.subtract)
                        last = P.vector.tensor_tensor(xo[r][:, f, :], xo[r][:, f, :],
                                                      outac[:, f, SW * tt4:SW * (tt4 + 1)],
                                                      ALU.add, _ms=True)
                    pool.done(br, last); pool.release(br)
                    pool.done(bm, last); pool.release(bm)
                    for f in range(NCH):
                        P.gpsimd.dma_start(xout[128 * f:128 * (f + 1), tq:tq + SW],
                                           xo[r][:, f, :], _sem=f"xs{r}")


def build_l1attn():
    nc = bass.Bass("TRN2", target_bir_lowering=False)
    xin = nc.dram_tensor("xin", [DM, NTOK], F32, kind="ExternalInput").ap()
    xout = nc.dram_tensor("xout", [DM, NTOK], F32, kind="ExternalOutput").ap()
    wqT = nc.dram_tensor("wqT", [DM, 2048], F32, kind="ExternalInput").ap()
    wkvdT = nc.dram_tensor("wkvdT", [DM, 64], F32, kind="ExternalInput").ap()
    wkuT = nc.dram_tensor("wkuT", [64, 2048], F32, kind="ExternalInput").ap()
    wvuT = nc.dram_tensor("wvuT", [64, 2048], F32, kind="ExternalInput").ap()
    woT = nc.dram_tensor("woT", [2048, DM], F32, kind="ExternalInput").ap()
    P = Prog(nc)
    emit_l1attn(nc, P, xin, xout, wqT, wkvdT, wkuT, wvuT, woT)
    P.wait_all_dmas()
    P.build()
    return nc



def emit_l1ffn(nc, P, xin, xout, A_R, B_R, A_I, B_I, e32t, e16t, bias):
    TW = 512
    NT = NTOK // TW
    with ExitStack() as octx:
        oe = octx.enter_context
        ones_bf = oe(nc.sbuf_tensor("ones_bf", [128, 1], BF16))
        ones_f = oe(nc.sbuf_tensor("ones_f", [1, 128], F32))
        epsc = oe(nc.sbuf_tensor("epsc", [1, 1], F32))
        rows = oe(nc.sbuf_tensor("rows", [1, 4 * TW], F32))
        bias_sb = oe(nc.sbuf_tensor("bias_sb", [128, NCH], F32))
        e32_sb = oe(nc.sbuf_tensor("e32_sb", [32, 512], BF16))
        e16_sb = oe(nc.sbuf_tensor("e16_sb", [16, 512], BF16))
        Af = oe(nc.sbuf_tensor("Af", [32, 2, 4, 32], F32))
        Bf = oe(nc.sbuf_tensor("Bf", [16, 2, 4, 16], F32))
        Ab = oe(nc.sbuf_tensor("Ab", [32, 2, 4, 32], BF16))
        Bb = oe(nc.sbuf_tensor("Bb", [16, 2, 4, 16], BF16))
        s1b = oe(nc.sbuf_tensor("s1b", [32, 512], BF16))
        s1c = oe(nc.sbuf_tensor("s1c", [16, 512], BF16))
        abig = oe(nc.sbuf_tensor("abig", [128, 512], F32))
        wacc = oe(nc.sbuf_tensor("wacc", [128, 4, 512], F32))
        wt = oe(nc.sbuf_tensor("wt", [128, 3, 4, 512], BF16))
        xt = [oe(nc.sbuf_tensor(f"xt{r}", [128, NCH, TW], F32)) for r in range(2)]
        xb = oe(nc.sbuf_tensor("xb", [128, NCH, TW], BF16))
        sq = oe(nc.sbuf_tensor("sq", [128, NCH, TW], BF16))
        yn = oe(nc.sbuf_tensor("yn", [128, NCH, TW], BF16))
        bt = oe(nc.sbuf_tensor("bt", [128, NCH, TW], F32))
        btb = oe(nc.sbuf_tensor("btb", [128, NCH, TW], BF16))
        sqb = oe(nc.sbuf_tensor("sqb", [128, NCH, TW], BF16))
        ps = [oe(nc.psum_tensor(f"ps{k}", [128, 512], F32)) for k in range(8)]

        pool = PsumPool(ps)
        P.vector.memset(ones_bf[:], 1.0)
        P.vector.memset(epsc[:], 1e-5)
        ones_ms = P.vector.memset(ones_f[:], 1.0, _ms=True)
        P.sync.dma_start(bias_sb[:], bias, _sem="f_init")
        P.sync.dma_start(e32_sb[:], e32t, _sem="f_init")
        P.sync.dma_start(e16_sb[:], e16t, _sem="f_init")
        for ri, (Aap, Bap) in enumerate(((A_R, B_R), (A_I, B_I))):
            for c in range(4):
                P.sync.dma_start(Af[:, ri, c, :], Aap[c], _sem="f_init")
                P.sync.dma_start(Bf[:, ri, c, :], Bap[c], _sem="f_init")
        P.vector.tensor_copy(Ab[:, :, :, :], Af[:, :, :, :], _ms=True)
        P.vector.tensor_copy(Bb[:, :, :, :], Bf[:, :, :, :], _ms=True)

        for ri in range(2):
            for c in range(4):
                b1, d1 = pool.acquire()
                m1 = P.tensor.matmul(b1["t"][0:32, 0:512], Ab[:, ri, c, :], e32_sb[:],
                                     start=True, stop=True, _deps=d1, _ms=True)
                c1m = P.vector.tensor_copy(s1b[:], b1["t"][0:32, 0:512], _ms=True)
                pool.done(b1, c1m); pool.release(b1)
                b2, d2 = pool.acquire()
                m2 = P.tensor.matmul(b2["t"][0:16, 0:512], Bb[:, ri, c, :], e16_sb[:],
                                     start=True, stop=True, _deps=d2, _ms=True)
                c2m = P.vector.tensor_copy(s1c[:], b2["t"][0:16, 0:512], _ms=True)
                pool.done(b2, c2m); pool.release(b2)
                for cj in range(4):
                    b3, d3 = pool.acquire()
                    m3 = P.tensor.matmul(b3["t"][:, 0:512], e32_sb[:, 128 * cj:128 * (cj + 1)],
                                         s1b[:], start=True, stop=True,
                                         _deps=(c1m,) + d3, _ms=True)
                    a1 = P.vector.tensor_copy(abig[:], b3["t"][:, 0:512], _ms=True)
                    pool.done(b3, a1); pool.release(b3)
                    b4, d4 = pool.acquire()
                    m4 = P.tensor.matmul(b4["t"][:, 0:512], e16_sb[:, 128 * cj:128 * (cj + 1)],
                                         s1c[:], start=True, stop=True,
                                         _deps=(c2m,) + d4, _ms=True)
                    pm = P.vector.tensor_tensor(abig[:], abig[:], b4["t"][:, 0:512],
                                                ALU.mult, _deps=(m4, a1), _ms=True)
                    pool.done(b4, pm); pool.release(b4)
                    if c == 0:
                        P.vector.tensor_copy(wacc[:, cj, :], abig[:], _ms=True)
                    else:
                        P.vector.tensor_tensor(wacc[:, cj, :], wacc[:, cj, :], abig[:],
                                               ALU.add, _ms=True)
            for cj in range(4):
                P.vector.tensor_copy(wt[:, ri, cj, :], wacc[:, cj, :], _ms=True)
                if ri == 1:
                    P.vector.tensor_scalar(wt[:, 2, cj, :], wacc[:, cj, :], -1.0, None,
                                           ALU.mult, _ms=True)

        for tt in range(NT):
            r = tt % 2
            for f in range(NCH):
                P.sync.dma_start(xt[r][:, f, :],
                                 xin[128 * f:128 * (f + 1), TW * tt:TW * (tt + 1)],
                                 _sem=f"xi{r}")
            P.vector.tensor_copy(xb[:, :, :], xt[r][:, :, :], _ms=True)
            P.scalar.activation(sq[:, :, :], xb[:, :, :], AF.Square, _ms=True)
            bs, d0 = pool.acquire()
            bqp, d1 = pool.acquire()
            summ = sumsq = None
            for f in range(NCH):
                summ = P.tensor.matmul(bs["t"][0:1, 0:TW], ones_bf[:], xb[:, f, :],
                                       start=(f == 0), stop=(f == NCH - 1),
                                       _deps=d0 if f == 0 else (), _ms=(f == NCH - 1))
            for f in range(NCH):
                sumsq = P.tensor.matmul(bqp["t"][0:1, 0:TW], ones_bf[:], sq[:, f, :],
                                        start=(f == 0), stop=(f == NCH - 1),
                                        _deps=d1 if f == 0 else (), _ms=(f == NCH - 1))
            rstd_ms, mr_ms = ln_rows(P, bs["t"][0:1, 0:TW], bqp["t"][0:1, 0:TW],
                                     summ, sumsq, rows, DM, TW, epsc[:])
            pool.done(bs, rstd_ms); pool.release(bs)
            pool.done(bqp, rstd_ms); pool.release(bqp)
            br, d2 = pool.acquire()
            bm, d3 = pool.acquire()
            P.tensor.matmul(br["t"][:, 0:TW], ones_f[:], rows[0:1, TW:2 * TW],
                            start=True, stop=True, _deps=d2, _ms=True)
            P.tensor.matmul(bm["t"][:, 0:TW], ones_f[:], rows[0:1, 3 * TW:4 * TW],
                            start=True, stop=True, _deps=d3, _ms=True)
            lastyn = None
            for f in range(NCH):
                P.vector.tensor_tensor(sq[:, f, :], xb[:, f, :], br["t"][:, 0:TW], ALU.mult)
                lastyn = P.vector.tensor_tensor(yn[:, f, :], sq[:, f, :], bm["t"][:, 0:TW],
                                                ALU.subtract, _ms=True)
            pool.done(br, lastyn); pool.release(br)
            pool.done(bm, lastyn); pool.release(bm)
            for half in range(2):
                for dj in range(4):
                    bo_, do_ = pool.acquire()
                    mm = None
                    for cj in range(4):
                        lhs1 = wt[:, 0, cj, 128 * dj:128 * (dj + 1)] if half == 0 else                                wt[:, 1, cj, 128 * dj:128 * (dj + 1)]
                        mm = P.tensor.matmul(bo_["t"][:, 0:TW], lhs1, yn[:, cj, :],
                                             start=(cj == 0), stop=False,
                                             _deps=do_ if cj == 0 else ())
                    for cj in range(4):
                        lhs2 = wt[:, 2, cj, 128 * dj:128 * (dj + 1)] if half == 0 else                                wt[:, 0, cj, 128 * dj:128 * (dj + 1)]
                        mm = P.tensor.matmul(bo_["t"][:, 0:TW], lhs2, yn[:, 4 + cj, :],
                                             start=False, stop=(cj == 3),
                                             _ms=(cj == 3))
                    f = 4 * half + dj
                    ev = P.scalar.activation(bt[:, f, :], bo_["t"][:, 0:TW], AF.Identity,
                                             bias=bias_sb[:, f:f + 1], _ms=True)
                    pool.done(bo_, ev)
                    pool.release(bo_)
            cb = P.vector.tensor_copy(btb[:, :, :], bt[:, :, :], _ms=True)
            P.scalar.activation(sqb[:, :, :], btb[:, :, :], AF.Square, _ms=True)
            bs, d0 = pool.acquire()
            bqp, d1 = pool.acquire()
            for f in range(NCH):
                summ = P.tensor.matmul(bs["t"][0:1, 0:TW], ones_bf[:], btb[:, f, :],
                                       start=(f == 0), stop=(f == NCH - 1),
                                       _deps=d0 if f == 0 else (), _ms=(f == NCH - 1))
            for f in range(NCH):
                sumsq = P.tensor.matmul(bqp["t"][0:1, 0:TW], ones_bf[:], sqb[:, f, :],
                                        start=(f == 0), stop=(f == NCH - 1),
                                        _deps=d1 if f == 0 else (), _ms=(f == NCH - 1))
            rstd_ms, mr_ms = ln_rows(P, bs["t"][0:1, 0:TW], bqp["t"][0:1, 0:TW],
                                     summ, sumsq, rows, DM, TW, epsc[:])
            pool.done(bs, rstd_ms); pool.release(bs)
            pool.done(bqp, rstd_ms); pool.release(bqp)
            br, d2 = pool.acquire()
            bm, d3 = pool.acquire()
            P.tensor.matmul(br["t"][:, 0:TW], ones_f[:], rows[0:1, TW:2 * TW],
                            start=True, stop=True, _deps=d2, _ms=True)
            P.tensor.matmul(bm["t"][:, 0:TW], ones_f[:], rows[0:1, 3 * TW:4 * TW],
                            start=True, stop=True, _deps=d3, _ms=True)
            last = None
            for f in range(NCH):
                P.vector.tensor_tensor(bt[:, f, :], bt[:, f, :], br["t"][:, 0:TW], ALU.mult)
                P.vector.tensor_tensor(bt[:, f, :], bt[:, f, :], bm["t"][:, 0:TW],
                                       ALU.subtract)
                last = P.vector.tensor_tensor(xt[r][:, f, :], xt[r][:, f, :], bt[:, f, :],
                                              ALU.add, _ms=True)
            pool.done(br, last); pool.release(br)
            pool.done(bm, last); pool.release(bm)
            for f in range(NCH):
                P.gpsimd.dma_start(xout[128 * f:128 * (f + 1), TW * tt:TW * (tt + 1)],
                                   xt[r][:, f, :], _sem=f"xs{r}")


def build_l1ffn():
    nc = bass.Bass("TRN2", target_bir_lowering=False)
    xin = nc.dram_tensor("xin", [DM, NTOK], F32, kind="ExternalInput").ap()
    xout = nc.dram_tensor("xout", [DM, NTOK], F32, kind="ExternalOutput").ap()
    A_R = nc.dram_tensor("A_R", [4, 32, 32], F32, kind="ExternalInput").ap()
    B_R = nc.dram_tensor("B_R", [4, 16, 16], F32, kind="ExternalInput").ap()
    A_I = nc.dram_tensor("A_I", [4, 32, 32], F32, kind="ExternalInput").ap()
    B_I = nc.dram_tensor("B_I", [4, 16, 16], F32, kind="ExternalInput").ap()
    e32t = nc.dram_tensor("e32t", [32, 512], BF16, kind="ExternalInput").ap()
    e16t = nc.dram_tensor("e16t", [16, 512], BF16, kind="ExternalInput").ap()
    bias = nc.dram_tensor("bias", [128, NCH], F32, kind="ExternalInput").ap()
    P = Prog(nc)
    emit_l1ffn(nc, P, xin, xout, A_R, B_R, A_I, B_I, e32t, e16t, bias)
    P.wait_all_dmas()
    P.build()
    return nc



def build_fused():
    nc = bass.Bass("TRN2", target_bir_lowering=False)
    x0 = nc.dram_tensor("x0", [DM, NTOK], F32, kind="ExternalInput").ap()
    xout = nc.dram_tensor("xout", [DM, NTOK], F32, kind="ExternalOutput").ap()
    x1 = nc.dram_tensor("x1", [DM, NTOK], F32).ap()
    x2 = nc.dram_tensor("x2", [DM, NTOK], F32).ap()
    x3 = nc.dram_tensor("x3", [DM, NTOK], F32).ap()
    yA = nc.dram_tensor("yA", [DM, NTOK], F32).ap()
    yB = nc.dram_tensor("yB", [DM, NTOK], F32).ap()
    xa = nc.dram_tensor("xa_i", [DM, NTOK], F32).ap()
    ofm_dr = nc.dram_tensor("ofm_dr", [DM, NTOK], BF16).ap()
    w1s = nc.dram_tensor("w1s", [NDJ, 128, 2048], BF16).ap()
    w2s = nc.dram_tensor("w2s", [NCH, 128, NDJ * 128], BF16).ap()

    l0a_wqT = nc.dram_tensor("l0a_wqT", [DM, DM], F32, kind="ExternalInput").ap()
    l0a_wuT = nc.dram_tensor("l0a_wuT", [DM, DM], F32, kind="ExternalInput").ap()
    l0a_bq = nc.dram_tensor("l0a_bq", [128, NCH], F32, kind="ExternalInput").ap()
    l0a_bu = nc.dram_tensor("l0a_bu", [128, NCH], F32, kind="ExternalInput").ap()
    idb = nc.dram_tensor("idb", [128, 128], BF16, kind="ExternalInput").ap()
    rot = nc.dram_tensor("rot", [4, 128, NTOK], BF16, kind="ExternalInput").ap()
    msk = nc.dram_tensor("msk", [4, 128, QB], F32, kind="ExternalInput").ap()
    r2t = nc.dram_tensor("r2t", [128, 128], BF16, kind="ExternalInput").ap()
    bc33 = nc.dram_tensor("bc33", [33, 128], F32, kind="ExternalInput").ap()
    ffnw = []
    shapes = [("wqkvT", [DM, 3072]), ("woT", [DM, DM]), ("w1a", [DM, DIP]),
              ("w1g", [DM, DIP]), ("w2", [DIP, DM]), ("b1a", [128, NDJ]),
              ("b1g", [128, NDJ]), ("b2", [128, NCH]), ("lng", [128, NCH]),
              ("lnb", [128, NCH]), ("qs", [128, 1]), ("ks", [128, 1]),
              ("qsr", [1, 64]), ("ksr", [1, 64])]
    for li in range(6):
        ffnw.append([nc.dram_tensor(f"{n}_{li}", s, F32, kind="ExternalInput").ap()
                     for n, s in shapes])
    l1a = [nc.dram_tensor("l1a_wqT", [DM, 2048], F32, kind="ExternalInput").ap(),
           nc.dram_tensor("l1a_wkvdT", [DM, 64], F32, kind="ExternalInput").ap(),
           nc.dram_tensor("l1a_wkuT", [64, 2048], F32, kind="ExternalInput").ap(),
           nc.dram_tensor("l1a_wvuT", [64, 2048], F32, kind="ExternalInput").ap(),
           nc.dram_tensor("l1a_woT", [2048, DM], F32, kind="ExternalInput").ap()]
    l1f = [nc.dram_tensor("A_R", [4, 32, 32], F32, kind="ExternalInput").ap(),
           nc.dram_tensor("B_R", [4, 16, 16], F32, kind="ExternalInput").ap(),
           nc.dram_tensor("A_I", [4, 32, 32], F32, kind="ExternalInput").ap(),
           nc.dram_tensor("B_I", [4, 16, 16], F32, kind="ExternalInput").ap(),
           nc.dram_tensor("e32t", [32, 512], BF16, kind="ExternalInput").ap(),
           nc.dram_tensor("e16t", [16, 512], BF16, kind="ExternalInput").ap(),
           nc.dram_tensor("l1f_bias", [128, NCH], F32, kind="ExternalInput").ap()]

    P = Prog(nc)

    def phase(tag):
        P.phase_tag = tag
        P.phase_fence()

    phase("p0")
    emit_l0attn(_NCP(nc, "a_"), P, x0, x1, l0a_wqT, l0a_wuT, l0a_bq, l0a_bu,
                idb, ofm_dr)
    phase("p1")
    emit_ln_stream(_NCP(nc, "e_"), P, "ln", x1, yA)
    ys = [yA, yB]
    for li in range(6):
        phase(f"p2{li}")
        emit_ffn_layer(_NCP(nc, f"f{li}_"), P, ys[li % 2], ys[(li + 1) % 2],
                       xa, w1s, w2s, *ffnw[li], rot, msk, r2t, bc33)
    phase("p3")
    emit_ln_stream(_NCP(nc, "g_"), P, "res_ln", ys[0], x2, xres=x1)
    phase("p4")
    emit_l1attn(_NCP(nc, "m_"), P, x2, x3, *l1a)
    phase("p5")
    emit_l1ffn(_NCP(nc, "t_"), P, x3, xout, *l1f)

    P.wait_all_dmas()
    P.build()
    return nc


def fused_maps(per_core, P):
    idb = np.ascontiguousarray(np.eye(128, dtype=np.float32).astype(BF))
    maps = []
    for c in range(NCORES):
        m = {"x0": per_core[c], "l0a_wqT": P["l0a_wqT"], "l0a_wuT": P["l0a_wuT"],
             "l0a_bq": P["l0a_bq"], "l0a_bu": P["l0a_bu"], "idb": idb,
             "rot": P["rot"], "msk": P["msk"], "r2t": P["r2t"], "bc33": P["bc33"],
             "l1a_wqT": P["l1a_wqT"], "l1a_wkvdT": P["l1a_wkvdT"],
             "l1a_wkuT": P["l1a_wkuT"], "l1a_wvuT": P["l1a_wvuT"],
             "l1a_woT": P["l1a_woT"],
             "A_R": P["l1f_A_R"], "B_R": P["l1f_B_R"], "A_I": P["l1f_A_I"],
             "B_I": P["l1f_B_I"], "e32t": P["e32t"], "e16t": P["e16t"],
             "l1f_bias": P["l1f_bias"]}
        for li in range(6):
            for n in ("wqkvT", "woT", "w1a", "w1g", "w2", "b1a", "b1g", "b2",
                      "lng", "lnb", "qs", "ks", "qsr", "ksr"):
                m[f"{n}_{li}"] = P[f"ffn{li}_{n}"]
        maps.append(m)
    return maps



_CORE_IDS = list(range(NCORES))


def _launch(nc, in_maps):
    res = run_bass_kernel_spmd(nc, in_maps, core_ids=_CORE_IDS)
    return res.results


FUSED = True


def kernel(**inputs):
    per_core, P = host_prep(inputs)
    if FUSED:
        nc = build_fused()
        res = _launch(nc, fused_maps(per_core, P))
        return np.stack([r["xout"].T for r in res], axis=0).astype(np.float32)
    idb = np.ascontiguousarray(np.eye(128, dtype=np.float32).astype(BF))

    nc = build_l0attn()
    res = _launch(nc, [{"xin": per_core[c], "wqT": P["l0a_wqT"], "wuT": P["l0a_wuT"],
                        "bq": P["l0a_bq"], "bu": P["l0a_bu"], "idb": idb}
                       for c in range(NCORES)])
    x1 = [np.ascontiguousarray(r["xout"]) for r in res]

    nc_ln = build_ln_stream("ln")
    res = _launch(nc_ln, [{"xin": x1[c]} for c in range(NCORES)])
    y = [np.ascontiguousarray(r["yout"]) for r in res]
    nc_f = build_ffn_layer()
    for li in range(6):
        res = _launch(nc_f, [{
            "yin": y[c],
            "wqkvT": P[f"ffn{li}_wqkvT"], "woT": P[f"ffn{li}_woT"],
            "w1a": P[f"ffn{li}_w1a"], "w1g": P[f"ffn{li}_w1g"], "w2": P[f"ffn{li}_w2"],
            "b1a": P[f"ffn{li}_b1a"], "b1g": P[f"ffn{li}_b1g"], "b2": P[f"ffn{li}_b2"],
            "lng": P[f"ffn{li}_lng"], "lnb": P[f"ffn{li}_lnb"],
            "qs": P[f"ffn{li}_qs"], "ks": P[f"ffn{li}_ks"],
            "qsr": P[f"ffn{li}_qsr"], "ksr": P[f"ffn{li}_ksr"],
            "rot": P["rot"], "msk": P["msk"], "r2t": P["r2t"],
        } for c in range(NCORES)])
        y = [np.ascontiguousarray(r["ynext"]) for r in res]
    nc_x = build_ln_stream("res_ln")
    res = _launch(nc_x, [{"xin": y[c], "xres": x1[c]} for c in range(NCORES)])
    x2 = [np.ascontiguousarray(r["yout"]) for r in res]

    nc = build_l1attn()
    res = _launch(nc, [{"xin": x2[c], "wqT": P["l1a_wqT"], "wkvdT": P["l1a_wkvdT"],
                        "wkuT": P["l1a_wkuT"], "wvuT": P["l1a_wvuT"],
                        "woT": P["l1a_woT"]} for c in range(NCORES)])
    x3 = [np.ascontiguousarray(r["xout"]) for r in res]

    nc = build_l1ffn()
    res = _launch(nc, [{"xin": x3[c], "A_R": P["l1f_A_R"], "B_R": P["l1f_B_R"],
                        "A_I": P["l1f_A_I"], "B_I": P["l1f_B_I"], "e32t": P["e32t"],
                        "e16t": P["e16t"], "bias": P["l1f_bias"]}
                       for c in range(NCORES)])
    x4 = [r["xout"] for r in res]

    return np.stack([x4[c].T for c in range(NCORES)], axis=0).astype(np.float32)


# version 08
# speedup vs baseline: 1.3387x; 1.3387x over previous
import numpy as np
import ml_dtypes
from contextlib import ExitStack

import concourse.bass as bass
import concourse.mybir as mybir
from concourse.bass_utils import run_bass_kernel_spmd

F32 = mybir.dt.float32
BF16 = mybir.dt.bfloat16
AF = mybir.ActivationFunctionType
ALU = mybir.AluOpType
AX = mybir.AxisListType

NCORES = 8
DM = 1024
NTOK = 4096
NCH = DM // 128
BF = ml_dtypes.bfloat16


def _rotary_tables():
    DH, W, n = 64, 256, NTOK
    half = np.arange(0, DH, 2, dtype=np.float64)
    inv_freq = 1.0 / 10000.0 ** (half / DH)
    t = np.arange(n, dtype=np.float64)
    freqs = t[:, None] * inv_freq[None, :]
    freqs = np.concatenate([freqs, freqs], -1)
    base_scale = (half + 0.4 * DH) / (1.4 * DH)
    power = (t - n // 2) / (W // 2)
    scale = base_scale[None, :] ** power[:, None]
    scale = np.concatenate([scale, scale], -1)
    cos, sin = np.cos(freqs), np.sin(freqs)
    inv = 1.0 / scale
    outs = []
    for tab in (cos * scale, sin * scale, cos * inv, sin * inv):
        tt = tab.T.astype(np.float32)
        outs.append(np.concatenate([tt, tt], 0).astype(BF))
    return outs


def _local_masks():
    m = np.zeros((4, 128, 256), dtype=np.float32)
    a = np.arange(256)[None, :]
    p = np.arange(128)[:, None]
    for kc in range(4):
        if kc < 2:
            vis = (kc * 128 + p) >= a
        else:
            vis = ((kc - 2) * 128 + p) <= a
        m[kc] = np.where(vis, 0.0, -1e9).astype(np.float32)
    return m


def _r2t():
    R = np.zeros((64, 64), dtype=np.float32)
    for d in range(32):
        R[d, d + 32] = -1.0
        R[d + 32, d] = 1.0
    R2 = np.zeros((128, 128), dtype=np.float32)
    R2[:64, :64] = R
    R2[64:, 64:] = R
    return np.ascontiguousarray(R2.T).astype(BF)


def _expanders():
    dg = np.arange(512)
    e32 = (np.arange(32)[:, None] == (dg // 16)[None, :]).astype(BF)
    e16 = (np.arange(16)[:, None] == (dg % 16)[None, :]).astype(BF)
    return np.ascontiguousarray(e32), np.ascontiguousarray(e16)


def _chunked_vec(v, pad_to=None):
    v = np.asarray(v, dtype=np.float32).ravel()
    if pad_to is not None and v.shape[0] < pad_to:
        v = np.concatenate([v, np.zeros(pad_to - v.shape[0], np.float32)])
    assert v.shape[0] % 128 == 0
    return np.ascontiguousarray(v.reshape(-1, 128).T)


def host_prep(inputs):
    P = {}
    x = np.asarray(inputs["L0_attn_x"], dtype=np.float32)
    per_core = [np.ascontiguousarray(x[c].T) for c in range(NCORES)]

    P["l0a_wqT"] = np.ascontiguousarray(np.asarray(inputs["L0_attn_wq"], np.float32).T)
    P["l0a_bq"] = _chunked_vec(inputs["L0_attn_bq"])
    P["l0a_wuT"] = np.ascontiguousarray(np.asarray(inputs["L0_attn_wu"], np.float32).T)
    P["l0a_bu"] = _chunked_vec(inputs["L0_attn_bu"])

    W1 = np.asarray(inputs["L0_ffn_W1"], np.float32)
    W2 = np.asarray(inputs["L0_ffn_W2"], np.float32)
    DI, DIP = 2730, 2816
    for i in range(6):
        P[f"ffn{i}_wqkvT"] = np.ascontiguousarray(np.asarray(inputs["L0_ffn_Wqkv"], np.float32)[i])
        P[f"ffn{i}_woT"] = np.ascontiguousarray(np.asarray(inputs["L0_ffn_Wo"], np.float32)[i])
        w1a = np.zeros((DM, DIP), np.float32); w1a[:, :DI] = W1[i][:, :DI]
        w1g = np.zeros((DM, DIP), np.float32); w1g[:, :DI] = W1[i][:, DI:]
        P[f"ffn{i}_w1a"] = w1a
        P[f"ffn{i}_w1g"] = w1g
        w2 = np.zeros((DIP, DM), np.float32); w2[:DI, :] = W2[i]
        P[f"ffn{i}_w2"] = w2
        b1 = np.asarray(inputs["L0_ffn_b1"], np.float32)[i]
        P[f"ffn{i}_b1a"] = _chunked_vec(b1[:DI], pad_to=DIP)
        P[f"ffn{i}_b1g"] = _chunked_vec(b1[DI:], pad_to=DIP)
        P[f"ffn{i}_b2"] = _chunked_vec(np.asarray(inputs["L0_ffn_b2"], np.float32)[i])
        P[f"ffn{i}_lng"] = _chunked_vec(np.asarray(inputs["L0_ffn_ln_g"], np.float32)[i])
        P[f"ffn{i}_lnb"] = _chunked_vec(np.asarray(inputs["L0_ffn_ln_b"], np.float32)[i])
        qs = np.asarray(inputs["L0_ffn_q_scale"], np.float32)[i]
        ks = np.asarray(inputs["L0_ffn_k_scale"], np.float32)[i]
        P[f"ffn{i}_qs"] = np.ascontiguousarray(np.tile(qs, 2)[:, None])
        P[f"ffn{i}_ks"] = np.ascontiguousarray(np.tile(ks, 2)[:, None])
        P[f"ffn{i}_qsr"] = np.ascontiguousarray(qs[None, :])
        P[f"ffn{i}_ksr"] = np.ascontiguousarray(ks[None, :])

    P["l1a_wqT"] = np.ascontiguousarray(np.asarray(inputs["L1_attn_Wq"], np.float32).T)
    P["l1a_wkvdT"] = np.ascontiguousarray(np.asarray(inputs["L1_attn_Wkv_down"], np.float32).T)
    P["l1a_wkuT"] = np.ascontiguousarray(np.asarray(inputs["L1_attn_Wk_up"], np.float32).T)
    P["l1a_wvuT"] = np.ascontiguousarray(np.asarray(inputs["L1_attn_Wv_up"], np.float32).T)
    P["l1a_woT"] = np.ascontiguousarray(np.asarray(inputs["L1_attn_Wo"], np.float32).T)

    G1R = np.asarray(inputs["L1_ffn_G1_R"], np.float32); G2R = np.asarray(inputs["L1_ffn_G2_R"], np.float32)
    G1I = np.asarray(inputs["L1_ffn_G1_I"], np.float32); G2I = np.asarray(inputs["L1_ffn_G2_I"], np.float32)
    P["l1f_A_R"] = np.ascontiguousarray(G1R[0].transpose(1, 0, 2))
    P["l1f_B_R"] = np.ascontiguousarray(G2R[:, :, 0, :])
    P["l1f_A_I"] = np.ascontiguousarray(G1I[0].transpose(1, 0, 2))
    P["l1f_B_I"] = np.ascontiguousarray(G2I[:, :, 0, :])
    P["l1f_bias"] = np.concatenate([
        _chunked_vec(inputs["L1_ffn_bias_real"]),
        _chunked_vec(inputs["L1_ffn_bias_imag"])], axis=1)

    cs, ss, ci, si = _rotary_tables()
    P["rot"] = np.ascontiguousarray(np.stack([cs, ss, ci, si]))
    P["msk"] = _local_masks()
    P["r2t"] = _r2t()
    e32, e16 = _expanders()
    P["e32t"] = e32
    P["e16t"] = e16
    bc33 = np.zeros((33, 128), np.float32)
    bc33[0, 0:64] = 1.0
    bc33[32, 64:128] = 1.0
    P["bc33"] = bc33
    return per_core, P



ENGINES = ("sync", "tensor", "scalar", "vector", "gpsimd")
_DMA_NAMES = ("dma_start", "dma_start_transpose")


class _Rec:
    __slots__ = ("eng", "name", "args", "kwargs", "inc", "ms_no", "waits", "stream",
                 "sem_slot")
    def __init__(self, eng, name, args, kwargs):
        self.eng = eng
        self.name = name
        self.args = args
        self.kwargs = kwargs
        self.inc = 0
        self.ms_no = None
        self.waits = []
        self.stream = None


class Milestone:
    __slots__ = ("rec",)
    def __init__(self, rec):
        self.rec = rec


class StreamMark:
    __slots__ = ("stream", "value")
    def __init__(self, stream, value):
        self.stream = stream
        self.value = value


class _EngineProxy:
    def __init__(self, prog, eng):
        self._prog = prog
        self._eng = eng

    def __getattr__(self, name):
        prog, eng = self._prog, self._eng
        def record(*args, _deps=(), _ms=False, _sem=None, **kwargs):
            return prog.record(eng, name, args, kwargs, deps=_deps, ms=_ms, sem=_sem)
        return record


def _ap_footprint(ap):
    t = ap.tensor
    name = t.name
    shape = list(t.shape)
    dims = [list(d) for d in ap.ap]
    off = ap.offset
    is_dram = type(t).__name__.startswith("DRam")
    if is_dram:
        lo = off
        ext = sum(st * (ct - 1) for st, ct in dims if ct > 1)
        return (name, 0, 0, 0, lo, lo + ext)
    epp = 1
    for s in shape[1:]:
        epp *= s
    p_lo = off // epp if epp else 0
    f_lo = off % epp if epp else 0
    p_hi = p_lo
    f_hi = f_lo
    for (st, ct) in dims:
        if ct <= 1:
            continue
        ext = st * (ct - 1)
        if st == epp:
            p_hi += ct - 1
        elif st > 0 and st % epp == 0 and epp > 0:
            p_hi += (st // epp) * (ct - 1)
        else:
            f_hi += ext
    base = getattr(t, "base_partition", 0) or 0
    return (name, 1, base + p_lo, base + p_hi, f_lo, f_hi)


def _overlap(a, b):
    return (a[0] == b[0] and a[2] <= b[3] and b[2] <= a[3]
            and a[4] <= b[5] and b[4] <= a[5])


_NO_AP_OPS = {"__wait_stream__"}


class Prog:

    def __init__(self, nc):
        self.nc = nc
        self.recs = {e: [] for e in ENGINES}
        self.sem_count = {e: 0 for e in ENGINES}
        self.stream_count = {}
        self.stream_eng = {}
        self.writers = {}
        self.readers = {}
        for e in ENGINES:
            setattr(self, e, _EngineProxy(self, e))

    def _dep_on(self, rec, r):
        if r is rec:
            return
        if r.stream is not None:
            if r.stream.startswith("f_"):
                rec.waits.append(("s", r.stream, -1))
            else:
                rec.waits.append(("s", r.stream, self.stream_count[r.stream]))
        else:
            r.inc = 1
            rec.waits.append(("e", r))

    def record(self, eng, name, args, kwargs, deps=(), ms=False, sem=None):
        is_dma = name in _DMA_NAMES
        rec = _Rec(eng, name, args, kwargs)
        if is_dma:
            stream = sem if sem is not None else f"q_{eng}"
            if stream.startswith("f_"):
                stream = f"f_{getattr(self, 'phase_tag', '')}" + stream[1:]
            rec.stream = stream
        elif ms:
            rec.inc = 1
        pf = getattr(self, "_pf", None)
        if pf is not None and eng in getattr(self, "_pf_pending", ()):
            self._pf_pending.discard(eng)
            deps = tuple(deps) + tuple(pf)
        for d in deps:
            if d is None:
                continue
            if isinstance(d, StreamMark):
                rec.waits.append(("s", d.stream, d.value))
            else:
                self._dep_on(rec, d.rec)
        outs = []
        ins = []
        for i, a in enumerate(args):
            if isinstance(a, bass.AP):
                (outs if i == 0 else ins).append(a)
        for k, v in kwargs.items():
            if isinstance(v, bass.AP):
                (outs if k == "accum_out" else ins).append(v)
        if name == "matmul" or name == "transpose":
            pass
        for a in ins:
            fp = _ap_footprint(a)
            for (wfp, wrec) in self.writers.get(fp[0], ()):
                if _overlap(fp, wfp):
                    if (wrec.stream is None and wrec.eng == "tensor"
                            and eng == "tensor"):
                        continue
                    self._dep_on(rec, wrec)
            rl = self.readers.setdefault(fp[0], [])
            rl[:] = [(f, r) for (f, r) in rl
                     if not (r.eng == eng and r.stream is None
                             and f[2] >= fp[2] and f[3] <= fp[3]
                             and f[4] >= fp[4] and f[5] <= fp[5])]
            rl.append((fp, rec))
        for a in outs:
            fp = _ap_footprint(a)
            for (wfp, wrec) in self.writers.get(fp[0], ()):
                if _overlap(fp, wfp):
                    if (wrec.stream is None and wrec.eng == "tensor"
                            and eng == "tensor"):
                        continue
                    self._dep_on(rec, wrec)
            war = {}
            for (rfp, rrec) in self.readers.get(fp[0], ()):
                if _overlap(fp, rfp):
                    if (rrec.stream is None and rrec.eng == "tensor"
                            and eng == "tensor"):
                        continue
                    key = rrec.stream if rrec.stream is not None else rrec.eng
                    war[key] = rrec
            for rrec in war.values():
                self._dep_on(rec, rrec)
            def covers(big, small):
                return (big[2] <= small[2] and big[3] >= small[3]
                        and big[4] <= small[4] and big[5] >= small[5])
            wl = self.writers.setdefault(fp[0], [])
            wl[:] = [(f, r) for (f, r) in wl if not covers(fp, f)]
            wl.append((fp, rec))
            rl = self.readers.get(fp[0])
            if rl:
                rl[:] = [(f, r) for (f, r) in rl if not covers(fp, f)]
        if is_dma:
            rec.inc = 16
            self.stream_count[rec.stream] = self.stream_count.get(rec.stream, 0) + 16
            self.stream_eng[rec.stream] = eng
            rec.ms_no = self.stream_count[rec.stream]
        self.recs[eng].append(rec)
        return Milestone(rec)

    def phase_fence(self):
        self._pf = self.fence_all()
        self._pf_pending = set(ENGINES)

    def fence_all(self):
        out = []
        for eng in ENGINES:
            recs = self.recs[eng]
            if not recs:
                continue
            r = recs[-1]
            if r.stream is None:
                r.inc = 1
                out.append(Milestone(r))
        for s, cnt in self.stream_count.items():
            out.append(StreamMark(s, cnt))
        return tuple(out)

    def wait_all_dmas(self):
        for stream, eng in self.stream_eng.items():
            rec = _Rec(eng, "__wait_stream__", (stream,), {})
            self.recs[eng].append(rec)

    SEM_SLOT = 60000

    def build(self):
        nc = self.nc
        slots = {}
        for en in ENGINES:
            n = 0
            for r in self.recs[en]:
                if r.inc == 1:
                    r.sem_slot = n // self.SEM_SLOT
                    r.ms_no = n % self.SEM_SLOT + 1
                    n += 1
            slots[en] = max(1, (n + self.SEM_SLOT - 1) // self.SEM_SLOT)
        with ExitStack() as ctx:
            e = ctx.enter_context
            sems = {(en, sl): e(nc.semaphore(f"sem_{en}{sl}"))
                    for en in ENGINES for sl in range(slots[en])}
            ssems = {s: e(nc.semaphore(f"ss_{i}")) for i, s in
                     enumerate(self.stream_count)}
            block = e(nc.Block())
            for eng_name in ENGINES:
                recs = self.recs[eng_name]
                if not recs:
                    continue
                def body(eng, recs=recs, eng_name=eng_name):
                    hwm = {}
                    for r in recs:
                        for w in r.waits:
                            if w[0] == "s":
                                kind, key, val = w
                                if val < 0:
                                    val = self.stream_count[key]
                                sem = ssems[key]
                            else:
                                p = w[1]
                                kind = "e"
                                key = (p.eng, p.sem_slot)
                                val = p.ms_no
                                assert val is not None
                                sem = sems[key]
                            if hwm.get((kind, key), -1) >= val:
                                continue
                            hwm[(kind, key)] = val
                            eng.wait_ge(sem, val)
                        if r.name == "__wait_stream__":
                            s = r.args[0]
                            eng.wait_ge(ssems[s], self.stream_count[s])
                            continue
                        instr = getattr(eng, r.name)(*r.args, **r.kwargs)
                        if r.inc == 16:
                            instr.then_inc(ssems[r.stream], 16)
                        elif r.inc == 1:
                            instr.then_inc(sems[(eng_name, r.sem_slot)], 1)
                getattr(block, eng_name)(body)
        return nc


class _NCP:
    def __init__(self, nc, pfx):
        self._nc = nc
        self._pfx = pfx

    def sbuf_tensor(self, name, *a, **kw):
        return self._nc.sbuf_tensor(self._pfx + name, *a, **kw)

    def psum_tensor(self, name, *a, **kw):
        return self._nc.psum_tensor(self._pfx + name, *a, **kw)

    def __getattr__(self, n):
        return getattr(self._nc, n)


class PsumPool:
    def __init__(self, tensors):
        self.free = [{"t": t, "readers": []} for t in tensors]

    def acquire(self):
        b = self.free.pop(0)
        deps = tuple(b["readers"])
        b["readers"] = []
        return b, deps

    @staticmethod
    def done(bank, ms):
        bank["readers"].append(ms)

    def release(self, bank):
        self.free.append(bank)


def ln_rows(P, sum_ps, sumsq_ps, sum_dep, sumsq_dep, rows, n, tw, epsc, war=()):
    v, s = P.vector, P.scalar
    R = lambda k: rows[0:1, k * tw:(k + 1) * tw]
    v.tensor_scalar(R(0), sum_ps, 1.0 / n, None, ALU.mult,
                    _deps=(sum_dep,) + tuple(war))
    v.tensor_scalar(R(2), sumsq_ps, 1.0 / n, None, ALU.mult,
                    _deps=(sumsq_dep,))
    v.tensor_tensor(R(3), R(0), R(0), ALU.mult)
    var_ms = v.tensor_tensor(R(2), R(2), R(3), ALU.subtract, _ms=True)
    s.activation(R(1), R(2), AF.Ln, bias=epsc, _deps=(var_ms,))
    rstd_ms = s.activation(R(1), R(1), AF.Exp, scale=-0.5, _ms=True)
    mr_ms = v.tensor_tensor(R(3), R(0), R(1), ALU.mult, _deps=(rstd_ms,), _ms=True)
    return rstd_ms, mr_ms



def emit_ln_stream(nc, P, mode, xin, yout, xres=None):
    TW = 512
    NT = NTOK // TW
    with ExitStack() as ctx:
        e = ctx.enter_context
        xt = [e(nc.sbuf_tensor(f"xt{r}", [128, NCH, TW], F32)) for r in range(2)]
        xrt = [e(nc.sbuf_tensor(f"xrt{r}", [128, NCH, TW], F32)) for r in range(2)] \
            if mode == "res_ln" else None
        xb = [e(nc.sbuf_tensor(f"xb{r}", [128, NCH, TW], BF16)) for r in range(2)]
        sq = [e(nc.sbuf_tensor(f"sq{r}", [128, NCH, TW], BF16)) for r in range(2)]
        yt = [e(nc.sbuf_tensor(f"yt{r}", [128, NCH, TW], F32)) for r in range(2)]
        rows = [e(nc.sbuf_tensor(f"rows{r}", [1, 4 * TW], F32)) for r in range(2)]
        ones_bf = e(nc.sbuf_tensor("ones_bf", [128, 1], BF16))
        ones_f = e(nc.sbuf_tensor("ones_f", [1, 128], F32))
        epsc = e(nc.sbuf_tensor("epsc", [1, 1], F32))
        ps = [e(nc.psum_tensor(f"ps{k}", [128, 512], F32)) for k in range(6)]

        pool = PsumPool(ps)
        P.vector.memset(ones_bf[:], 1.0)
        P.vector.memset(epsc[:], 1e-5)
        ones_ms = P.vector.memset(ones_f[:], 1.0, _ms=True)

        prev = {}
        out_dmas = []
        for tt in range(NT):
            r = tt % 2
            dep_in = prev.get(("xt_free", r), ())
            dm = None
            for f in range(NCH):
                dm = P.sync.dma_start(xt[r][:, f, :], xin[128 * f:128 * (f + 1),
                                      TW * tt:TW * (tt + 1)], _deps=dep_in if f == 0 else (),
                                      _sem=f"xi{r}")
            if mode == "res_ln":
                dep_res = prev.get(("xrt_free", r), ())
                for f in range(NCH):
                    dm = P.sync.dma_start(xrt[r][:, f, :], xres[128 * f:128 * (f + 1),
                                          TW * tt:TW * (tt + 1)], _deps=dep_res if f == 0 else (),
                                          _sem=f"xr{r}")
            cast_ms = P.vector.tensor_copy(xb[r].rearrange("p a b -> p (a b)"),
                                           xt[r].rearrange("p a b -> p (a b)"),
                                           _deps=(dm,) + prev.get(("xb_free", r), ()), _ms=True)
            sq_ms = P.scalar.activation(sq[r].rearrange("p a b -> p (a b)"),
                                        xb[r].rearrange("p a b -> p (a b)"), AF.Square,
                                        _deps=(cast_ms,) + prev.get(("sq_free", r), ()), _ms=True)
            bs, d0 = pool.acquire()
            bq, d1 = pool.acquire()
            sum_ms = sumsq_ms = None
            for f in range(NCH):
                sum_ms = P.tensor.matmul(bs["t"][0:1, 0:TW], ones_bf[:], xb[r][:, f, :],
                                         start=(f == 0), stop=(f == NCH - 1),
                                         _deps=((cast_ms, ones_ms) + d0) if f == 0 else (),
                                         _ms=(f == NCH - 1))
            for f in range(NCH):
                sumsq_ms = P.tensor.matmul(bq["t"][0:1, 0:TW], ones_bf[:], sq[r][:, f, :],
                                           start=(f == 0), stop=(f == NCH - 1),
                                           _deps=((sq_ms,) + d1) if f == 0 else (),
                                           _ms=(f == NCH - 1))
            rstd_ms, mr_ms = ln_rows(P, bs["t"][0:1, 0:TW], bq["t"][0:1, 0:TW],
                                     sum_ms, sumsq_ms, rows[r], DM, TW, epsc[:],
                                     war=prev.get(("rows_free", r), ()))
            pool.done(bs, rstd_ms)
            pool.done(bq, rstd_ms)
            pool.release(bs)
            pool.release(bq)
            br, d2 = pool.acquire()
            bm, d3 = pool.acquire()
            bcr_ms = P.tensor.matmul(br["t"][:, 0:TW], ones_f[:], rows[r][0:1, TW:2 * TW],
                                     start=True, stop=True,
                                     _deps=(rstd_ms, ones_ms) + d2, _ms=True)
            bcm_ms = P.tensor.matmul(bm["t"][:, 0:TW], ones_f[:], rows[r][0:1, 3 * TW:4 * TW],
                                     start=True, stop=True, _deps=(mr_ms,) + d3, _ms=True)
            prev[("rows_free", r)] = (bcr_ms, bcm_ms)
            last_v = None
            for f in range(NCH):
                P.vector.tensor_tensor(yt[r][:, f, :], xt[r][:, f, :], br["t"][:, 0:TW],
                                       ALU.mult, _deps=(bcr_ms,) + (prev.get(("yt_free", r), ()) if f == 0 else ()))
                if mode == "res_ln":
                    P.vector.tensor_tensor(yt[r][:, f, :], yt[r][:, f, :], bm["t"][:, 0:TW],
                                           ALU.subtract, _deps=(bcm_ms,))
                    last_v = P.vector.tensor_tensor(yt[r][:, f, :], yt[r][:, f, :],
                                                    xrt[r][:, f, :], ALU.add,
                                                    _ms=True)
                else:
                    last_v = P.vector.tensor_tensor(yt[r][:, f, :], yt[r][:, f, :],
                                                    bm["t"][:, 0:TW], ALU.subtract,
                                                    _deps=(bcm_ms,), _ms=True)
            pool.done(br, last_v)
            pool.done(bm, last_v)
            pool.release(br)
            pool.release(bm)
            prev[("xt_free", r)] = (cast_ms, last_v)
            if mode == "res_ln":
                prev[("xrt_free", r)] = (last_v,)
            prev[("xb_free", r)] = (sum_ms, sq_ms)
            prev[("sq_free", r)] = (sumsq_ms,)
            od = None
            for f in range(NCH):
                od = P.gpsimd.dma_start(yout[128 * f:128 * (f + 1), TW * tt:TW * (tt + 1)],
                                        yt[r][:, f, :], _deps=(last_v,) if f == 0 else (),
                                        _sem=f"yo{r}")
            out_dmas.append(od)
            prev[("yt_free", r)] = (od,)



QB = 256
NQB = NTOK // QB
SUP = 1024
NSUP = NTOK // SUP
DIP = 2816
NDJ = DIP // 128


def emit_ffn_layer(nc, P, yin, ynext, xa, w1s, w2s, wqkvT, woT, w1a, w1g, w2,
                   b1a, b1g, b2, lng, lnb, qs, ks, qsr, ksr, rot, msk, r2t, bc33,
                   nqb=NQB, nsup=NSUP, passa=True, passb=True, astage=6,
                   nhp=NCH, qko=("q", "k"), evac=("v", "s")):
    with ExitStack() as octx:
        oe = octx.enter_context
        ones_bf = oe(nc.sbuf_tensor("ones_bf", [128, 1], BF16))
        ones_f = oe(nc.sbuf_tensor("ones_f", [1, 128], F32))
        epsc = oe(nc.sbuf_tensor("epsc", [1, 1], F32))
        negB = oe(nc.sbuf_tensor("negB", [128, 1], F32))
        nrow = oe(nc.sbuf_tensor("nrow", [1, 4], F32))
        b1a_sb = oe(nc.sbuf_tensor("b1a_sb", [128, NDJ], F32))
        b1g_sb = oe(nc.sbuf_tensor("b1g_sb", [128, NDJ], F32))
        b2_sb = oe(nc.sbuf_tensor("b2_sb", [128, NCH], F32))
        lng_sb = oe(nc.sbuf_tensor("lng_sb", [128, NCH], F32))
        lnb_sb = oe(nc.sbuf_tensor("lnb_sb", [128, NCH], F32))
        qs_sb = oe(nc.sbuf_tensor("qs_sb", [128, 1], F32))
        ks_sb = oe(nc.sbuf_tensor("ks_sb", [128, 1], F32))
        r2t_sb = oe(nc.sbuf_tensor("r2t_sb", [128, 128], BF16))
        msk_sb = oe(nc.sbuf_tensor("msk_sb", [128, 4, QB], F32))
        qsr_sb = oe(nc.sbuf_tensor("qsr_sb", [1, 64], F32))
        ksr_sb = oe(nc.sbuf_tensor("ksr_sb", [1, 64], F32))
        bc33_sb = oe(nc.sbuf_tensor("bc33_sb", [33, 128], F32))
        ps = [oe(nc.psum_tensor(f"ps{k}", [128, 512], F32)) for k in range(8)]

        pool = PsumPool(ps)
        P.vector.memset(ones_bf[:], 1.0)
        P.vector.memset(epsc[:], 1e-5)
        ones_ms = P.vector.memset(ones_f[:], 1.0, _ms=True)

        sm = None
        for dst, srcap in ((b1a_sb, b1a), (b1g_sb, b1g), (b2_sb, b2), (lng_sb, lng),
                           (lnb_sb, lnb), (qs_sb, qs), (ks_sb, ks), (r2t_sb, r2t),
                           (bc33_sb, bc33)):
            sm = P.sync.dma_start(dst[:], srcap, _sem="f_init")
        for kc in range(4):
            sm = P.sync.dma_start(msk_sb[:, kc, :], msk[kc], _sem="f_init")

        dqs = P.sync.dma_start(qsr_sb[:], qsr, _sem="f_init")
        dks = P.sync.dma_start(ksr_sb[:], ksr, _sem="f_init")
        P.vector.reduce_max(nrow[0:1, 0:1], qsr_sb[:], axis=AX.X,
                            apply_absolute_value=True, _deps=(dqs,))
        P.vector.reduce_max(nrow[0:1, 1:2], ksr_sb[:], axis=AX.X,
                            apply_absolute_value=True, _deps=(dks,))
        P.vector.tensor_tensor(nrow[0:1, 2:3], nrow[0:1, 0:1], nrow[0:1, 1:2], ALU.mult)
        nb_ms = P.vector.tensor_scalar(nrow[0:1, 2:3], nrow[0:1, 2:3], -8.0, None,
                                       ALU.mult, _ms=True)
        bnb, dnb = pool.acquire()
        nbm = P.tensor.matmul(bnb["t"][:, 0:1], ones_f[:], nrow[0:1, 2:3],
                              start=True, stop=True,
                              _deps=(nb_ms, ones_ms) + dnb, _ms=True)
        negB_ms = P.vector.tensor_copy(negB[:], bnb["t"][:, 0:1], _deps=(nbm,), _ms=True)
        pool.done(bnb, negB_ms)
        pool.release(bnb)

        w1s_done = []
        w2s_done = []
        with ExitStack() as pctx:
            pe = pctx.enter_context
            wpf = [pe(nc.sbuf_tensor(f"wpf{r}", [128, NCH, 128], F32)) for r in range(2)]
            wpb = [pe(nc.sbuf_tensor(f"wpb{r}", [128, 2048], BF16)) for r in range(2)]
            prevd = {}
            for dj in range(NDJ):
                r = dj % 2
                war_b = prevd.get(("wpb", r), ())
                war_f = prevd.get(("wpf", r), ())
                cast2 = None
                for half, wsrc in ((0, w1a), (1, w1g)):
                    dm = None
                    for f in range(NCH):
                        dm = P.sync.dma_start(
                            wpf[r][:, f, :],
                            wsrc[128 * f:128 * (f + 1), 128 * dj:128 * (dj + 1)],
                            _deps=war_f if f == 0 else (), _sem=f"wpf{r}")
                    cms = P.vector.tensor_copy(wpb[r][:, 1024 * half:1024 * (half + 1)],
                                               wpf[r].rearrange("p a b -> p (a b)"),
                                               _deps=(dm,) + (war_b if half == 0 else ()),
                                               _ms=True)
                    war_f = (cms,)
                    cast2 = cms
                od = P.gpsimd.dma_start(w1s[dj], wpb[r][:], _deps=(cast2,), _sem=f"wps{r}")
                prevd[("wpf", r)] = war_f
                prevd[("wpb", r)] = (od,)
                if dj >= NDJ - 2:
                    w1s_done.append(od)
            wpf2 = [pe(nc.sbuf_tensor(f"wpf2{r}", [128, NDJ, 128], F32)) for r in range(2)]
            wpb2 = [pe(nc.sbuf_tensor(f"wpb2{r}", [128, NDJ * 128], BF16)) for r in range(2)]
            for f in range(NCH):
                r = f % 2
                dm = None
                for cj in range(NDJ):
                    dm = P.sync.dma_start(wpf2[r][:, cj, :],
                                          w2[128 * cj:128 * (cj + 1), 128 * f:128 * (f + 1)],
                                          _deps=prevd.get(("wpf2", r), ()) if cj == 0 else (),
                                          _sem=f"wpf2{r}")
                cms = P.vector.tensor_copy(wpb2[r][:], wpf2[r].rearrange("p a b -> p (a b)"),
                                           _deps=(dm,) + prevd.get(("wpb2", r), ()), _ms=True)
                od = P.gpsimd.dma_start(w2s[f], wpb2[r][:], _deps=(cms,), _sem=f"wps2{r}")
                prevd[("wpf2", r)] = (cms,)
                prevd[("wpb2", r)] = (od,)
                if f >= NCH - 2:
                    w2s_done.append(od)

        xa_ms = [None] * NQB
        fenceP = P.fence_all()
        with ExitStack() as actx:
            ae = actx.enter_context
            wqkv_sb = ae(nc.sbuf_tensor("wqkv_sb", [128, NCH, 3072], BF16))
            wo_sb = ae(nc.sbuf_tensor("wo_sb", [128, NCH, DM], BF16))
            wstg = [ae(nc.sbuf_tensor(f"wstg{r}", [128, 1024], F32)) for r in range(2)]
            xin = [ae(nc.sbuf_tensor(f"xinA{r}", [128, NCH, QB + 1], F32)) for r in range(2)]
            xs = [ae(nc.sbuf_tensor(f"xsA{r}", [128, NCH, QB], BF16)) for r in range(2)]
            qpre = ae(nc.sbuf_tensor("qpre", [128, NCH, QB], BF16))
            kpre = ae(nc.sbuf_tensor("kpre", [128, NCH, QB], BF16))
            qsq = ae(nc.sbuf_tensor("qsq", [128, NCH, QB], BF16))
            ksq = ae(nc.sbuf_tensor("ksq", [128, NCH, QB], BF16))
            qhat = ae(nc.sbuf_tensor("qhat", [128, NCH, QB], BF16))
            khat = ae(nc.sbuf_tensor("khat", [128, NCH, QB], BF16))
            qr = ae(nc.sbuf_tensor("qr", [128, NCH, QB], BF16))
            kr_ring = ae(nc.sbuf_tensor("kr_ring", [128, 2, NCH, QB], BF16))
            rote = [ae(nc.sbuf_tensor(f"rote{r}", [128, 4, QB], BF16)) for r in range(2)]
            vslot = ae(nc.sbuf_tensor("vslot", [128, 2, 2, 16, 65], BF16))
            pstage = ae(nc.sbuf_tensor("pstage", [128, 2, 4, QB], BF16))
            ofm = ae(nc.sbuf_tensor("ofm", [128, NCH, QB], BF16))
            o_bf = ae(nc.sbuf_tensor("o_bf", [64, 4, QB], BF16))
            o2_bf = ae(nc.sbuf_tensor("o2_bf", [64, 4, QB], BF16))
            rdrow = ae(nc.sbuf_tensor("rdrow", [1, 4, QB], F32))
            nrm = ae(nc.sbuf_tensor("nrm", [1, 4, 2 * QB], F32))
            nrm2 = ae(nc.sbuf_tensor("nrm2", [33, 4, QB], F32))
            t1f = ae(nc.sbuf_tensor("t1f", [128, 4, QB], F32))
            t2f = ae(nc.sbuf_tensor("t2f", [128, 4, QB], F32))
            xast = [ae(nc.sbuf_tensor(f"xast{r}", [128, NCH, QB], F32)) for r in range(2)]

            P.vector.memset(nrm2[0:33, :, :], 1.0, _ms=True)
            prevw = {}
            cms = None
            for f in range(NCH):
                for part in range(3):
                    r = (f * 3 + part) % 2
                    dm = P.sync.dma_start(wstg[r][:],
                                          wqkvT[128 * f:128 * (f + 1), 1024 * part:1024 * (part + 1)],
                                          _deps=prevw.get(("wstg", r), fenceP), _sem=f"wstg{r}")
                    cms = P.vector.tensor_copy(wqkv_sb[:, f, 1024 * part:1024 * (part + 1)],
                                               wstg[r][:], _deps=(dm,), _ms=True)
                    prevw[("wstg", r)] = (cms,)
            for f in range(NCH):
                r = f % 2
                dm = P.sync.dma_start(wstg[r][:], woT[128 * f:128 * (f + 1), :],
                                      _deps=prevw.get(("wstg", r), fenceP), _sem=f"wstg{r}")
                cms = P.vector.tensor_copy(wo_sb[:, f, :], wstg[r][:], _deps=(dm,), _ms=True)
                prevw[("wstg", r)] = (cms,)
            wload_ms = cms

            vones_ms = None
            for sl in range(2):
                for half in range(2):
                    for j in range(16):
                        vones_ms = P.vector.memset(vslot[:, sl, half, j, 64:65], 1.0,
                                                   _ms=(sl == 1 and half == 1 and j == 15))

            prevq = {}
            for qb in range(nqb if passa else 0):
                rr = qb % 2
                sl = qb % 2
                slp = (qb - 1) % 2
                s0 = qb * QB
                war = prevq.get(("xin", rr), ())
                dm_x = None
                if qb == 0:
                    zm = P.vector.memset(xin[rr].rearrange("p a b -> p (a b)"), 0.0,
                                         _deps=war, _ms=True)
                    for f in range(NCH):
                        dm_x = P.sync.dma_start(xin[rr][:, f, 1:QB + 1],
                                                yin[128 * f:128 * (f + 1), 0:QB],
                                                _deps=(zm,) if f == 0 else (), _sem=f"xiA{rr}")
                else:
                    for f in range(NCH):
                        dm_x = P.sync.dma_start(xin[rr][:, f, :],
                                                yin[128 * f:128 * (f + 1), s0 - 1:s0 + QB],
                                                _deps=war if f == 0 else (), _sem=f"xiA{rr}")
                dm_rot = None
                for t in range(4):
                    dm_rot = P.sync.dma_start(rote[rr][:, t, :], rot[t, :, s0:s0 + QB],
                                              _deps=prevq.get(("rote", rr), ()) if t == 0 else (),
                                              _sem=f"rot{rr}")
                P.scalar.activation(
                    xs[rr][:, 0:4, :], xin[rr][:, 0:4, 1:QB + 1],
                    AF.Copy, _deps=(dm_x,) + prevq.get(("xs", rr), ()))
                xs_ms = P.scalar.activation(
                    xs[rr][:, 4:8, :], xin[rr][:, 4:8, 0:QB],
                    AF.Copy, _ms=True)
                if astage < 2:
                    continue
                for which, pre, sqt in (("q", qpre, qsq), ("k", kpre, ksq)):
                    if which not in qko:
                        continue
                    off = 0 if which == "q" else 1024
                    for hp in range(nhp):
                        bqk, dq = pool.acquire()
                        mm = None
                        for f in range(NCH):
                            mm = P.tensor.matmul(
                                bqk["t"][:, 0:QB],
                                wqkv_sb[:, f, off + 128 * hp:off + 128 * (hp + 1)],
                                xs[rr][:, f, :], start=(f == 0), stop=(f == NCH - 1),
                                _deps=((xs_ms, wload_ms) + dq) if f == 0 else (),
                                _ms=(f == NCH - 1))
                        e1 = e2 = mm
                        if "v" in evac:
                            e1 = P.vector.tensor_copy(pre[:, hp, :], bqk["t"][:, 0:QB],
                                                      _deps=(mm,), _ms=True)
                            pool.done(bqk, e1)
                        if "s" in evac:
                            e2 = P.scalar.activation(sqt[:, hp, :], bqk["t"][:, 0:QB], AF.Square,
                                                     _deps=(mm, e1) + prevq.get(("sq_read", which, hp), ()),
                                                     _ms=True)
                            pool.done(bqk, e2)
                        pool.release(bqk)
                        prevq[("qkpre_ev", which, hp)] = (e1, e2)
                if astage < 3:
                    continue
                v_ev = []
                for th in range(2):
                    for dv in range(2):
                        bv, dv_ = pool.acquire()
                        mm = None
                        for f in range(NCH):
                            mm = P.tensor.matmul(
                                bv["t"][:, 0:512],
                                xs[rr][:, f, 128 * th:128 * (th + 1)],
                                wqkv_sb[:, f, 2048 + 512 * dv:2048 + 512 * (dv + 1)],
                                start=(f == 0), stop=(f == NCH - 1),
                                _deps=((xs_ms,) + dv_ + prevq.get(("v_read", sl), ())) if f == 0 else (),
                                _ms=(f == NCH - 1))
                        ev = P.scalar.activation(
                            vslot[:, sl, th, 8 * dv:8 * (dv + 1), 0:64],
                            bv["t"][:, 0:512].rearrange("p (a b) -> p a b", a=8),
                            AF.Copy, _deps=(mm,), _ms=True)
                        v_ev.append(ev)
                        pool.done(bv, ev)
                        pool.release(bv)
                last_v_mm = mm
                prevq[("xs", rr)] = (last_v_mm,)
                if astage < 4:
                    continue
                qr_ms = {}
                kr_ms = {}
                t2_ms = None
                for which, pre, sqt, ssc, tb0, tb1 in (
                        ("q", qpre, qsq, qs_sb, 0, 1),
                        ("k", kpre, ksq, ks_sb, 2, 3)):
                    qkof = 0 if which == "q" else 2
                    for hp in range(NCH):
                        sc = (hp % 2) + qkof
                        e1, e2 = prevq[("qkpre_ev", which, hp)]
                        bn_e, dne = pool.acquire()
                        bn_o, dno = pool.acquire()
                        m_e = P.tensor.matmul(bn_e["t"][0:1, 0:QB], ones_bf[0:64, :],
                                              sqt[0:64, hp, :], start=True, stop=True,
                                              tile_position=(0, 0),
                                              _deps=(e2, ones_ms) + dne, _ms=True)
                        m_o = P.tensor.matmul(bn_o["t"][0:1, 0:QB], ones_bf[64:128, :],
                                              sqt[64:128, hp, :], start=True, stop=True,
                                              tile_position=(64, 0),
                                              _deps=(e2,) + dno, _ms=True)
                        prevq[("sq_read", which, hp)] = (m_o,)
                        c1 = P.vector.tensor_copy(nrm[0:1, sc, 0:QB], bn_e["t"][0:1, 0:QB],
                                                  _deps=(m_e,), _ms=True)
                        c2 = P.vector.tensor_copy(nrm[0:1, sc, QB:2 * QB], bn_o["t"][0:1, 0:QB],
                                                  _deps=(m_o,), _ms=True)
                        pool.done(bn_e, c1); pool.release(bn_e)
                        pool.done(bn_o, c2); pool.release(bn_o)
                        mx_ms = P.vector.tensor_scalar(nrm[0:1, sc, :], nrm[0:1, sc, :],
                                                       1e-18, None, ALU.max,
                                                       _deps=(c1, c2), _ms=True)
                        P.scalar.activation(nrm[0:1, sc, :], nrm[0:1, sc, :],
                                            AF.Ln, _deps=(mx_ms,))
                        rc_ms = P.scalar.activation(nrm[0:1, sc, :], nrm[0:1, sc, :],
                                                    AF.Exp, scale=-0.5, _ms=True)
                        P.vector.tensor_copy(nrm2[0:1, sc, :], nrm[0:1, sc, 0:QB],
                                             _deps=(rc_ms,))
                        cp2 = P.vector.tensor_copy(nrm2[32:33, sc, :], nrm[0:1, sc, QB:2 * QB],
                                                   _ms=True)
                        bb, dbb = pool.acquire()
                        bmm = P.tensor.matmul(bb["t"][:, 0:QB], bc33_sb[:],
                                              nrm2[0:33, sc, :], start=True, stop=True,
                                              _deps=(cp2, sm) + dbb, _ms=True)
                        prevq[("nrm",)] = (bmm,)
                        hat = qhat if which == "q" else khat
                        hat_ms = P.vector.scalar_tensor_tensor(
                            hat[:, hp, :], pre[:, hp, :], ssc[:], bb["t"][:, 0:QB],
                            ALU.mult, ALU.mult, _deps=(bmm, sm), _ms=True)
                        pool.done(bb, hat_ms)
                        pool.release(bb)
                        br, drr = pool.acquire()
                        rmm = P.tensor.matmul(br["t"][:, 0:QB], r2t_sb[:], hat[:, hp, :],
                                              start=True, stop=True,
                                              _deps=(hat_ms, sm) + drr, _ms=True)
                        P.vector.tensor_tensor(t1f[:, sc, :], hat[:, hp, :], rote[rr][:, tb0, :],
                                               ALU.mult, _deps=(dm_rot, hat_ms))
                        t2_ms = P.vector.tensor_tensor(t2f[:, sc, :], br["t"][:, 0:QB],
                                                       rote[rr][:, tb1, :], ALU.mult,
                                                       _deps=(rmm,), _ms=True)
                        pool.done(br, t2_ms)
                        pool.release(br)
                        if which == "q":
                            rm = P.vector.tensor_tensor(qr[:, hp, :], t1f[:, sc, :], t2f[:, sc, :],
                                                        ALU.add,
                                                        _deps=prevq.get(("qr_read", hp), ()),
                                                        _ms=True)
                            qr_ms[hp] = rm
                        else:
                            rm = P.vector.tensor_tensor(kr_ring[:, sl, hp, :], t1f[:, sc, :], t2f[:, sc, :],
                                                        ALU.add,
                                                        _deps=prevq.get(("kr_read", sl), ()),
                                                        _ms=True)
                            kr_ms[hp] = rm
                prevq[("rote", rr)] = (t2_ms,)
                if astage < 5:
                    continue
                wo_dep = []
                last_av = None
                last_s = None
                for hp in range(NCH):
                    kcs = list(range(2, 4)) if qb == 0 else list(range(4))
                    exp_ms = {}
                    for kc in kcs:
                        ksl = slp if kc < 2 else sl
                        koff = 128 * (kc % 2)
                        kdep = kr_ms[hp] if kc >= 2 else prevq.get(("kr_prev", hp), (None,))[0]
                        bse, dse = pool.acquire()
                        bso, dso = pool.acquire()
                        me = P.tensor.matmul(bse["t"][:, 0:QB],
                                             kr_ring[0:64, ksl, hp, koff:koff + 128],
                                             qr[0:64, hp, :], start=True, stop=True,
                                             tile_position=(0, 0),
                                             _deps=(qr_ms[hp], kdep) + dse, _ms=True)
                        mo = P.tensor.matmul(bso["t"][:, 0:QB],
                                             kr_ring[64:128, ksl, hp, koff:koff + 128],
                                             qr[64:128, hp, :], start=True, stop=True,
                                             tile_position=(64, 0),
                                             _deps=dso, _ms=True)
                        last_s = mo
                        for par, bsx, mx in ((0, bse, me), (1, bso, mo)):
                            mk = P.vector.tensor_tensor(bsx["t"][:, 0:QB], bsx["t"][:, 0:QB],
                                                        msk_sb[:, kc, :], ALU.add,
                                                        _deps=(mx, sm), _ms=True)
                            em = P.scalar.activation(pstage[:, par, kc, :], bsx["t"][:, 0:QB],
                                                     AF.Exp, bias=negB[:], scale=8.0,
                                                     _deps=(mk, negB_ms) +
                                                           prevq.get(("pst", hp, par, kc), ()),
                                                     _ms=True)
                            exp_ms[(par, kc)] = em
                            pool.done(bsx, em)
                            pool.release(bsx)
                    for par in range(2):
                        h = 2 * hp + par
                        bav, dav = pool.acquire()
                        mm = None
                        for ii, kc in enumerate(kcs):
                            ksl = slp if kc < 2 else sl
                            vdep = vones_ms if qb == 0 else None
                            mm = P.tensor.matmul(
                                bav["t"][0:65, 0:QB],
                                vslot[:, ksl, kc % 2, h, 0:65],
                                pstage[:, par, kc, :],
                                start=(ii == 0), stop=(ii == len(kcs) - 1),
                                _deps=((exp_ms[(par, kc)], vdep, v_ev[-1]) + dav) if ii == 0
                                      else (exp_ms[(par, kc)],),
                                _ms=(ii == len(kcs) - 1))
                        last_av = mm
                        for kc in kcs:
                            prevq[("pst", hp, par, kc)] = (mm,)
                        hh = (2 * hp + par) % 4
                        P.scalar.activation(rdrow[:, hh, :], bav["t"][64:65, 0:QB], AF.Ln,
                                            _deps=(mm,))
                        rd = P.scalar.activation(rdrow[:, hh, :], rdrow[:, hh, :], AF.Exp,
                                                 scale=-1.0, _ms=True)
                        bbc, dbc = pool.acquire()
                        bcm = P.tensor.matmul(bbc["t"][0:64, 0:QB], ones_f[:, 0:64],
                                              rdrow[:, hh, :],
                                              start=True, stop=True, tile_position=(0, 0),
                                              _deps=(rd,) + dbc, _ms=True)
                        oc = P.scalar.activation(o_bf[:, hh, :], bav["t"][0:64, 0:QB], AF.Copy,
                                                 _deps=(mm, rd), _ms=True)
                        pool.done(bav, oc)
                        pool.done(bav, rd)
                        pool.release(bav)
                        if par == 0:
                            tms = P.vector.tensor_tensor(ofm[0:64, hp, :], o_bf[:, hh, :],
                                                         bbc["t"][0:64, 0:QB], ALU.mult,
                                                         _deps=(oc, bcm) + prevq.get(("ofm_w",), ()),
                                                         _ms=True)
                            wo_dep.append(tms)
                        else:
                            tms = P.vector.tensor_tensor(o2_bf[:, hh, :], o_bf[:, hh, :],
                                                         bbc["t"][0:64, 0:QB], ALU.mult,
                                                         _deps=(oc, bcm),
                                                         _ms=True)
                            dmo = P.gpsimd.dma_start(ofm[64:128, hp, :], o2_bf[:, hh, :],
                                                     _deps=(tms,) + prevq.get(("ofm_w",), ()),
                                                     _sem="dmo")
                            wo_dep.append(dmo)
                        pool.done(bbc, tms)
                        pool.release(bbc)
                    prevq[("qr_read", hp)] = (last_s,)
                    prevq[("kr_prev", hp)] = (kr_ms[hp],)
                prevq[("kr_read", sl)] = (last_s,)
                prevq[("kr_read", slp)] = (last_s,)
                prevq[("v_read", sl)] = (last_av,)
                prevq[("v_read", slp)] = (last_av,)
                if astage < 6:
                    continue
                rms = None
                wo_mm = None
                for f in range(NCH):
                    bwo, dwo = pool.acquire()
                    mm = None
                    for cc in range(NCH):
                        mm = P.tensor.matmul(bwo["t"][:, 0:QB],
                                             wo_sb[:, cc, 128 * f:128 * (f + 1)],
                                             ofm[:, cc, :], start=(cc == 0), stop=(cc == NCH - 1),
                                             _deps=(tuple(wo_dep) + dwo) if cc == 0 else (),
                                             _ms=(cc == NCH - 1))
                    wo_mm = mm
                    rms = P.vector.tensor_tensor(xast[rr][:, f, :], xin[rr][:, f, 1:QB + 1],
                                                 bwo["t"][:, 0:QB], ALU.add,
                                                 _deps=(mm,) + (prevq.get(("xast", rr), ()) if f == 0 else ()),
                                                 _ms=True)
                    pool.done(bwo, rms)
                    pool.release(bwo)
                    xa_ms[qb] = P.gpsimd.dma_start(xa[128 * f:128 * (f + 1), s0:s0 + QB],
                                                   xast[rr][:, f, :], _deps=(rms,),
                                                   _sem=f"xa{rr}")
                prevq[("ofm_w",)] = (wo_mm,)
                prevq[("xast", rr)] = (xa_ms[qb],)
                prevq[("xin", rr)] = (xs_ms, rms)

        fence = P.fence_all()

        with ExitStack() as bctx:
            be = bctx.enter_context
            xin2 = [be(nc.sbuf_tensor(f"xinB{r}", [128, NCH, SUP + 1], F32)) for r in range(2)]
            xs2 = be(nc.sbuf_tensor("xs2", [128, NCH, SUP], BF16))
            sq2 = be(nc.sbuf_tensor("sq2", [128, NCH, 512], BF16))
            xnb = be(nc.sbuf_tensor("xnb", [128, 2, NCH, SUP], BF16))
            hst = be(nc.sbuf_tensor("hst", [128, NDJ, SUP], BF16))
            w1b = [be(nc.sbuf_tensor(f"w1b{r}", [128, 2048], BF16)) for r in range(2)]
            w2b = [be(nc.sbuf_tensor(f"w2b{r}", [128, NDJ * 128], BF16)) for r in range(2)]
            gat = be(nc.sbuf_tensor("gat", [128, 512], BF16))
            tfa = be(nc.sbuf_tensor("tfa", [128, 512], F32))
            tfb = be(nc.sbuf_tensor("tfb", [128, 512], F32))
            tfc = be(nc.sbuf_tensor("tfc", [128, 2, 512], F32))
            rows = be(nc.sbuf_tensor("rowsB", [1, 4 * 512], F32))

            prevs = {}
            nts = nsup if passb else 0

            def stage(ts):
                r = ts % 2
                s0 = ts * SUP
                if ts == 0:
                    zm = P.vector.memset(xin2[r][:, :, 0:1], 0.0, _deps=fence, _ms=True)
                    for f in range(NCH):
                        P.sync.dma_start(
                            xin2[r][:, f, 1:SUP + 1], xa[128 * f:128 * (f + 1), 0:SUP],
                            _deps=((zm,) + tuple(xa_ms[0:4])) if f == 0 else (),
                            _sem=f"xiB{r}")
                else:
                    for f in range(NCH):
                        P.sync.dma_start(
                            xin2[r][:, f, :], xa[128 * f:128 * (f + 1), s0 - 1:s0 + SUP],
                            _deps=tuple(xa_ms[4 * ts - 1:4 * ts + 4]) if f == 0 else (),
                            _sem=f"xiB{r}")
                P.scalar.activation(xs2[:, 0:4, :], xin2[r][:, 0:4, 1:SUP + 1],
                                    AF.Copy, _deps=fence if ts == 0 else ())
                xc2 = P.scalar.activation(xs2[:, 4:8, :], xin2[r][:, 4:8, 0:SUP],
                                          AF.Copy, _ms=True)
                for nt in range(2):
                    t0 = 512 * nt
                    sqm = P.scalar.activation(sq2[:, :, :], xs2[:, :, t0:t0 + 512],
                                              AF.Square,
                                              _deps=fence if ts == 0 and nt == 0 else (),
                                              _ms=True)
                    bs_, d0 = pool.acquire()
                    bq_, d1 = pool.acquire()
                    summ = sqsm = None
                    for f in range(NCH):
                        summ = P.tensor.matmul(bs_["t"][0:1, 0:512], ones_bf[:],
                                               xs2[:, f, t0:t0 + 512],
                                               start=(f == 0), stop=(f == NCH - 1),
                                               _deps=((xc2, ones_ms) + d0) if f == 0 else (),
                                               _ms=(f == NCH - 1))
                    for f in range(NCH):
                        sqsm = P.tensor.matmul(bq_["t"][0:1, 0:512], ones_bf[:],
                                               sq2[:, f, :], start=(f == 0),
                                               stop=(f == NCH - 1),
                                               _deps=((sqm,) + d1) if f == 0 else (),
                                               _ms=(f == NCH - 1))
                    rstd_ms, mr_ms = ln_rows(P, bs_["t"][0:1, 0:512], bq_["t"][0:1, 0:512],
                                             summ, sqsm, rows, DM, 512, epsc[:],
                                             war=prevs.get(("rows",), fence))
                    pool.done(bs_, rstd_ms); pool.release(bs_)
                    pool.done(bq_, rstd_ms); pool.release(bq_)
                    bcr, d2 = pool.acquire()
                    bcm_, d3 = pool.acquire()
                    bcr_ms = P.tensor.matmul(bcr["t"][:, 0:512], ones_f[:],
                                             rows[0:1, 512:1024],
                                             start=True, stop=True,
                                             _deps=(rstd_ms,) + d2, _ms=True)
                    bcm_ms = P.tensor.matmul(bcm_["t"][:, 0:512], ones_f[:],
                                             rows[0:1, 3 * 512:4 * 512],
                                             start=True, stop=True,
                                             _deps=(mr_ms,) + d3, _ms=True)
                    prevs[("rows",)] = (bcr_ms, bcm_ms)
                    lastxn = None
                    for f in range(NCH):
                        P.vector.tensor_tensor(tfa[:], xs2[:, f, t0:t0 + 512],
                                               bcr["t"][:, 0:512], ALU.mult,
                                               _deps=(bcr_ms,))
                        P.vector.tensor_tensor(tfb[:], tfa[:], bcm_["t"][:, 0:512],
                                               ALU.subtract, _deps=(bcm_ms,))
                        lastxn = P.scalar.activation(xnb[:, r, f, t0:t0 + 512], tfb[:],
                                                     AF.Identity,
                                                     bias=lnb_sb[:, f:f + 1],
                                                     scale=lng_sb[:, f:f + 1],
                                                     _deps=(sm,), _ms=True)
                    pool.done(bcr, lastxn); pool.release(bcr)
                    pool.done(bcm_, lastxn); pool.release(bcm_)

            def heavy(ts):
                r = ts % 2
                s0 = ts * SUP
                h_ms = []
                mma = mmg = None
                for dj in range(NDJ):
                    rw = dj % 2
                    wd = P.sync.dma_start(w1b[rw][:], w1s[dj], _sem=f"w1b{rw}",
                                          _deps=() if ts else fence)
                    for nt in range(2):
                        t0 = 512 * nt
                        ba, da = pool.acquire()
                        bg, dg = pool.acquire()
                        for f in range(NCH):
                            mma = P.tensor.matmul(ba["t"][:, 0:512],
                                                  w1b[rw][:, 128 * f:128 * (f + 1)],
                                                  xnb[:, r, f, t0:t0 + 512],
                                                  start=(f == 0), stop=(f == NCH - 1),
                                                  _deps=((wd,) + da) if f == 0 else (),
                                                  _ms=(f == NCH - 1))
                        for f in range(NCH):
                            mmg = P.tensor.matmul(bg["t"][:, 0:512],
                                                  w1b[rw][:, 1024 + 128 * f:1024 + 128 * (f + 1)],
                                                  xnb[:, r, f, t0:t0 + 512],
                                                  start=(f == 0), stop=(f == NCH - 1),
                                                  _deps=((wd,) + dg) if f == 0 else (),
                                                  _ms=(f == NCH - 1))
                        gm = P.scalar.activation(gat[:], bg["t"][:, 0:512], AF.Gelu,
                                                 bias=b1g_sb[:, dj:dj + 1],
                                                 _deps=(mmg, sm), _ms=True)
                        pool.done(bg, gm); pool.release(bg)
                        hm = P.vector.scalar_tensor_tensor(hst[:, dj, t0:t0 + 512],
                                                           ba["t"][:, 0:512],
                                                           b1a_sb[:, dj:dj + 1],
                                                           gat[:], ALU.add, ALU.mult,
                                                           _deps=(mma, gm), _ms=True)
                        pool.done(ba, hm); pool.release(ba)
                        h_ms.append(hm)
                for f in range(NCH):
                    rw = f % 2
                    wd = P.sync.dma_start(w2b[rw][:], w2s[f], _sem=f"w2b{rw}",
                                          _deps=() if ts else fence)
                    for nt in range(2):
                        t0 = 512 * nt
                        bo, do_ = pool.acquire()
                        mm = None
                        for cj in range(NDJ):
                            mm = P.tensor.matmul(bo["t"][:, 0:512],
                                                 w2b[rw][:, 128 * cj:128 * (cj + 1)],
                                                 hst[:, cj, t0:t0 + 512],
                                                 start=(cj == 0), stop=(cj == NDJ - 1),
                                                 _deps=((wd, h_ms[-1]) + do_) if cj == 0 else (),
                                                 _ms=(cj == NDJ - 1))
                        tm = P.scalar.activation(tfc[:, f % 2, :], bo["t"][:, 0:512],
                                                 AF.Identity,
                                                 bias=b2_sb[:, f:f + 1],
                                                 _deps=(mm,), _ms=True)
                        pool.done(bo, tm); pool.release(bo)
                        rm = P.vector.tensor_tensor(xin2[r][:, f, 1 + t0:1 + t0 + 512],
                                                    xin2[r][:, f, 1 + t0:1 + t0 + 512],
                                                    tfc[:, f % 2, :], ALU.add,
                                                    _deps=(tm,), _ms=True)
                        P.gpsimd.dma_start(
                            ynext[128 * f:128 * (f + 1), s0 + t0:s0 + t0 + 512],
                            xin2[r][:, f, 1 + t0:1 + t0 + 512], _deps=(rm,),
                            _sem=f"yn{r}")

            if nts:
                stage(0)
                for ts in range(nts):
                    if ts + 1 < nts:
                        stage(ts + 1)
                    heavy(ts)


def build_ffn_layer(**kw):
    nc = bass.Bass("TRN2", target_bir_lowering=False)
    yin = nc.dram_tensor("yin", [DM, NTOK], F32, kind="ExternalInput").ap()
    ynext = nc.dram_tensor("ynext", [DM, NTOK], F32, kind="ExternalOutput").ap()
    xa = nc.dram_tensor("xa", [DM, NTOK], F32).ap()
    w1s = nc.dram_tensor("w1s", [NDJ, 128, 2048], BF16).ap()
    w2s = nc.dram_tensor("w2s", [NCH, 128, NDJ * 128], BF16).ap()
    names = [("wqkvT", [DM, 3072]), ("woT", [DM, DM]), ("w1a", [DM, DIP]),
             ("w1g", [DM, DIP]), ("w2", [DIP, DM]), ("b1a", [128, NDJ]),
             ("b1g", [128, NDJ]), ("b2", [128, NCH]), ("lng", [128, NCH]),
             ("lnb", [128, NCH]), ("qs", [128, 1]), ("ks", [128, 1]),
             ("qsr", [1, 64]), ("ksr", [1, 64])]
    aps = [nc.dram_tensor(n, s, F32, kind="ExternalInput").ap() for n, s in names]
    rot = nc.dram_tensor("rot", [4, 128, NTOK], BF16, kind="ExternalInput").ap()
    msk = nc.dram_tensor("msk", [4, 128, QB], F32, kind="ExternalInput").ap()
    r2t = nc.dram_tensor("r2t", [128, 128], BF16, kind="ExternalInput").ap()
    bc33 = nc.dram_tensor("bc33", [33, 128], F32, kind="ExternalInput").ap()
    P = Prog(nc)
    emit_ffn_layer(nc, P, yin, ynext, xa, w1s, w2s, *aps, rot, msk, r2t, bc33, **kw)
    P.wait_all_dmas()
    P.build()
    return nc



def emit_l0attn(nc, P, xin, xout, wqT, wuT, bq, bu, idb, ofm_dr, stage=9, s2v="mtea"):
    TW = 512
    NT = NTOK // TW
    with ExitStack() as octx:
        oe = octx.enter_context
        ones_bf = oe(nc.sbuf_tensor("ones_bf", [128, 1], BF16))
        ones_f = oe(nc.sbuf_tensor("ones_f", [1, 128], F32))
        epsc = oe(nc.sbuf_tensor("epsc", [1, 1], F32))
        id_sb = oe(nc.sbuf_tensor("id_sb", [128, 128], BF16))
        bq_sb = oe(nc.sbuf_tensor("bq_sb", [128, NCH], F32))
        bu_sb = oe(nc.sbuf_tensor("bu_sb", [128, NCH], F32))
        qgrp = oe(nc.sbuf_tensor("qgrp", [128, 8, 16, 256], BF16))
        rows = oe(nc.sbuf_tensor("rows", [1, 4 * TW], F32))
        negBg = oe(nc.sbuf_tensor("negBg", [128, 16], F32))
        nBrow = oe(nc.sbuf_tensor("nBrow", [1, 16], F32))
        ps = [oe(nc.psum_tensor(f"ps{k}", [128, 512], F32)) for k in range(6)]
        tps2 = [oe(nc.psum_tensor(f"tps{k}", [128, 512], BF16)) for k in range(2)]

        pool = PsumPool(ps)
        P.vector.memset(ones_bf[:], 1.0)
        P.vector.memset(epsc[:], 1e-5)
        ones_ms = P.vector.memset(ones_f[:], 1.0, _ms=True)
        for dst, srcap in ((bq_sb, bq), (bu_sb, bu), (id_sb, idb)):
            P.sync.dma_start(dst[:], srcap, _sem="f_init")

        with ExitStack() as c0:
            e0 = c0.enter_context
            y_bf = e0(nc.sbuf_tensor("y_bf", [128, NCH, NTOK], BF16))
            c1 = ExitStack()
            e1 = c1.enter_context
            xt = [e1(nc.sbuf_tensor(f"xt{r}", [128, NCH, TW], F32)) for r in range(2)]
            xb = [e1(nc.sbuf_tensor(f"xb{r}", [128, NCH, TW], BF16)) for r in range(1)]
            sq = [e1(nc.sbuf_tensor(f"sq{r}", [128, NCH, TW], BF16)) for r in range(1)]

            for tt in range(NT):
                r = tt % 2
                dm = None
                for f in range(NCH):
                    dm = P.sync.dma_start(xt[r][:, f, :],
                                          xin[128 * f:128 * (f + 1), TW * tt:TW * (tt + 1)],
                                          _sem=f"xi{r}")
                cast_ms = P.vector.tensor_copy(xb[0][:, :, :], xt[r][:, :, :], _ms=True)
                sq_ms = P.scalar.activation(sq[0][:, :, :], xb[0][:, :, :], AF.Square, _ms=True)
                bs, d0 = pool.acquire()
                bqp, d1 = pool.acquire()
                summ = sumsq = None
                for f in range(NCH):
                    summ = P.tensor.matmul(bs["t"][0:1, 0:TW], ones_bf[:], xb[0][:, f, :],
                                           start=(f == 0), stop=(f == NCH - 1),
                                           _deps=d0 if f == 0 else (), _ms=(f == NCH - 1))
                for f in range(NCH):
                    sumsq = P.tensor.matmul(bqp["t"][0:1, 0:TW], ones_bf[:], sq[0][:, f, :],
                                            start=(f == 0), stop=(f == NCH - 1),
                                            _deps=d1 if f == 0 else (), _ms=(f == NCH - 1))
                rstd_ms, mr_ms = ln_rows(P, bs["t"][0:1, 0:TW], bqp["t"][0:1, 0:TW],
                                         summ, sumsq, rows, DM, TW, epsc[:])
                pool.done(bs, rstd_ms); pool.release(bs)
                pool.done(bqp, rstd_ms); pool.release(bqp)
                br, d2 = pool.acquire()
                bm, d3 = pool.acquire()
                bcr = P.tensor.matmul(br["t"][:, 0:TW], ones_f[:], rows[0:1, TW:2 * TW],
                                      start=True, stop=True, _deps=d2, _ms=True)
                bcm = P.tensor.matmul(bm["t"][:, 0:TW], ones_f[:], rows[0:1, 3 * TW:4 * TW],
                                      start=True, stop=True, _deps=d3, _ms=True)
                last = None
                for f in range(NCH):
                    P.vector.tensor_tensor(xt[r][:, f, :], xt[r][:, f, :], br["t"][:, 0:TW],
                                           ALU.mult)
                    last = P.vector.tensor_tensor(y_bf[:, f, TW * tt:TW * (tt + 1)],
                                                  xt[r][:, f, :], bm["t"][:, 0:TW],
                                                  ALU.subtract, _ms=True)
                pool.done(br, last); pool.release(br)
                pool.done(bm, last); pool.release(bm)

            c1.close()
            e2s = c0.enter_context
            wstg = [e2s(nc.sbuf_tensor(f"wstg{r}", [128, 1024], F32)) for r in range(2)]
            wq_sb = e2s(nc.sbuf_tensor("wq_sb", [128, NCH, DM], BF16))
            tmpb = [e2s(nc.sbuf_tensor(f"tmpb{r}", [128, 512], BF16)) for r in range(2)]
            fence1 = P.fence_all()
            for f in range(NCH):
                r = f % 2
                dm = P.sync.dma_start(wstg[r][:], wqT[128 * f:128 * (f + 1), :],
                                      _sem=f"wstg{r}", _deps=fence1 if f < 2 else ())
                P.vector.tensor_copy(wq_sb[:, f, :], wstg[r][:], _ms=True)

            for tt in range(NT):
                parity = tt // 4
                for f in range(NCH):
                    bpq, dq = pool.acquire()
                    mm = None
                    for cc in range(NCH):
                        mm = P.tensor.matmul(bpq["t"][:, 0:TW],
                                             wq_sb[:, cc, 128 * f:128 * (f + 1)],
                                             y_bf[:, cc, TW * tt:TW * (tt + 1)],
                                             start=(cc == 0), stop=(cc == NCH - 1),
                                             _deps=dq if cc == 0 else (), _ms=(cc == NCH - 1))
                    r = (tt * NCH + f) % 2
                    ev = P.scalar.activation(tmpb[r][:], bpq["t"][:, 0:TW], AF.Identity,
                                             bias=bq_sb[:, f:f + 1], _ms=True)
                    pool.done(bpq, ev)
                    pool.release(bpq)
                    for rh in range(2):
                        for sh in range(2):
                            g = 2 * tt + sh
                            src = tmpb[r][64 * rh:64 * (rh + 1), 256 * sh:256 * (sh + 1)]
                            dst = qgrp[64 * parity:64 * (parity + 1), g % 8, 2 * f + rh, :]
                            if rh == parity:
                                P.vector.tensor_copy(dst, src, _ms=True)
                            else:
                                P.gpsimd.dma_start(dst, src, _sem=f"qgrp{r}")

        if stage < 2:
            return
        fence1b = P.fence_all()
        with ExitStack() as c2:
            e2 = c2.enter_context
            vgrp = e2(nc.sbuf_tensor("vgrp", [128, 16, 32, 65], BF16))
            tvb = e2(nc.sbuf_tensor("tvb", [128, 2, 128], BF16))
            qsqg = e2(nc.sbuf_tensor("qsqg", [128, 16, 256], BF16))
            nrmr = e2(nc.sbuf_tensor("nrmr", [1, 256 + 1], F32))
            pstg = e2(nc.sbuf_tensor("pstg", [128, 2, 2, 512], BF16))
            oq = e2(nc.sbuf_tensor("oq", [64, 512], BF16))
            onrm = e2(nc.sbuf_tensor("onrm", [64, 512], BF16))
            rdrow = e2(nc.sbuf_tensor("rdrow", [1, 512], F32))

            if "m" in s2v:
                for g in range(16):
                    for kc in range(32):
                        P.vector.memset(vgrp[:, g, kc, 64:65], 1.0,
                                        _deps=fence1b if (g == 0 and kc == 0) else (),
                                        _ms=(g == 15 and kc == 31))
            for gp in range(8 if "t" in s2v else 0):
                for c in range(16):
                    for h in range(2):
                        kc = 2 * c + h
                        slot = kc % 2
                        tm = P.tensor.transpose(tps2[slot][:, 0:128],
                                                qgrp[:, gp, c, 128 * h:128 * (h + 1)],
                                                id_sb[:], _deps=fence1b, _ms=True)
                        tf = P.vector.tensor_copy(tvb[:, slot, :],
                                                  tps2[slot][:, 0:128],
                                                  _deps=(tm,), _ms=True)
                        P.vector.tensor_copy(vgrp[:, gp, kc, 0:64],
                                             tvb[:, slot, 0:64], _ms=True)
                        P.scalar.activation(vgrp[:, gp + 8, kc, 0:64],
                                            tvb[:, slot, 64:128],
                                            AF.Copy, _deps=(tf,), _ms=True)
            for gp in range(8 if stage >= 3 else 0):
                sqm = P.scalar.activation(qsqg[:, :, :], qgrp[:, gp, :, :], AF.Square,
                                          _ms=True)
                for par in range(2):
                    g = gp + 8 * par
                    for c in range(16):
                        bn, dn = pool.acquire()
                        nm = P.tensor.matmul(bn["t"][0:1, 0:256],
                                             ones_bf[64 * par:64 * (par + 1), :],
                                             qsqg[64 * par:64 * (par + 1), c, :],
                                             start=True, stop=True,
                                             tile_position=(64 * par, 0),
                                             _deps=dn, _ms=True)
                        rm = P.vector.reduce_max(nrmr[0:1, 256:257], bn["t"][0:1, 0:256],
                                                 axis=AX.X, _ms=True)
                        pool.done(bn, rm)
                        pool.release(bn)
                        if c == 0:
                            P.vector.tensor_copy(nBrow[0:1, g:g + 1], nrmr[0:1, 256:257])
                        else:
                            P.vector.tensor_tensor(nBrow[0:1, g:g + 1], nBrow[0:1, g:g + 1],
                                                   nrmr[0:1, 256:257], ALU.max)
            nb_ms = P.vector.tensor_scalar(nBrow[0:1, :], nBrow[0:1, :], -1.0 / 32.0,
                                           None, ALU.mult, _ms=True)
            bnb, dnb = pool.acquire()
            nbm = P.tensor.matmul(bnb["t"][:, 0:16], ones_f[:], nBrow[0:1, :],
                                  start=True, stop=True, _deps=dnb + (ones_ms,), _ms=True)
            ngb_ms = P.vector.tensor_copy(negBg[:], bnb["t"][:, 0:16], _ms=True)
            pool.done(bnb, ngb_ms)
            pool.release(bnb)

            for gp in range(8 if stage >= 4 else 0):
                for qt in range(8):
                    bo = {}
                    od = {}
                    for par in range(2):
                        bo[par], od[par] = pool.acquire()
                    avm = {0: None, 1: None}
                    pend = None
                    qv = qgrp.rearrange("p g c s -> p g (c s)")

                    def _av(kc, rr):
                        for par in range(2):
                            g = gp + 8 * par
                            avm[par] = P.tensor.matmul(
                                bo[par]["t"][0:65, 0:512],
                                vgrp[:, g, kc, 0:65],
                                pstg[:, par, rr, :],
                                start=(kc == 0), stop=(kc == 31),
                                _deps=(od[par] if kc == 0 else ()),
                                _ms=(kc == 31))

                    for kc in range(32):
                        rr = kc % 2
                        bse, dse = pool.acquire()
                        bso, dso = pool.acquire()
                        me = P.tensor.matmul(bse["t"][:, 0:512],
                                             qgrp[0:64, gp, kc // 2, 128 * (kc % 2):128 * (kc % 2) + 128],
                                             qv[0:64, gp, 512 * qt:512 * (qt + 1)],
                                             start=True, stop=True, tile_position=(0, 0),
                                             _deps=dse, _ms=True)
                        mo = P.tensor.matmul(bso["t"][:, 0:512],
                                             qgrp[64:128, gp, kc // 2, 128 * (kc % 2):128 * (kc % 2) + 128],
                                             qv[64:128, gp, 512 * qt:512 * (qt + 1)],
                                             start=True, stop=True, tile_position=(64, 0),
                                             _deps=dso, _ms=True)
                        if pend is not None:
                            _av(*pend)
                        for par, bsx, mx in ((0, bse, me), (1, bso, mo)):
                            g = gp + 8 * par
                            em = P.scalar.activation(pstg[:, par, rr, :], bsx["t"][:, 0:512],
                                                     AF.Exp, bias=negBg[:, g:g + 1],
                                                     scale=1.0 / 32.0,
                                                     _deps=(mx, ngb_ms), _ms=True)
                            pool.done(bsx, em)
                            pool.release(bsx)
                        pend = (kc, rr)
                    _av(*pend)
                    for par in range(2):
                        g = gp + 8 * par
                        P.scalar.activation(rdrow[:], bo[par]["t"][64:65, 0:512], AF.Ln,
                                            _deps=(avm[par],))
                        rd = P.scalar.activation(rdrow[:], rdrow[:], AF.Exp, scale=-1.0,
                                                 _ms=True)
                        bbc, dbc = pool.acquire()
                        bcm = P.tensor.matmul(bbc["t"][0:64, 0:512], ones_f[:, 0:64],
                                              rdrow[:], start=True, stop=True,
                                              tile_position=(0, 0), _deps=(rd,) + dbc,
                                              _ms=True)
                        oc = P.scalar.activation(oq[:], bo[par]["t"][0:64, 0:512], AF.Copy,
                                                 _deps=(avm[par], rd), _ms=True)
                        pool.done(bo[par], oc)
                        pool.release(bo[par])
                        nm = P.vector.tensor_tensor(onrm[:], oq[:], bbc["t"][0:64, 0:512],
                                                    ALU.mult, _ms=True)
                        pool.done(bbc, nm)
                        pool.release(bbc)
                        s0 = 256 * g
                        P.gpsimd.dma_start(ofm_dr[128 * qt:128 * qt + 64, s0:s0 + 256],
                                           onrm[:, 0:256], _sem="ofm")
                        P.gpsimd.dma_start(ofm_dr[128 * qt + 64:128 * qt + 128, s0:s0 + 256],
                                           onrm[:, 256:512], _sem="ofm")

        if stage < 5:
            return
        fence2 = P.fence_all()
        with ExitStack() as c3:
            e3 = c3.enter_context
            wu_sb = e3(nc.sbuf_tensor("wu_sb", [128, NCH, DM], BF16))
            wstg2 = [e3(nc.sbuf_tensor(f"wstg2{r}", [128, 1024], F32)) for r in range(2)]
            x0t = [e3(nc.sbuf_tensor(f"x0t{r}", [128, NCH, TW], F32)) for r in range(2)]
            wut = [e3(nc.sbuf_tensor(f"wut{r}", [128, NCH, TW], F32)) for r in range(1)]
            ofmt = [e3(nc.sbuf_tensor(f"ofmt{r}", [128, NCH, TW], BF16)) for r in range(2)]
            wub = [e3(nc.sbuf_tensor(f"wub{r}", [128, NCH, TW], BF16)) for r in range(1)]
            sq3 = [e3(nc.sbuf_tensor(f"sq3{r}", [128, NCH, TW], BF16)) for r in range(1)]

            for f in range(NCH):
                r = f % 2
                P.sync.dma_start(wstg2[r][:], wuT[128 * f:128 * (f + 1), :],
                                 _sem=f"ws2{r}", _deps=fence2 if f < 2 else ())
                P.vector.tensor_copy(wu_sb[:, f, :], wstg2[r][:], _ms=True)

            out_dmas = []
            for tt in range(NT):
                r = tt % 2
                for f in range(NCH):
                    P.sync.dma_start(x0t[r][:, f, :],
                                     xin[128 * f:128 * (f + 1), TW * tt:TW * (tt + 1)],
                                     _sem=f"x0{r}", _deps=fence2 if (tt < 2 and f == 0) else ())
                for f in range(NCH):
                    P.sync.dma_start(ofmt[r][:, f, :],
                                     ofm_dr[128 * f:128 * (f + 1), TW * tt:TW * (tt + 1)],
                                     _sem=f"ofi{r}")
                for f in range(NCH):
                    bw, dw = pool.acquire()
                    mm = None
                    for cc in range(NCH):
                        mm = P.tensor.matmul(bw["t"][:, 0:TW],
                                             wu_sb[:, cc, 128 * f:128 * (f + 1)],
                                             ofmt[r][:, cc, :],
                                             start=(cc == 0), stop=(cc == NCH - 1),
                                             _deps=dw if cc == 0 else (), _ms=(cc == NCH - 1))
                    ev = P.scalar.activation(wut[0][:, f, :], bw["t"][:, 0:TW], AF.Identity,
                                             bias=bu_sb[:, f:f + 1], _ms=True)
                    pool.done(bw, ev)
                    pool.release(bw)
                cast_ms = P.vector.tensor_copy(wub[0][:, :, :], wut[0][:, :, :], _ms=True)
                sq_ms = P.scalar.activation(sq3[0][:, :, :], wub[0][:, :, :], AF.Square,
                                            _ms=True)
                bs, d0 = pool.acquire()
                bqp, d1 = pool.acquire()
                summ = sumsq = None
                for f in range(NCH):
                    summ = P.tensor.matmul(bs["t"][0:1, 0:TW], ones_bf[:], wub[0][:, f, :],
                                           start=(f == 0), stop=(f == NCH - 1),
                                           _deps=d0 if f == 0 else (), _ms=(f == NCH - 1))
                for f in range(NCH):
                    sumsq = P.tensor.matmul(bqp["t"][0:1, 0:TW], ones_bf[:], sq3[0][:, f, :],
                                            start=(f == 0), stop=(f == NCH - 1),
                                            _deps=d1 if f == 0 else (), _ms=(f == NCH - 1))
                rstd_ms, mr_ms = ln_rows(P, bs["t"][0:1, 0:TW], bqp["t"][0:1, 0:TW],
                                         summ, sumsq, rows, DM, TW, epsc[:])
                pool.done(bs, rstd_ms); pool.release(bs)
                pool.done(bqp, rstd_ms); pool.release(bqp)
                br, d2 = pool.acquire()
                bm, d3 = pool.acquire()
                bcr = P.tensor.matmul(br["t"][:, 0:TW], ones_f[:], rows[0:1, TW:2 * TW],
                                      start=True, stop=True, _deps=d2, _ms=True)
                bcm = P.tensor.matmul(bm["t"][:, 0:TW], ones_f[:], rows[0:1, 3 * TW:4 * TW],
                                      start=True, stop=True, _deps=d3, _ms=True)
                last = None
                for f in range(NCH):
                    P.vector.tensor_tensor(wut[0][:, f, :], wut[0][:, f, :], br["t"][:, 0:TW],
                                           ALU.mult)
                    P.vector.tensor_tensor(wut[0][:, f, :], wut[0][:, f, :], bm["t"][:, 0:TW],
                                           ALU.subtract)
                    last = P.vector.tensor_tensor(x0t[r][:, f, :], x0t[r][:, f, :],
                                                  wut[0][:, f, :], ALU.add, _ms=True)
                pool.done(br, last); pool.release(br)
                pool.done(bm, last); pool.release(bm)
                od = None
                for f in range(NCH):
                    od = P.gpsimd.dma_start(xout[128 * f:128 * (f + 1), TW * tt:TW * (tt + 1)],
                                            x0t[r][:, f, :], _sem=f"xo{r}")
                out_dmas.append(od)


def build_l0attn(**kw):
    nc = bass.Bass("TRN2", target_bir_lowering=False)
    xin = nc.dram_tensor("xin", [DM, NTOK], F32, kind="ExternalInput").ap()
    xout = nc.dram_tensor("xout", [DM, NTOK], F32, kind="ExternalOutput").ap()
    wqT = nc.dram_tensor("wqT", [DM, DM], F32, kind="ExternalInput").ap()
    wuT = nc.dram_tensor("wuT", [DM, DM], F32, kind="ExternalInput").ap()
    bq = nc.dram_tensor("bq", [128, NCH], F32, kind="ExternalInput").ap()
    bu = nc.dram_tensor("bu", [128, NCH], F32, kind="ExternalInput").ap()
    idb = nc.dram_tensor("idb", [128, 128], BF16, kind="ExternalInput").ap()
    ofm_dr = nc.dram_tensor("ofm_dr", [DM, NTOK], BF16).ap()
    P = Prog(nc)
    emit_l0attn(nc, P, xin, xout, wqT, wuT, bq, bu, idb, ofm_dr, **kw)
    P.wait_all_dmas()
    P.build()
    return nc



def emit_l1attn(nc, P, xin, xout, wqT, wkvdT, wkuT, wvuT, woT):
    TW = 512
    NT = NTOK // TW
    NH = 8
    HQ = 1024
    with ExitStack() as octx:
        oe = octx.enter_context
        ones_bf = oe(nc.sbuf_tensor("ones_bf", [128, 1], BF16))
        ones_f = oe(nc.sbuf_tensor("ones_f", [1, 128], F32))
        epsc = oe(nc.sbuf_tensor("epsc", [1, 1], F32))
        rows = oe(nc.sbuf_tensor("rows", [1, 4 * TW], F32))
        y_bf = oe(nc.sbuf_tensor("y_bf", [128, NCH, NTOK], BF16))
        outac = oe(nc.sbuf_tensor("outac", [128, NCH, HQ], F32))
        kvl = oe(nc.sbuf_tensor("kvl", [64, NTOK], BF16))
        wku_sb = oe(nc.sbuf_tensor("wku_sb", [64, 2048], BF16))
        wvu_sb = oe(nc.sbuf_tensor("wvu_sb", [64, 2048], BF16))
        nrow = oe(nc.sbuf_tensor("nrow", [1, 8], F32))
        negB = oe(nc.sbuf_tensor("negB", [128, 1], F32))
        ps = [oe(nc.psum_tensor(f"ps{k}", [128, 512], F32)) for k in range(8)]

        pool = PsumPool(ps)
        P.vector.memset(ones_bf[:], 1.0)
        P.vector.memset(epsc[:], 1e-5)
        ones_ms = P.vector.memset(ones_f[:], 1.0, _ms=True)

        with ExitStack() as c1:
            e1 = c1.enter_context
            xt = [e1(nc.sbuf_tensor(f"xt{r}", [128, NCH, TW], F32)) for r in range(2)]
            xb = e1(nc.sbuf_tensor("xb", [128, NCH, TW], BF16))
            sq = e1(nc.sbuf_tensor("sq", [128, NCH, TW], BF16))
            wstg = [e1(nc.sbuf_tensor(f"wstg{r}", [128, 1024], F32)) for r in range(1)]
            for tt in range(NT):
                r = tt % 2
                for f in range(NCH):
                    P.sync.dma_start(xt[r][:, f, :],
                                     xin[128 * f:128 * (f + 1), TW * tt:TW * (tt + 1)],
                                     _sem=f"xi{r}")
                P.vector.tensor_copy(xb[:, :, :], xt[r][:, :, :], _ms=True)
                P.scalar.activation(sq[:, :, :], xb[:, :, :], AF.Square, _ms=True)
                bs, d0 = pool.acquire()
                bqp, d1 = pool.acquire()
                summ = sumsq = None
                for f in range(NCH):
                    summ = P.tensor.matmul(bs["t"][0:1, 0:TW], ones_bf[:], xb[:, f, :],
                                           start=(f == 0), stop=(f == NCH - 1),
                                           _deps=d0 if f == 0 else (), _ms=(f == NCH - 1))
                for f in range(NCH):
                    sumsq = P.tensor.matmul(bqp["t"][0:1, 0:TW], ones_bf[:], sq[:, f, :],
                                            start=(f == 0), stop=(f == NCH - 1),
                                            _deps=d1 if f == 0 else (), _ms=(f == NCH - 1))
                rstd_ms, mr_ms = ln_rows(P, bs["t"][0:1, 0:TW], bqp["t"][0:1, 0:TW],
                                         summ, sumsq, rows, DM, TW, epsc[:])
                pool.done(bs, rstd_ms); pool.release(bs)
                pool.done(bqp, rstd_ms); pool.release(bqp)
                br, d2 = pool.acquire()
                bm, d3 = pool.acquire()
                P.tensor.matmul(br["t"][:, 0:TW], ones_f[:], rows[0:1, TW:2 * TW],
                                start=True, stop=True, _deps=d2, _ms=True)
                P.tensor.matmul(bm["t"][:, 0:TW], ones_f[:], rows[0:1, 3 * TW:4 * TW],
                                start=True, stop=True, _deps=d3, _ms=True)
                last = None
                for f in range(NCH):
                    P.vector.tensor_tensor(xt[r][:, f, :], xt[r][:, f, :], br["t"][:, 0:TW],
                                           ALU.mult)
                    last = P.vector.tensor_tensor(y_bf[:, f, TW * tt:TW * (tt + 1)],
                                                  xt[r][:, f, :], bm["t"][:, 0:TW],
                                                  ALU.subtract, _ms=True)
                pool.done(br, last); pool.release(br)
                pool.done(bm, last); pool.release(bm)

            wkv_sb = e1(nc.sbuf_tensor("wkv_sb", [128, NCH, 64], BF16))
            for f in range(NCH):
                dm = P.sync.dma_start(wstg[0][:, 0:64], wkvdT[128 * f:128 * (f + 1), :],
                                      _sem="wst0")
                P.vector.tensor_copy(wkv_sb[:, f, :], wstg[0][:, 0:64], _ms=True)
            for tt in range(NT):
                bkv, dkv = pool.acquire()
                mm = None
                for f in range(NCH):
                    mm = P.tensor.matmul(bkv["t"][0:64, 0:TW], wkv_sb[:, f, :],
                                         y_bf[:, f, TW * tt:TW * (tt + 1)],
                                         start=(f == 0), stop=(f == NCH - 1),
                                         _deps=dkv if f == 0 else (), _ms=(f == NCH - 1))
                ev = P.vector.tensor_copy(kvl[:, TW * tt:TW * (tt + 1)], bkv["t"][0:64, 0:TW],
                                          _ms=True)
                pool.done(bkv, ev)
                pool.release(bkv)
            for dst, srcap in ((wku_sb, wkuT), (wvu_sb, wvuT)):
                for hlf in range(2):
                    dm = P.sync.dma_start(wstg[0][0:64, :], srcap[:, 1024 * hlf:1024 * (hlf + 1)],
                                          _sem="wst0")
                    P.vector.tensor_copy(dst[:, 1024 * hlf:1024 * (hlf + 1)],
                                         wstg[0][0:64, :], _ms=True)

        fence1 = P.fence_all()
        with ExitStack() as c2:
            e2 = c2.enter_context
            qh = e2(nc.sbuf_tensor("qh", [128, 2, HQ], BF16))
            kh = e2(nc.sbuf_tensor("kh", [128, 2, NTOK], BF16))
            vh = e2(nc.sbuf_tensor("vh", [128, 32, 256], BF16))
            pstg = e2(nc.sbuf_tensor("pstg", [128, 2, 512], BF16))
            sqs = e2(nc.sbuf_tensor("sqs", [128, 2, 512], BF16))
            wqh = [e2(nc.sbuf_tensor(f"wqh{r}", [128, NCH, 256], F32)) for r in range(1)]
            wqhb = [e2(nc.sbuf_tensor(f"wqhb{r}", [128, NCH, 256], BF16)) for r in range(1)]
            woh = [e2(nc.sbuf_tensor(f"woh{r}", [128, 2, DM], F32)) for r in range(1)]
            wohb = [e2(nc.sbuf_tensor(f"wohb{r}", [128, 2, DM], BF16)) for r in range(1)]
            o_bf = e2(nc.sbuf_tensor("o_bf", [128, 2, 512], BF16))
            rdrow = e2(nc.sbuf_tensor("rdrow", [1, 512], F32))
            xo = [e2(nc.sbuf_tensor(f"xo{r}", [128, NCH, 256], F32)) for r in range(1)]
            ob = e2(nc.sbuf_tensor("ob", [128, NCH, 256], BF16))
            sqo = e2(nc.sbuf_tensor("sqo", [128, NCH, 256], BF16))

            for half in range(4):
                q0 = half * HQ
                for h in range(NH):
                    rr = 0
                    for f in range(NCH):
                        P.sync.dma_start(wqh[rr][:, f, :],
                                         wqT[128 * f:128 * (f + 1), 256 * h:256 * (h + 1)],
                                         _sem=f"wq{rr}", _deps=fence1 if (half == 0 and h < 2 and f == 0) else ())
                    P.vector.tensor_copy(wqhb[rr][:, :, :], wqh[rr][:, :, :], _ms=True)
                    P.sync.dma_start(woh[rr][:, 0, :], woT[256 * h:256 * h + 128, :],
                                     _sem=f"wo{rr}")
                    P.sync.dma_start(woh[rr][:, 1, :], woT[256 * h + 128:256 * (h + 1), :],
                                     _sem=f"wo{rr}")
                    P.vector.tensor_copy(wohb[rr][:, :, :], woh[rr][:, :, :], _ms=True)
                    for dc in range(2):
                        for nt4 in range(HQ // TW):
                            bq_, dq = pool.acquire()
                            mm = None
                            for f in range(NCH):
                                mm = P.tensor.matmul(
                                    bq_["t"][:, 0:TW],
                                    wqhb[rr][:, f, 128 * dc:128 * (dc + 1)],
                                    y_bf[:, f, q0 + TW * nt4:q0 + TW * (nt4 + 1)],
                                    start=(f == 0), stop=(f == NCH - 1),
                                    _deps=dq if f == 0 else (), _ms=(f == NCH - 1))
                            ev = P.vector.tensor_copy(qh[:, dc, TW * nt4:TW * (nt4 + 1)],
                                                      bq_["t"][:, 0:TW], _ms=True)
                            pool.done(bq_, ev)
                            pool.release(bq_)
                    for dc in range(2):
                        for nt in range(NT):
                            bk_, dk = pool.acquire()
                            mm = P.tensor.matmul(
                                bk_["t"][:, 0:TW],
                                wku_sb[0:64, 256 * h + 128 * dc:256 * h + 128 * (dc + 1)],
                                kvl[:, TW * nt:TW * (nt + 1)],
                                start=True, stop=True, _deps=dk, _ms=True)
                            ev = P.vector.tensor_copy(kh[:, dc, TW * nt:TW * (nt + 1)],
                                                      bk_["t"][:, 0:TW], _ms=True)
                            pool.done(bk_, ev)
                            pool.release(bk_)
                    for kc in range(32):
                        bv_, dv = pool.acquire()
                        mm = P.tensor.matmul(bv_["t"][:, 0:256],
                                             kvl[:, 128 * kc:128 * (kc + 1)],
                                             wvu_sb[0:64, 256 * h:256 * (h + 1)],
                                             start=True, stop=True, _deps=dv, _ms=True)
                        ev = P.scalar.activation(vh[:, kc, :], bv_["t"][:, 0:256], AF.Copy,
                                                 _ms=True)
                        pool.done(bv_, ev)
                        pool.release(bv_)
                    bn, dn = pool.acquire()
                    rm = None
                    for nt4 in range(HQ // TW):
                        P.scalar.activation(sqs[:, 0:2, :], qh[:, :, TW * nt4:TW * (nt4 + 1)],
                                            AF.Square, _ms=True)
                        mm = None
                        for dc in range(2):
                            mm = P.tensor.matmul(bn["t"][0:1, 0:TW], ones_bf[:],
                                                 sqs[:, dc, :],
                                                 start=(dc == 0), stop=(dc == 1),
                                                 _deps=dn if (nt4 == 0 and dc == 0) else (),
                                                 _ms=(dc == 1))
                        rm = P.vector.reduce_max(nrow[0:1, 2:3], bn["t"][0:1, 0:TW],
                                                 axis=AX.X, _ms=True)
                        if nt4 == 0:
                            P.vector.tensor_copy(nrow[0:1, 0:1], nrow[0:1, 2:3])
                        else:
                            P.vector.tensor_tensor(nrow[0:1, 0:1], nrow[0:1, 0:1],
                                                   nrow[0:1, 2:3], ALU.max)
                    pool.done(bn, rm)
                    pool.release(bn)
                    P.scalar.activation(nrow[0:1, 0:1], nrow[0:1, 0:1], AF.Sqrt, _ms=True)
                    bn2, dn2 = pool.acquire()
                    for nt in range(NT):
                        P.scalar.activation(sqs[:, 0:2, :], kh[:, :, TW * nt:TW * (nt + 1)],
                                            AF.Square, _ms=True)
                        mm = None
                        for dc in range(2):
                            mm = P.tensor.matmul(bn2["t"][0:1, 0:TW], ones_bf[:],
                                                 sqs[:, dc, :],
                                                 start=(dc == 0), stop=(dc == 1),
                                                 _deps=dn2 if (nt == 0 and dc == 0) else (),
                                                 _ms=(dc == 1))
                        rm = P.vector.reduce_max(nrow[0:1, 3:4], bn2["t"][0:1, 0:TW],
                                                 axis=AX.X, _ms=True)
                        if nt == 0:
                            P.vector.tensor_copy(nrow[0:1, 1:2], nrow[0:1, 3:4])
                        else:
                            P.vector.tensor_tensor(nrow[0:1, 1:2], nrow[0:1, 1:2],
                                                   nrow[0:1, 3:4], ALU.max)
                    pool.done(bn2, rm)
                    pool.release(bn2)
                    P.scalar.activation(nrow[0:1, 1:2], nrow[0:1, 1:2], AF.Sqrt, _ms=True)
                    P.vector.tensor_tensor(nrow[0:1, 4:5], nrow[0:1, 0:1], nrow[0:1, 1:2],
                                           ALU.mult)
                    nb = P.vector.tensor_scalar(nrow[0:1, 4:5], nrow[0:1, 4:5], -1.0 / 16.0,
                                                None, ALU.mult, _ms=True)
                    bnb, dnb = pool.acquire()
                    nbm = P.tensor.matmul(bnb["t"][:, 0:1], ones_f[:], nrow[0:1, 4:5],
                                          start=True, stop=True, _deps=(nb,) + dnb, _ms=True)
                    ngb = P.vector.tensor_copy(negB[:], bnb["t"][:, 0:1], _ms=True)
                    pool.done(bnb, ngb)
                    pool.release(bnb)
                    for qt in range(HQ // TW):
                        bo1, do1 = pool.acquire()
                        bo2, do2 = pool.acquire()
                        bd, dd = pool.acquire()

                        def _av1(kc, rr2):
                            P.tensor.matmul(bd["t"][0:1, 0:TW], ones_bf[:], pstg[:, rr2, :],
                                            start=(kc == 0), stop=(kc == 31),
                                            _deps=(dd if kc == 0 else ()),
                                            _ms=(kc == 31))
                            for dh in range(2):
                                P.tensor.matmul(bo1["t"][:, 0:TW] if dh == 0 else bo2["t"][:, 0:TW],
                                                vh[:, kc, 128 * dh:128 * (dh + 1)],
                                                pstg[:, rr2, :],
                                                start=(kc == 0), stop=(kc == 31),
                                                _deps=((do1 + do2) if kc == 0 else ()),
                                                _ms=(kc == 31))

                        pend = None
                        for kc in range(32):
                            rr2 = kc % 2
                            bS, dS = pool.acquire()
                            mm = None
                            for dc in range(2):
                                mm = P.tensor.matmul(
                                    bS["t"][:, 0:TW],
                                    kh[:, dc, 128 * kc:128 * (kc + 1)],
                                    qh[:, dc, TW * qt:TW * (qt + 1)],
                                    start=(dc == 0), stop=(dc == 1),
                                    _deps=dS if dc == 0 else (), _ms=(dc == 1))
                            if pend is not None:
                                _av1(*pend)
                            em = P.scalar.activation(pstg[:, rr2, :], bS["t"][:, 0:TW],
                                                     AF.Exp, bias=negB[:], scale=1.0 / 16.0,
                                                     _deps=(mm, ngb), _ms=True)
                            pool.done(bS, em)
                            pool.release(bS)
                            pend = (kc, rr2)
                        _av1(*pend)
                        rd = P.vector.reciprocal(rdrow[:], bd["t"][0:1, 0:TW], _ms=True)
                        pool.done(bd, rd)
                        pool.release(bd)
                        bbc, dbc = pool.acquire()
                        bcm = P.tensor.matmul(bbc["t"][:, 0:TW], ones_f[:], rdrow[:],
                                              start=True, stop=True, _deps=(rd,) + dbc,
                                              _ms=True)
                        for dh, bo_ in ((0, bo1), (1, bo2)):
                            oc = P.scalar.activation(o_bf[:, dh, :], bo_["t"][:, 0:TW],
                                                     AF.Copy, _ms=True)
                            nm = P.vector.tensor_tensor(o_bf[:, dh, :], o_bf[:, dh, :],
                                                        bbc["t"][:, 0:TW], ALU.mult,
                                                        _deps=(oc, bcm), _ms=True)
                            pool.done(bo_, oc)
                            pool.release(bo_)
                        pool.done(bbc, nm)
                        pool.release(bbc)
                        for f in range(NCH):
                            bw, dw = pool.acquire()
                            mm = None
                            for dc in range(2):
                                mm = P.tensor.matmul(bw["t"][:, 0:TW],
                                                     wohb[rr][:, dc, 128 * f:128 * (f + 1)],
                                                     o_bf[:, dc, :],
                                                     start=(dc == 0), stop=(dc == 1),
                                                     _deps=dw if dc == 0 else (),
                                                     _ms=(dc == 1))
                            if h == 0:
                                ra = P.vector.tensor_copy(outac[:, f, TW * qt:TW * (qt + 1)],
                                                          bw["t"][:, 0:TW], _ms=True)
                            else:
                                ra = P.vector.tensor_tensor(outac[:, f, TW * qt:TW * (qt + 1)],
                                                            outac[:, f, TW * qt:TW * (qt + 1)],
                                                            bw["t"][:, 0:TW], ALU.add, _ms=True)
                            pool.done(bw, ra)
                            pool.release(bw)
                SW = 256
                for tt4 in range(HQ // SW):
                    tq = half * HQ + SW * tt4
                    r = 0
                    for f in range(NCH):
                        P.sync.dma_start(xo[r][:, f, :],
                                         xin[128 * f:128 * (f + 1), tq:tq + SW],
                                         _sem=f"xr{r}")
                    P.vector.tensor_copy(ob[:, :, :],
                                         outac[:, :, SW * tt4:SW * (tt4 + 1)], _ms=True)
                    P.scalar.activation(sqo[:, :, :], ob[:, :, :], AF.Square, _ms=True)
                    bs, d0 = pool.acquire()
                    bqp, d1 = pool.acquire()
                    summ = sumsq = None
                    for f in range(NCH):
                        summ = P.tensor.matmul(bs["t"][0:1, 0:SW], ones_bf[:], ob[:, f, :],
                                               start=(f == 0), stop=(f == NCH - 1),
                                               _deps=d0 if f == 0 else (), _ms=(f == NCH - 1))
                    for f in range(NCH):
                        sumsq = P.tensor.matmul(bqp["t"][0:1, 0:SW], ones_bf[:], sqo[:, f, :],
                                                start=(f == 0), stop=(f == NCH - 1),
                                                _deps=d1 if f == 0 else (), _ms=(f == NCH - 1))
                    rstd_ms, mr_ms = ln_rows(P, bs["t"][0:1, 0:SW], bqp["t"][0:1, 0:SW],
                                             summ, sumsq, rows, DM, SW, epsc[:])
                    pool.done(bs, rstd_ms); pool.release(bs)
                    pool.done(bqp, rstd_ms); pool.release(bqp)
                    br, d2 = pool.acquire()
                    bm, d3 = pool.acquire()
                    P.tensor.matmul(br["t"][:, 0:SW], ones_f[:], rows[0:1, SW:2 * SW],
                                    start=True, stop=True, _deps=d2, _ms=True)
                    P.tensor.matmul(bm["t"][:, 0:SW], ones_f[:], rows[0:1, 3 * SW:4 * SW],
                                    start=True, stop=True, _deps=d3, _ms=True)
                    last = None
                    for f in range(NCH):
                        t1 = P.vector.tensor_tensor(outac[:, f, SW * tt4:SW * (tt4 + 1)],
                                                    outac[:, f, SW * tt4:SW * (tt4 + 1)],
                                                    br["t"][:, 0:SW], ALU.mult, _ms=True)
                        P.vector.tensor_tensor(outac[:, f, SW * tt4:SW * (tt4 + 1)],
                                               outac[:, f, SW * tt4:SW * (tt4 + 1)],
                                               bm["t"][:, 0:SW], ALU.subtract)
                        last = P.vector.tensor_tensor(xo[r][:, f, :], xo[r][:, f, :],
                                                      outac[:, f, SW * tt4:SW * (tt4 + 1)],
                                                      ALU.add, _ms=True)
                    pool.done(br, last); pool.release(br)
                    pool.done(bm, last); pool.release(bm)
                    for f in range(NCH):
                        P.gpsimd.dma_start(xout[128 * f:128 * (f + 1), tq:tq + SW],
                                           xo[r][:, f, :], _sem=f"xs{r}")


def build_l1attn():
    nc = bass.Bass("TRN2", target_bir_lowering=False)
    xin = nc.dram_tensor("xin", [DM, NTOK], F32, kind="ExternalInput").ap()
    xout = nc.dram_tensor("xout", [DM, NTOK], F32, kind="ExternalOutput").ap()
    wqT = nc.dram_tensor("wqT", [DM, 2048], F32, kind="ExternalInput").ap()
    wkvdT = nc.dram_tensor("wkvdT", [DM, 64], F32, kind="ExternalInput").ap()
    wkuT = nc.dram_tensor("wkuT", [64, 2048], F32, kind="ExternalInput").ap()
    wvuT = nc.dram_tensor("wvuT", [64, 2048], F32, kind="ExternalInput").ap()
    woT = nc.dram_tensor("woT", [2048, DM], F32, kind="ExternalInput").ap()
    P = Prog(nc)
    emit_l1attn(nc, P, xin, xout, wqT, wkvdT, wkuT, wvuT, woT)
    P.wait_all_dmas()
    P.build()
    return nc



def emit_l1ffn(nc, P, xin, xout, A_R, B_R, A_I, B_I, e32t, e16t, bias):
    TW = 512
    NT = NTOK // TW
    with ExitStack() as octx:
        oe = octx.enter_context
        ones_bf = oe(nc.sbuf_tensor("ones_bf", [128, 1], BF16))
        ones_f = oe(nc.sbuf_tensor("ones_f", [1, 128], F32))
        epsc = oe(nc.sbuf_tensor("epsc", [1, 1], F32))
        rows = oe(nc.sbuf_tensor("rows", [1, 4 * TW], F32))
        bias_sb = oe(nc.sbuf_tensor("bias_sb", [128, NCH], F32))
        e32_sb = oe(nc.sbuf_tensor("e32_sb", [32, 512], BF16))
        e16_sb = oe(nc.sbuf_tensor("e16_sb", [16, 512], BF16))
        Af = oe(nc.sbuf_tensor("Af", [32, 2, 4, 32], F32))
        Bf = oe(nc.sbuf_tensor("Bf", [16, 2, 4, 16], F32))
        Ab = oe(nc.sbuf_tensor("Ab", [32, 2, 4, 32], BF16))
        Bb = oe(nc.sbuf_tensor("Bb", [16, 2, 4, 16], BF16))
        s1b = oe(nc.sbuf_tensor("s1b", [32, 512], BF16))
        s1c = oe(nc.sbuf_tensor("s1c", [16, 512], BF16))
        abig = oe(nc.sbuf_tensor("abig", [128, 512], F32))
        wacc = oe(nc.sbuf_tensor("wacc", [128, 4, 512], F32))
        wt = oe(nc.sbuf_tensor("wt", [128, 3, 4, 512], BF16))
        xt = [oe(nc.sbuf_tensor(f"xt{r}", [128, NCH, TW], F32)) for r in range(2)]
        xb = oe(nc.sbuf_tensor("xb", [128, NCH, TW], BF16))
        sq = oe(nc.sbuf_tensor("sq", [128, NCH, TW], BF16))
        yn = oe(nc.sbuf_tensor("yn", [128, NCH, TW], BF16))
        bt = oe(nc.sbuf_tensor("bt", [128, NCH, TW], F32))
        btb = oe(nc.sbuf_tensor("btb", [128, NCH, TW], BF16))
        sqb = oe(nc.sbuf_tensor("sqb", [128, NCH, TW], BF16))
        ps = [oe(nc.psum_tensor(f"ps{k}", [128, 512], F32)) for k in range(8)]

        pool = PsumPool(ps)
        P.vector.memset(ones_bf[:], 1.0)
        P.vector.memset(epsc[:], 1e-5)
        ones_ms = P.vector.memset(ones_f[:], 1.0, _ms=True)
        P.sync.dma_start(bias_sb[:], bias, _sem="f_init")
        P.sync.dma_start(e32_sb[:], e32t, _sem="f_init")
        P.sync.dma_start(e16_sb[:], e16t, _sem="f_init")
        for ri, (Aap, Bap) in enumerate(((A_R, B_R), (A_I, B_I))):
            for c in range(4):
                P.sync.dma_start(Af[:, ri, c, :], Aap[c], _sem="f_init")
                P.sync.dma_start(Bf[:, ri, c, :], Bap[c], _sem="f_init")
        P.vector.tensor_copy(Ab[:, :, :, :], Af[:, :, :, :], _ms=True)
        P.vector.tensor_copy(Bb[:, :, :, :], Bf[:, :, :, :], _ms=True)

        for ri in range(2):
            for c in range(4):
                b1, d1 = pool.acquire()
                m1 = P.tensor.matmul(b1["t"][0:32, 0:512], Ab[:, ri, c, :], e32_sb[:],
                                     start=True, stop=True, _deps=d1, _ms=True)
                c1m = P.vector.tensor_copy(s1b[:], b1["t"][0:32, 0:512], _ms=True)
                pool.done(b1, c1m); pool.release(b1)
                b2, d2 = pool.acquire()
                m2 = P.tensor.matmul(b2["t"][0:16, 0:512], Bb[:, ri, c, :], e16_sb[:],
                                     start=True, stop=True, _deps=d2, _ms=True)
                c2m = P.vector.tensor_copy(s1c[:], b2["t"][0:16, 0:512], _ms=True)
                pool.done(b2, c2m); pool.release(b2)
                for cj in range(4):
                    b3, d3 = pool.acquire()
                    m3 = P.tensor.matmul(b3["t"][:, 0:512], e32_sb[:, 128 * cj:128 * (cj + 1)],
                                         s1b[:], start=True, stop=True,
                                         _deps=(c1m,) + d3, _ms=True)
                    a1 = P.vector.tensor_copy(abig[:], b3["t"][:, 0:512], _ms=True)
                    pool.done(b3, a1); pool.release(b3)
                    b4, d4 = pool.acquire()
                    m4 = P.tensor.matmul(b4["t"][:, 0:512], e16_sb[:, 128 * cj:128 * (cj + 1)],
                                         s1c[:], start=True, stop=True,
                                         _deps=(c2m,) + d4, _ms=True)
                    pm = P.vector.tensor_tensor(abig[:], abig[:], b4["t"][:, 0:512],
                                                ALU.mult, _deps=(m4, a1), _ms=True)
                    pool.done(b4, pm); pool.release(b4)
                    if c == 0:
                        P.vector.tensor_copy(wacc[:, cj, :], abig[:], _ms=True)
                    else:
                        P.vector.tensor_tensor(wacc[:, cj, :], wacc[:, cj, :], abig[:],
                                               ALU.add, _ms=True)
            for cj in range(4):
                P.vector.tensor_copy(wt[:, ri, cj, :], wacc[:, cj, :], _ms=True)
                if ri == 1:
                    P.vector.tensor_scalar(wt[:, 2, cj, :], wacc[:, cj, :], -1.0, None,
                                           ALU.mult, _ms=True)

        for tt in range(NT):
            r = tt % 2
            for f in range(NCH):
                P.sync.dma_start(xt[r][:, f, :],
                                 xin[128 * f:128 * (f + 1), TW * tt:TW * (tt + 1)],
                                 _sem=f"xi{r}")
            P.vector.tensor_copy(xb[:, :, :], xt[r][:, :, :], _ms=True)
            P.scalar.activation(sq[:, :, :], xb[:, :, :], AF.Square, _ms=True)
            bs, d0 = pool.acquire()
            bqp, d1 = pool.acquire()
            summ = sumsq = None
            for f in range(NCH):
                summ = P.tensor.matmul(bs["t"][0:1, 0:TW], ones_bf[:], xb[:, f, :],
                                       start=(f == 0), stop=(f == NCH - 1),
                                       _deps=d0 if f == 0 else (), _ms=(f == NCH - 1))
            for f in range(NCH):
                sumsq = P.tensor.matmul(bqp["t"][0:1, 0:TW], ones_bf[:], sq[:, f, :],
                                        start=(f == 0), stop=(f == NCH - 1),
                                        _deps=d1 if f == 0 else (), _ms=(f == NCH - 1))
            rstd_ms, mr_ms = ln_rows(P, bs["t"][0:1, 0:TW], bqp["t"][0:1, 0:TW],
                                     summ, sumsq, rows, DM, TW, epsc[:])
            pool.done(bs, rstd_ms); pool.release(bs)
            pool.done(bqp, rstd_ms); pool.release(bqp)
            br, d2 = pool.acquire()
            bm, d3 = pool.acquire()
            P.tensor.matmul(br["t"][:, 0:TW], ones_f[:], rows[0:1, TW:2 * TW],
                            start=True, stop=True, _deps=d2, _ms=True)
            P.tensor.matmul(bm["t"][:, 0:TW], ones_f[:], rows[0:1, 3 * TW:4 * TW],
                            start=True, stop=True, _deps=d3, _ms=True)
            lastyn = None
            for f in range(NCH):
                P.vector.tensor_tensor(sq[:, f, :], xb[:, f, :], br["t"][:, 0:TW], ALU.mult)
                lastyn = P.vector.tensor_tensor(yn[:, f, :], sq[:, f, :], bm["t"][:, 0:TW],
                                                ALU.subtract, _ms=True)
            pool.done(br, lastyn); pool.release(br)
            pool.done(bm, lastyn); pool.release(bm)
            for half in range(2):
                for dj in range(4):
                    bo_, do_ = pool.acquire()
                    mm = None
                    for cj in range(4):
                        lhs1 = wt[:, 0, cj, 128 * dj:128 * (dj + 1)] if half == 0 else                                wt[:, 1, cj, 128 * dj:128 * (dj + 1)]
                        mm = P.tensor.matmul(bo_["t"][:, 0:TW], lhs1, yn[:, cj, :],
                                             start=(cj == 0), stop=False,
                                             _deps=do_ if cj == 0 else ())
                    for cj in range(4):
                        lhs2 = wt[:, 2, cj, 128 * dj:128 * (dj + 1)] if half == 0 else                                wt[:, 0, cj, 128 * dj:128 * (dj + 1)]
                        mm = P.tensor.matmul(bo_["t"][:, 0:TW], lhs2, yn[:, 4 + cj, :],
                                             start=False, stop=(cj == 3),
                                             _ms=(cj == 3))
                    f = 4 * half + dj
                    ev = P.scalar.activation(bt[:, f, :], bo_["t"][:, 0:TW], AF.Identity,
                                             bias=bias_sb[:, f:f + 1], _ms=True)
                    pool.done(bo_, ev)
                    pool.release(bo_)
            cb = P.vector.tensor_copy(btb[:, :, :], bt[:, :, :], _ms=True)
            P.scalar.activation(sqb[:, :, :], btb[:, :, :], AF.Square, _ms=True)
            bs, d0 = pool.acquire()
            bqp, d1 = pool.acquire()
            for f in range(NCH):
                summ = P.tensor.matmul(bs["t"][0:1, 0:TW], ones_bf[:], btb[:, f, :],
                                       start=(f == 0), stop=(f == NCH - 1),
                                       _deps=d0 if f == 0 else (), _ms=(f == NCH - 1))
            for f in range(NCH):
                sumsq = P.tensor.matmul(bqp["t"][0:1, 0:TW], ones_bf[:], sqb[:, f, :],
                                        start=(f == 0), stop=(f == NCH - 1),
                                        _deps=d1 if f == 0 else (), _ms=(f == NCH - 1))
            rstd_ms, mr_ms = ln_rows(P, bs["t"][0:1, 0:TW], bqp["t"][0:1, 0:TW],
                                     summ, sumsq, rows, DM, TW, epsc[:])
            pool.done(bs, rstd_ms); pool.release(bs)
            pool.done(bqp, rstd_ms); pool.release(bqp)
            br, d2 = pool.acquire()
            bm, d3 = pool.acquire()
            P.tensor.matmul(br["t"][:, 0:TW], ones_f[:], rows[0:1, TW:2 * TW],
                            start=True, stop=True, _deps=d2, _ms=True)
            P.tensor.matmul(bm["t"][:, 0:TW], ones_f[:], rows[0:1, 3 * TW:4 * TW],
                            start=True, stop=True, _deps=d3, _ms=True)
            last = None
            for f in range(NCH):
                P.vector.tensor_tensor(bt[:, f, :], bt[:, f, :], br["t"][:, 0:TW], ALU.mult)
                P.vector.tensor_tensor(bt[:, f, :], bt[:, f, :], bm["t"][:, 0:TW],
                                       ALU.subtract)
                last = P.vector.tensor_tensor(xt[r][:, f, :], xt[r][:, f, :], bt[:, f, :],
                                              ALU.add, _ms=True)
            pool.done(br, last); pool.release(br)
            pool.done(bm, last); pool.release(bm)
            for f in range(NCH):
                P.gpsimd.dma_start(xout[128 * f:128 * (f + 1), TW * tt:TW * (tt + 1)],
                                   xt[r][:, f, :], _sem=f"xs{r}")


def build_l1ffn():
    nc = bass.Bass("TRN2", target_bir_lowering=False)
    xin = nc.dram_tensor("xin", [DM, NTOK], F32, kind="ExternalInput").ap()
    xout = nc.dram_tensor("xout", [DM, NTOK], F32, kind="ExternalOutput").ap()
    A_R = nc.dram_tensor("A_R", [4, 32, 32], F32, kind="ExternalInput").ap()
    B_R = nc.dram_tensor("B_R", [4, 16, 16], F32, kind="ExternalInput").ap()
    A_I = nc.dram_tensor("A_I", [4, 32, 32], F32, kind="ExternalInput").ap()
    B_I = nc.dram_tensor("B_I", [4, 16, 16], F32, kind="ExternalInput").ap()
    e32t = nc.dram_tensor("e32t", [32, 512], BF16, kind="ExternalInput").ap()
    e16t = nc.dram_tensor("e16t", [16, 512], BF16, kind="ExternalInput").ap()
    bias = nc.dram_tensor("bias", [128, NCH], F32, kind="ExternalInput").ap()
    P = Prog(nc)
    emit_l1ffn(nc, P, xin, xout, A_R, B_R, A_I, B_I, e32t, e16t, bias)
    P.wait_all_dmas()
    P.build()
    return nc



def build_fused():
    nc = bass.Bass("TRN2", target_bir_lowering=False)
    x0 = nc.dram_tensor("x0", [DM, NTOK], F32, kind="ExternalInput").ap()
    xout = nc.dram_tensor("xout", [DM, NTOK], F32, kind="ExternalOutput").ap()
    x1 = nc.dram_tensor("x1", [DM, NTOK], F32).ap()
    x2 = nc.dram_tensor("x2", [DM, NTOK], F32).ap()
    x3 = nc.dram_tensor("x3", [DM, NTOK], F32).ap()
    yA = nc.dram_tensor("yA", [DM, NTOK], F32).ap()
    yB = nc.dram_tensor("yB", [DM, NTOK], F32).ap()
    xa = nc.dram_tensor("xa_i", [DM, NTOK], F32).ap()
    ofm_dr = nc.dram_tensor("ofm_dr", [DM, NTOK], BF16).ap()
    w1s = nc.dram_tensor("w1s", [NDJ, 128, 2048], BF16).ap()
    w2s = nc.dram_tensor("w2s", [NCH, 128, NDJ * 128], BF16).ap()

    l0a_wqT = nc.dram_tensor("l0a_wqT", [DM, DM], F32, kind="ExternalInput").ap()
    l0a_wuT = nc.dram_tensor("l0a_wuT", [DM, DM], F32, kind="ExternalInput").ap()
    l0a_bq = nc.dram_tensor("l0a_bq", [128, NCH], F32, kind="ExternalInput").ap()
    l0a_bu = nc.dram_tensor("l0a_bu", [128, NCH], F32, kind="ExternalInput").ap()
    idb = nc.dram_tensor("idb", [128, 128], BF16, kind="ExternalInput").ap()
    rot = nc.dram_tensor("rot", [4, 128, NTOK], BF16, kind="ExternalInput").ap()
    msk = nc.dram_tensor("msk", [4, 128, QB], F32, kind="ExternalInput").ap()
    r2t = nc.dram_tensor("r2t", [128, 128], BF16, kind="ExternalInput").ap()
    bc33 = nc.dram_tensor("bc33", [33, 128], F32, kind="ExternalInput").ap()
    ffnw = []
    shapes = [("wqkvT", [DM, 3072]), ("woT", [DM, DM]), ("w1a", [DM, DIP]),
              ("w1g", [DM, DIP]), ("w2", [DIP, DM]), ("b1a", [128, NDJ]),
              ("b1g", [128, NDJ]), ("b2", [128, NCH]), ("lng", [128, NCH]),
              ("lnb", [128, NCH]), ("qs", [128, 1]), ("ks", [128, 1]),
              ("qsr", [1, 64]), ("ksr", [1, 64])]
    for li in range(6):
        ffnw.append([nc.dram_tensor(f"{n}_{li}", s, F32, kind="ExternalInput").ap()
                     for n, s in shapes])
    l1a = [nc.dram_tensor("l1a_wqT", [DM, 2048], F32, kind="ExternalInput").ap(),
           nc.dram_tensor("l1a_wkvdT", [DM, 64], F32, kind="ExternalInput").ap(),
           nc.dram_tensor("l1a_wkuT", [64, 2048], F32, kind="ExternalInput").ap(),
           nc.dram_tensor("l1a_wvuT", [64, 2048], F32, kind="ExternalInput").ap(),
           nc.dram_tensor("l1a_woT", [2048, DM], F32, kind="ExternalInput").ap()]
    l1f = [nc.dram_tensor("A_R", [4, 32, 32], F32, kind="ExternalInput").ap(),
           nc.dram_tensor("B_R", [4, 16, 16], F32, kind="ExternalInput").ap(),
           nc.dram_tensor("A_I", [4, 32, 32], F32, kind="ExternalInput").ap(),
           nc.dram_tensor("B_I", [4, 16, 16], F32, kind="ExternalInput").ap(),
           nc.dram_tensor("e32t", [32, 512], BF16, kind="ExternalInput").ap(),
           nc.dram_tensor("e16t", [16, 512], BF16, kind="ExternalInput").ap(),
           nc.dram_tensor("l1f_bias", [128, NCH], F32, kind="ExternalInput").ap()]

    P = Prog(nc)

    def phase(tag):
        P.phase_tag = tag
        P.phase_fence()

    phase("p0")
    emit_l0attn(_NCP(nc, "a_"), P, x0, x1, l0a_wqT, l0a_wuT, l0a_bq, l0a_bu,
                idb, ofm_dr)
    phase("p1")
    emit_ln_stream(_NCP(nc, "e_"), P, "ln", x1, yA)
    ys = [yA, yB]
    for li in range(6):
        phase(f"p2{li}")
        emit_ffn_layer(_NCP(nc, f"f{li}_"), P, ys[li % 2], ys[(li + 1) % 2],
                       xa, w1s, w2s, *ffnw[li], rot, msk, r2t, bc33)
    phase("p3")
    emit_ln_stream(_NCP(nc, "g_"), P, "res_ln", ys[0], x2, xres=x1)
    phase("p4")
    emit_l1attn(_NCP(nc, "m_"), P, x2, x3, *l1a)
    phase("p5")
    emit_l1ffn(_NCP(nc, "t_"), P, x3, xout, *l1f)

    P.wait_all_dmas()
    P.build()
    return nc


def fused_maps(per_core, P):
    idb = np.ascontiguousarray(np.eye(128, dtype=np.float32).astype(BF))
    maps = []
    for c in range(NCORES):
        m = {"x0": per_core[c], "l0a_wqT": P["l0a_wqT"], "l0a_wuT": P["l0a_wuT"],
             "l0a_bq": P["l0a_bq"], "l0a_bu": P["l0a_bu"], "idb": idb,
             "rot": P["rot"], "msk": P["msk"], "r2t": P["r2t"], "bc33": P["bc33"],
             "l1a_wqT": P["l1a_wqT"], "l1a_wkvdT": P["l1a_wkvdT"],
             "l1a_wkuT": P["l1a_wkuT"], "l1a_wvuT": P["l1a_wvuT"],
             "l1a_woT": P["l1a_woT"],
             "A_R": P["l1f_A_R"], "B_R": P["l1f_B_R"], "A_I": P["l1f_A_I"],
             "B_I": P["l1f_B_I"], "e32t": P["e32t"], "e16t": P["e16t"],
             "l1f_bias": P["l1f_bias"]}
        for li in range(6):
            for n in ("wqkvT", "woT", "w1a", "w1g", "w2", "b1a", "b1g", "b2",
                      "lng", "lnb", "qs", "ks", "qsr", "ksr"):
                m[f"{n}_{li}"] = P[f"ffn{li}_{n}"]
        maps.append(m)
    return maps



_CORE_IDS = list(range(NCORES))


def _launch(nc, in_maps):
    res = run_bass_kernel_spmd(nc, in_maps, core_ids=_CORE_IDS)
    return res.results


FUSED = True


def kernel(**inputs):
    per_core, P = host_prep(inputs)
    if FUSED:
        nc = build_fused()
        res = _launch(nc, fused_maps(per_core, P))
        return np.stack([r["xout"].T for r in res], axis=0).astype(np.float32)
    idb = np.ascontiguousarray(np.eye(128, dtype=np.float32).astype(BF))

    nc = build_l0attn()
    res = _launch(nc, [{"xin": per_core[c], "wqT": P["l0a_wqT"], "wuT": P["l0a_wuT"],
                        "bq": P["l0a_bq"], "bu": P["l0a_bu"], "idb": idb}
                       for c in range(NCORES)])
    x1 = [np.ascontiguousarray(r["xout"]) for r in res]

    nc_ln = build_ln_stream("ln")
    res = _launch(nc_ln, [{"xin": x1[c]} for c in range(NCORES)])
    y = [np.ascontiguousarray(r["yout"]) for r in res]
    nc_f = build_ffn_layer()
    for li in range(6):
        res = _launch(nc_f, [{
            "yin": y[c],
            "wqkvT": P[f"ffn{li}_wqkvT"], "woT": P[f"ffn{li}_woT"],
            "w1a": P[f"ffn{li}_w1a"], "w1g": P[f"ffn{li}_w1g"], "w2": P[f"ffn{li}_w2"],
            "b1a": P[f"ffn{li}_b1a"], "b1g": P[f"ffn{li}_b1g"], "b2": P[f"ffn{li}_b2"],
            "lng": P[f"ffn{li}_lng"], "lnb": P[f"ffn{li}_lnb"],
            "qs": P[f"ffn{li}_qs"], "ks": P[f"ffn{li}_ks"],
            "qsr": P[f"ffn{li}_qsr"], "ksr": P[f"ffn{li}_ksr"],
            "rot": P["rot"], "msk": P["msk"], "r2t": P["r2t"],
        } for c in range(NCORES)])
        y = [np.ascontiguousarray(r["ynext"]) for r in res]
    nc_x = build_ln_stream("res_ln")
    res = _launch(nc_x, [{"xin": y[c], "xres": x1[c]} for c in range(NCORES)])
    x2 = [np.ascontiguousarray(r["yout"]) for r in res]

    nc = build_l1attn()
    res = _launch(nc, [{"xin": x2[c], "wqT": P["l1a_wqT"], "wkvdT": P["l1a_wkvdT"],
                        "wkuT": P["l1a_wkuT"], "wvuT": P["l1a_wvuT"],
                        "woT": P["l1a_woT"]} for c in range(NCORES)])
    x3 = [np.ascontiguousarray(r["xout"]) for r in res]

    nc = build_l1ffn()
    res = _launch(nc, [{"xin": x3[c], "A_R": P["l1f_A_R"], "B_R": P["l1f_B_R"],
                        "A_I": P["l1f_A_I"], "B_I": P["l1f_B_I"], "e32t": P["e32t"],
                        "e16t": P["e16t"], "bias": P["l1f_bias"]}
                       for c in range(NCORES)])
    x4 = [r["xout"] for r in res]

    return np.stack([x4[c].T for c in range(NCORES)], axis=0).astype(np.float32)
